# version 03
# baseline (speedup 1.0000x reference)
import os
import numpy as np
from contextlib import ExitStack

import concourse.bass as bass
import concourse.mybir as mybir
from concourse.bass_utils import run_bass_kernel_spmd

F32 = mybir.dt.float32
BF16 = mybir.dt.bfloat16
AF = mybir.ActivationFunctionType
ALU = mybir.AluOpType
AX = mybir.AxisListType

T = 2048
C = 1024
H = 16
DH = 64
FF = 4096
NCH = 8
NTB = 16
NEG = -1.0e10
EPS_LN = 1e-5
EPS_Q = 1e-5

XH = 0
QQ = 16384
KK = 32768
VV = 49152
WT = 65792
SC = 73984
WB = 78080
TPc = 80128
SP = 82176
SP_H = SP
SP_PREP = SP + 512
SP_ONE = SP + 1536
SP_EX = SP + 1552
NBF = SP + 1552 + 1024

def w2base(fc):
    if fc < 16:
        return VV + 1024 * fc
    if fc < 24:
        return WT + 1024 * (fc - 16)
    if fc < 28:
        return SC + 1024 * (fc - 24)
    return WB + 1024 * (fc - 28)

G0 = 0
WS = 4096
SELo = 6144
MTRIo = 7168
B1o = 7296
B2o = 7328
BQo = 7336
BKo = 7344
BPo = 7352
GACCo = 7360
ONECOLo = 7424
GT = 7425
BVo = 7456
ONERo = 8480
NFA = 8992

CONST_COLS = 1232


_SCHED = None


def _semname(sm):
    return getattr(sm, "name", None) if sm is not None else None


def build_nc():
    nc = bass.Bass("TRN2", target_bir_lowering=False, detect_race_conditions=False)
    xfm = nc.dram_tensor("xfm", [C, T], F32, kind="ExternalInput").ap()
    wq = nc.dram_tensor("wq", [C, C], F32, kind="ExternalInput").ap()
    wk = nc.dram_tensor("wk", [C, C], F32, kind="ExternalInput").ap()
    wv = nc.dram_tensor("wv", [C, C], F32, kind="ExternalInput").ap()
    wp = nc.dram_tensor("wp", [C, C], F32, kind="ExternalInput").ap()
    w1 = nc.dram_tensor("w1", [FF, C], F32, kind="ExternalInput").ap()
    w2 = nc.dram_tensor("w2", [C, FF], F32, kind="ExternalInput").ap()
    cr1 = nc.dram_tensor("cr1", [128, T], F32, kind="ExternalInput").ap()
    cr2 = nc.dram_tensor("cr2", [128, T], F32, kind="ExternalInput").ap()
    consts = nc.dram_tensor("consts", [128, CONST_COLS], F32, kind="ExternalInput").ap()
    bvr = nc.dram_tensor("bvr", [1, C], F32, kind="ExternalInput").ap()
    out_fm = nc.dram_tensor("out_fm", [C, T], F32, kind="ExternalOutput").ap()
    x1d = nc.dram_tensor("x1d", [C, T], F32, kind="Internal").ap()
    sums_d = nc.dram_tensor("sums_d", [H, T], F32, kind="Internal").ap()

    with ExitStack() as st:
        BFt = st.enter_context(nc.sbuf_tensor("BF", [128, NBF], BF16))
        FAt = st.enter_context(nc.sbuf_tensor("FA", [128, NFA], F32))
        PSt = st.enter_context(nc.psum_tensor("PS", [128, 4096], F32))
        sD = st.enter_context(nc.semaphore("sD"))
        sT = st.enter_context(nc.semaphore("sT"))
        sV = st.enter_context(nc.semaphore("sV"))
        sS = st.enter_context(nc.semaphore("sS"))
        sG = st.enter_context(nc.semaphore("sG"))
        block = st.enter_context(nc.Block())

        BF = BFt.ap()
        FA = FAt.ap()
        PS = PSt.ap()

        ops = {"sync": [], "tensor": [], "vector": [], "scalar": [], "gpsimd": []}
        cnt = {"D": 0, "T": 0, "V": 0, "S": 0, "G": 0}
        sem = {"D": sD, "T": sT, "V": sV, "S": sS, "G": sG}
        seen = {e: {} for e in ops}
        ENG_OF = {"D": "sync", "T": "tensor", "V": "vector", "S": "scalar", "G": "gpsimd"}

        def emit(ch, fn, waits=(), inc=False):
            eng = ENG_OF[ch]
            w2 = []
            for (sname, v) in waits:
                if v <= 0:
                    continue
                if sname == ch:
                    continue
                if v > seen[eng].get(sname, 0):
                    seen[eng][sname] = v
                    w2.append((sname, v))
            amt = 0
            if inc:
                amt = 16 if ch == "D" else 1
                cnt[ch] += amt
            ops[eng].append((fn, w2, sem[ch] if inc else None, amt))
            return cnt[ch]

        def dma(out_ap, in_ap, waits=(), transpose=False):
            return emit(
                "D",
                lambda e, o=out_ap, i=in_ap, tr=transpose: e.dma_start(out=o, in_=i, transpose=tr),
                waits, inc=True)

        def bf3(off, span, b, r):
            return BF[:, off:off + span].rearrange("p (b r) -> p b r", b=b, r=r)

        KPH = int(os.environ.get("KPH", "9"))

        def finish():
            final_d = cnt["D"]
            ops["sync"].append((lambda e: None, [("D", final_d)], None, 0))
            import kernel as _km
            _km._SCHED = {"ops": {e: [(w, _semname(sm), amt)
                                      for (fn, w, sm, amt) in ops[e]] for e in ops}}

            def emit_block(eng_name, e):
                for (fn, waits, s_, amt) in ops[eng_name]:
                    for (sname, v) in waits:
                        e.wait_ge(sem[sname], v)
                    inst = fn(e)
                    if s_ is not None:
                        inst.then_inc(s_, amt)

            @block.sync
            def _(e):
                emit_block("sync", e)

            @block.tensor
            def _(e):
                emit_block("tensor", e)

            @block.vector
            def _(e):
                emit_block("vector", e)

            @block.scalar
            def _(e):
                emit_block("scalar", e)

            @block.gpsimd
            def _(e):
                emit_block("gpsimd", e)

        d_const = dma(FA[:, SELo:SELo + CONST_COLS], consts)
        d_bv = dma(FA[0:1, BVo:BVo + C], bvr)
        v_ones = emit("V", lambda e: e.memset(FA[0:1, ONERo:ONERo + 512], 1.0), inc=True)
        v_ones = emit("V", lambda e: e.memset(FA[:, ONECOLo:ONECOLo + 1], 1.0), inc=True)
        v_ones = emit("V", lambda e: e.memset(BF[:, SP_ONE:SP_ONE + 16], 1.0), inc=True)
        vone_ap = BF[:, VV:VV + 16640].rearrange("p (t h d) -> p t h d", t=16, h=16, d=65)[:, :, :, 64:65]
        v_ones = emit("V", lambda e: e.memset(vone_ap, 1.0), inc=True)

        ONES_BF = BF[:, SP_ONE:SP_ONE + 1]
        ONER128 = FA[0:1, ONERo:ONERo + 128]

        def ln_phase(src_dram, first):
            t_before = cnt["T"]
            d_x = [0] * NCH
            v_st = [0] * NCH
            for c in range(NCH):
                w = []
                if c >= 2:
                    w.append(("V", v_st[c - 2]))
                d_x[c] = dma(FA[:, G0 + 2048 * (c % 2):G0 + 2048 * (c % 2) + 2048],
                             src_dram[:, :].rearrange("(a p) m -> a p m", a=NCH, p=128)[c],
                             waits=w)
                for g in range(4):
                    xsrc = FA[:, G0 + 2048 * (c % 2) + 512 * g:G0 + 2048 * (c % 2) + 512 * (g + 1)]
                    exb = BF[:, SP_EX:SP_EX + 512]
                    exq = BF[:, SP_EX + 512:SP_EX + 1024]
                    wv_ = [("D", d_x[c])]
                    if c > 0 or g > 0:
                        wv_.append(("T", cnt["T"]))
                    emit("V", lambda e, o=exb, i=xsrc: e.tensor_copy(o, i), wv_)
                    v_c = emit("V", lambda e, o=exq, i=exb: e.tensor_mul(o, i, i), inc=True)
                    emit("T", lambda e, o=PS[0:1, 512 * g:512 * g + 512], l=ONES_BF, r=exb,
                         s=(c == 0), p=(c == NCH - 1):
                         e.matmul(o, l, r, start=s, stop=p), [("V", v_c)])
                    t_c = emit("T", lambda e, o=PS[0:1, 2048 + 512 * g:2048 + 512 * g + 512],
                               l=ONES_BF, r=exq, s=(c == 0), p=(c == NCH - 1):
                               e.matmul(o, l, r, start=s, stop=p), [("V", v_c)], inc=True)
                v_st[c] = cnt["V"]
            t_stats = cnt["T"]
            slotA = FA[0:1, 0:2048]
            slotB = FA[0:1, 2048:4096]
            emit("V", lambda e: e.tensor_scalar(slotA, PS[0:1, 0:2048], 1.0 / C, None, ALU.mult),
                 [("T", t_stats)])
            emit("V", lambda e: e.tensor_mul(slotB, slotA, slotA))
            v_var = emit("V", lambda e: e.scalar_tensor_tensor(
                slotB, PS[0:1, 2048:4096], 1.0 / C, slotB, ALU.mult, ALU.subtract),
                [("T", t_stats)], inc=True)
            emit("V", lambda e: e.tensor_scalar(slotB, slotB, EPS_LN, None, ALU.add))
            s_sd = emit("S", lambda e: e.activation(slotB, slotB, AF.Sqrt),
                        [("V", v_var)], inc=True)
            emit("V", lambda e: e.reciprocal(slotB, slotB), [("S", s_sd)])
            v_negm = emit("V", lambda e: e.tensor_scalar(
                slotA, slotA, -1.0, None, ALU.mult), inc=True)
            t_bc = 0
            for g in range(4):
                emit("T", lambda e, o=PS[:, 512 * g:512 * g + 512],
                     r=FA[0:1, 2048 + 512 * g:2048 + 512 * (g + 1)]:
                     e.matmul(o, ONER128, r, start=True, stop=True), [("V", v_negm)])
                t_bc = emit("T", lambda e, o=PS[:, 2048 + 512 * g:2048 + 512 * g + 512],
                            r=FA[0:1, 512 * g:512 * (g + 1)]:
                            e.matmul(o, ONER128, r, start=True, stop=True),
                            [("V", v_negm)], inc=(g == 3))
            v_ap = [0] * NCH
            for c in range(NCH):
                w = [("T", t_bc)]
                if c >= 2:
                    w.append(("V", v_ap[c - 2]))
                d_a = dma(FA[:, G0 + 2048 * (c % 2):G0 + 2048 * (c % 2) + 2048],
                          src_dram[:, :].rearrange("(a p) m -> a p m", a=NCH, p=128)[c],
                          waits=w)
                xs = FA[:, G0 + 2048 * (c % 2):G0 + 2048 * (c % 2) + 2048]
                emit("V", lambda e, x=xs: e.tensor_add(x, x, PS[:, 2048:4096]),
                     [("D", d_a), ("T", t_bc)])
                v_ap[c] = emit("V", lambda e, x=xs, o=BF[:, XH + 2048 * c:XH + 2048 * (c + 1)]:
                               e.tensor_mul(o, x, PS[:, 0:2048]), inc=True)
            return v_ap

        v_ap1 = ln_phase(xfm, True)
        if KPH < 2:
            finish()
            return nc

        d_c1 = dma(FA[:, 0:2048], cr1, waits=[("V", v_ap1[6])])
        d_c2 = dma(FA[:, 2048:4096], cr2, waits=[("V", v_ap1[7])])

        prev_cast = [0]

        def prep_weight(wdram, wait_wt_free_t):
            v_cast = [0] * 8
            d_tr = [0] * 8
            for ot in range(8):
                w = []
                if ot >= 2:
                    w.append(("V", v_cast[ot - 2]))
                else:
                    w.append(("V", prev_cast[0]))
                d_w = dma(FA[:, WS + 1024 * (ot % 2):WS + 1024 * (ot % 2) + 1024],
                          wdram[128 * ot:128 * (ot + 1), :], waits=w)
                wc = [("D", d_w)]
                if ot >= 2:
                    wc.append(("D", d_tr[ot - 2]))
                else:
                    wc.append(("D", prev_cast[1] if len(prev_cast) > 1 else 0))
                v_cast[ot] = emit("V", lambda e, o=BF[:, WB + 1024 * (ot % 2):WB + 1024 * (ot % 2) + 1024],
                                  i=FA[:, WS + 1024 * (ot % 2):WS + 1024 * (ot % 2) + 1024]:
                                  e.tensor_copy(o, i), wc, inc=True)
                w = [("V", v_cast[ot])]
                if ot == 0 and wait_wt_free_t:
                    w.append(("T", wait_wt_free_t))
                oap = bf3(WT, 8192, 8, 1024)[:, :, 128 * ot:128 * ot + 128]
                d_tr[ot] = dma(oap, BF[:, WB + 1024 * (ot % 2):WB + 1024 * (ot % 2) + 1024],
                               waits=w, transpose=True)
            prev_cast[0] = v_cast[7]
            if len(prev_cast) > 1:
                prev_cast[1] = d_tr[7]
            else:
                prev_cast.append(d_tr[7])
            return d_tr[7]

        def fm_proj(dst_off, bias_off, d_wt, extra_first_waits):
            v_rot = [0] * NCH
            s_qb = [0] * NCH
            t_oc = [0] * NCH
            for oc in range(NCH):
                pcol = (oc % 2) * 2048
                w = [("D", d_wt)]
                if oc == 0:
                    w += extra_first_waits
                if oc >= 2:
                    w.append(("S", s_qb[oc - 2]))
                for cc in range(NCH):
                    for g in range(4):
                        last = (cc == NCH - 1 and g == 3)
                        t_oc[oc] = emit("T", lambda e,
                                        o=PS[:, pcol + 512 * g:pcol + 512 * (g + 1)],
                                        l=BF[:, WT + 1024 * cc + 128 * oc:WT + 1024 * cc + 128 * oc + 128],
                                        r=BF[:, XH + 2048 * cc + 512 * g:XH + 2048 * cc + 512 * (g + 1)],
                                        s=(cc == 0), p=(cc == NCH - 1):
                                        e.matmul(o, l, r, start=s, stop=p),
                                        w, inc=last)
                    w = []
                qb = BF[:, SC:SC + 2048]
                qs = BF[:, SC + 2048:SC + 4096]
                wS = [("T", t_oc[oc])]
                if oc >= 1:
                    wS.append(("V", v_rot[oc - 1]))
                s_qb[oc] = emit("S", lambda e, o=qb, i=PS[:, pcol:pcol + 2048],
                                b=FA[:, bias_off + oc:bias_off + oc + 1]:
                                e.activation(o, i, AF.Identity, bias=b), wS, inc=True)
                emit("V", lambda e, o=qs, i=qb:
                     e.stream_shuffle(o, i, mask=[(j + 16) % 32 for j in range(32)]),
                     [("S", s_qb[oc])])
                emit("V", lambda e, o=qb, i=qb: e.tensor_mul(o, i, FA[:, 0:2048]),
                     [("D", d_c1)])
                emit("V", lambda e, o=qs, i=qs: e.tensor_mul(o, i, FA[:, 2048:4096]),
                     [("D", d_c2)])
                v_rot[oc] = emit("V", lambda e, o=BF[:, dst_off + 2048 * oc:dst_off + 2048 * (oc + 1)],
                                 a=qb, b=qs: e.tensor_add(o, a, b), inc=True)
            return t_oc[NCH - 1], v_rot, s_qb

        d_wkT = prep_weight(wk, 0)
        t_kproj, v_rotk, s_qbk = fm_proj(KK, BKo, d_wkT, [("V", v_ap1[7])])
        d_wvT = prep_weight(wv, t_kproj)
        t_vp = [0] * NTB
        v_vev = [0] * NTB
        for tb in range(NTB):
            pcol = 2048 + 1024 * (tb % 2)
            w = [("D", d_wvT)]
            if tb < 2:
                w.append(("S", s_qbk[NCH - 1]))
            if tb >= 2:
                w.append(("V", v_vev[tb - 2]))
            for cc in range(NCH):
                for g in range(2):
                    emit("T", lambda e, o=PS[:, pcol + 512 * g:pcol + 512 * (g + 1)],
                         l=BF[:, XH + 2048 * cc + 128 * tb:XH + 2048 * cc + 128 * tb + 128],
                         r=BF[:, WT + 1024 * cc + 512 * g:WT + 1024 * cc + 512 * (g + 1)],
                         s=(cc == 0): e.matmul(o, l, r, start=s, stop=False), w)
                w = []
            for g in range(2):
                t_vp[tb] = emit("T", lambda e, o=PS[:, pcol + 512 * g:pcol + 512 * (g + 1)],
                                l=ONER128, r=FA[0:1, BVo + 512 * g:BVo + 512 * (g + 1)]:
                                e.matmul(o, l, r, start=False, stop=True),
                                [("D", d_bv)], inc=(g == 1))
            oap = BF[:, VV + 1040 * tb:VV + 1040 * tb + 1040].rearrange(
                "p (h d) -> p h d", h=16, d=65)[:, :, 0:64]
            iap = PS[:, pcol:pcol + 1024].rearrange("p (h d) -> p h d", h=16, d=64)
            v_vev[tb] = emit("V", lambda e, o=oap, i=iap: e.tensor_copy(o, i),
                             [("T", t_vp[tb])], inc=True)
        d_wqT = prep_weight(wq, t_vp[NTB - 1])
        t_qproj, v_rotq, s_qbq = fm_proj(
            QQ, BQo, d_wqT, [("V", v_vev[NTB - 1]), ("S", s_qbk[NCH - 1])])

        if KPH < 3:
            finish()
            return nc
        d_wpT = prep_weight(wp, t_qproj)

        gstate = {"gi": 0, "v_gprev": [prev_cast[0], prev_cast[0]], "v_gred": 0}

        def gamma_unit():
            gi = gstate["gi"]
            if gi >= 64:
                return
            w = [("V", gstate["v_gprev"][gi % 2])]
            if gi < 64 - 32:
                iap = w1[128 * gi:128 * (gi + 1), :]
            else:
                tt = gi - 32
                iap = w2[:, :].rearrange("(a p) (b m) -> a b p m", p=128, m=1024)[tt // 4, tt % 4]
            d_g = dma(FA[:, WS + 1024 * (gi % 2):WS + 1024 * (gi % 2) + 1024], iap, waits=w)
            v = emit("V", lambda e, o=FA[:, GACCo + gi:GACCo + gi + 1],
                     i=FA[:, WS + 1024 * (gi % 2):WS + 1024 * (gi % 2) + 1024]:
                     e.tensor_reduce(o, i, AX.X, ALU.add, apply_absolute_value=True),
                     [("D", d_g)], inc=True)
            gstate["v_gprev"][gi % 2] = v
            gstate["v_gred"] = v
            gstate["gi"] = gi + 1

        pc = 0
        s_exp = {}
        t_pv_last = 0
        v_yev = [0] * H
        v_sum = [0] * H
        d_sumst = [0] * H
        pending_pv = [None]

        def flush_pv():
            nonlocal_t = pending_pv[0]
            if nonlocal_t is not None:
                nonlocal_t()
                pending_pv[0] = None

        for h in range(H):
            ch = h // 2
            pb = 64 * (h % 2)
            for _ in range(4):
                gamma_unit()
            for i in range(NTB):
                q0 = 128 * i
                Nq = T - q0
                pieces = [(q0, min(1024, Nq))]
                if Nq > 1024:
                    pieces.append((q0 + 1024, Nq - 1024))
                for pi, (qoff, plen) in enumerate(pieces):
                    par = pc % 2
                    stc = 1024 * par
                    wT_ = []
                    if pc >= 2:
                        wT_.append(("S", s_exp[pc - 2]))
                    if h == 0 and i == 0 and pi == 0:
                        wT_.append(("V", v_rotq[NCH - 1]))
                        wT_.append(("V", v_rotk[NCH - 1]))
                    t_kq = 0
                    ng = (plen + 511) // 512
                    for g in range(ng):
                        gw = min(512, plen - 512 * g)
                        t_kq = emit("T", lambda e, o=PS[:, stc + 512 * g:stc + 512 * g + gw],
                                    l=BF[pb:pb + 64, KK + 2048 * ch + q0:KK + 2048 * ch + q0 + 128],
                                    r=BF[pb:pb + 64, QQ + 2048 * ch + qoff + 512 * g:
                                         QQ + 2048 * ch + qoff + 512 * g + gw]:
                                    e.matmul(o, l, r, start=True, stop=True),
                                    wT_, inc=(g == ng - 1))
                        wT_ = []
                    wS_ = [("T", t_kq)]
                    if pi == 0:
                        v_m = emit("V", lambda e, o=PS[:, stc:stc + 128]:
                                   e.tensor_add(o, o, FA[:, MTRIo:MTRIo + 128]),
                                   [("T", t_kq), ("D", d_const)], inc=True)
                        wS_ = [("V", v_m)]
                    s_exp[pc] = emit("S", lambda e, o=BF[:, TPc + 1024 * par:TPc + 1024 * par + plen],
                                     i=PS[:, stc:stc + plen]:
                                     e.activation(o, i, AF.Exp, scale=0.125), wS_, inc=True)

                    def make_pv(pc_=pc, par_=par, qoff_=qoff, plen_=plen, i_=i, h_=h, pi_=pi):
                        def do():
                            nonlocal t_pv_last
                            wPV = [("S", s_exp[pc_])]
                            if i_ == 0 and pi_ == 0 and h_ > 0:
                                wPV.append(("V", v_sum[h_ - 1]))
                            off = 0
                            while off < plen_:
                                gw = min(512 - ((qoff_ + off) % 512), plen_ - off)
                                if pi_ == 0 and off == 0:
                                    gw = min(gw, 128)
                                stp = (pi_ == 0 and off == 0 and (i_ % 4) == 3)
                                st_ = (i_ == 0 and ((qoff_ + off) % 512) == 0)
                                t_pv_last = emit("T", lambda e,
                                                 o=PS[0:65, 2048 + qoff_ + off:2048 + qoff_ + off + gw],
                                                 l=BF[:, VV + 1040 * i_ + 65 * h_:VV + 1040 * i_ + 65 * h_ + 65],
                                                 r=BF[:, TPc + 1024 * par_ + off:TPc + 1024 * par_ + off + gw],
                                                 s=st_, p=stp:
                                                 e.matmul(o, l, r, start=s, stop=p),
                                                 wPV, inc=(off + gw >= plen_))
                                wPV = []
                                off += gw
                        return do

                    flush_pv()
                    pending_pv[0] = make_pv()
                    pc += 1
            flush_pv()
            wY = [("T", t_pv_last)]
            if h == 0:
                wY.append(("T", t_qproj))
            v_yev[h] = emit("V", lambda e, o=BF[pb:pb + 64, XH + 2048 * ch:XH + 2048 * (ch + 1)],
                            i=PS[0:64, 2048:4096]: e.tensor_copy(o, i), wY, inc=True)
            wSm = []
            if h >= 1:
                wSm.append(("D", d_sumst[h - 1]))
            v_sum[h] = emit("V", lambda e, o=FA[0:1, 0:2048], i=PS[64:65, 2048:4096]:
                            e.tensor_copy(o, i), wSm, inc=True)
            d_sumst[h] = dma(sums_d[h:h + 1, :], FA[0:1, 0:2048], waits=[("V", v_sum[h])])

        d_sumld = dma(FA[0:16, 0:2048], sums_d, waits=[("D", d_sumst[H - 1])])
        v_inv = emit("V", lambda e: e.reciprocal(FA[0:16, 0:2048], FA[0:16, 0:2048]),
                     [("D", d_sumld)], inc=True)
        t_selbc = [0] * NCH
        v_ynorm = [0] * NCH
        for c2 in range(NCH):
            pcol = (c2 % 2) * 2048
            w = [("V", v_inv), ("D", d_const)]
            if c2 >= 2:
                w.append(("V", v_ynorm[c2 - 2]))
            for g in range(4):
                t_selbc[c2] = emit("T", lambda e, o=PS[:, pcol + 512 * g:pcol + 512 * (g + 1)],
                                   l=FA[0:16, SELo + 128 * c2:SELo + 128 * (c2 + 1)],
                                   r=FA[0:16, 512 * g:512 * (g + 1)]:
                                   e.matmul(o, l, r, start=True, stop=True), w, inc=(g == 3))
                w = []
            v_ynorm[c2] = emit("V", lambda e, o=BF[:, XH + 2048 * c2:XH + 2048 * (c2 + 1)],
                               i=PS[:, pcol:pcol + 2048]:
                               e.tensor_mul(o, o, i),
                               [("T", t_selbc[c2]), ("V", v_yev[2 * c2 + 1])], inc=True)

        if KPH < 4:
            finish()
            return nc
        while gstate["gi"] < 64:
            gamma_unit()
        emit("V", lambda e: e.tensor_reduce(FA[:, GT:GT + 1], FA[:, GACCo:GACCo + 32], AX.X, ALU.add),
             [("V", gstate["v_gred"])])
        v_g2 = emit("V", lambda e: e.tensor_reduce(FA[:, GT + 1:GT + 2], FA[:, GACCo + 32:GACCo + 64],
                                                   AX.X, ALU.add), inc=True)
        emit("T", lambda e: e.matmul(PS[0:1, 0:1], FA[:, ONECOLo:ONECOLo + 1], FA[:, GT:GT + 1],
                                     start=True, stop=True),
             [("V", v_g2), ("V", v_ynorm[NCH - 1])])
        t_gm = emit("T", lambda e: e.matmul(PS[0:1, 1:2], FA[:, ONECOLo:ONECOLo + 1],
                                            FA[:, GT + 1:GT + 2], start=True, stop=True),
                    [("V", v_g2)], inc=True)
        emit("V", lambda e: e.tensor_scalar(FA[0:1, GT + 4:GT + 5], PS[0:1, 0:1],
                                            1.0 / (FF * C), EPS_Q, ALU.mult, ALU.add),
             [("T", t_gm)])
        emit("V", lambda e: e.tensor_scalar(FA[0:1, GT + 5:GT + 6], PS[0:1, 1:2],
                                            1.0 / (FF * C), EPS_Q, ALU.mult, ALU.add))
        emit("V", lambda e: e.reciprocal(FA[0:1, GT + 4:GT + 5], FA[0:1, GT + 4:GT + 5]))
        v_gi = emit("V", lambda e: e.reciprocal(FA[0:1, GT + 5:GT + 6], FA[0:1, GT + 5:GT + 6]),
                    inc=True)
        emit("T", lambda e: e.matmul(PS[:, 4:5], ONER128, FA[0:1, GT + 4:GT + 5],
                                     start=True, stop=True), [("V", v_gi)])
        t_gb = emit("T", lambda e: e.matmul(PS[:, 5:6], ONER128, FA[0:1, GT + 5:GT + 6],
                                            start=True, stop=True), inc=True)
        emit("V", lambda e: e.tensor_copy(FA[:, GT + 2:GT + 3], PS[:, 4:5]), [("T", t_gb)])
        v_sg = emit("V", lambda e: e.tensor_copy(FA[:, GT + 3:GT + 4], PS[:, 5:6]), inc=True)
        SG1 = FA[:, GT + 2:GT + 3]
        SG2 = FA[:, GT + 3:GT + 4]

        t_pp = [0] * NCH
        v_res = [0] * NCH
        d_x1w = [0] * NCH
        for oc in range(NCH):
            pcol = (oc % 2) * 2048
            w = [("D", d_wpT), ("V", v_sg), ("V", v_ynorm[NCH - 1])]
            if oc >= 2:
                w.append(("V", v_res[oc - 2]))
            for cc in range(NCH):
                for g in range(4):
                    t_pp[oc] = emit("T", lambda e, o=PS[:, pcol + 512 * g:pcol + 512 * (g + 1)],
                                    l=BF[:, WT + 1024 * cc + 128 * oc:WT + 1024 * cc + 128 * oc + 128],
                                    r=BF[:, XH + 2048 * cc + 512 * g:XH + 2048 * cc + 512 * (g + 1)],
                                    s=(cc == 0), p=(cc == NCH - 1): e.matmul(o, l, r, start=s, stop=p),
                                    w, inc=(cc == NCH - 1 and g == 3))
                    w = []
            wD = [("T", t_selbc[NCH - 1]), ("V", v_ynorm[NCH - 1])]
            if oc >= 2:
                wD.append(("V", v_res[oc - 2]))
            d_xr = dma(FA[:, G0 + 2048 * (oc % 2):G0 + 2048 * (oc % 2) + 2048],
                       xfm[:, :].rearrange("(a p) m -> a p m", a=NCH, p=128)[oc], waits=wD)
            xs = FA[:, G0 + 2048 * (oc % 2):G0 + 2048 * (oc % 2) + 2048]
            v_res[oc] = emit("V", lambda e, x=xs, i=PS[:, pcol:pcol + 2048],
                             b=FA[:, BPo + oc:BPo + oc + 1]:
                             e.scalar_tensor_tensor(x, i, b, x, ALU.add, ALU.add),
                             [("T", t_pp[oc]), ("D", d_xr)], inc=True)
            d_x1w[oc] = dma(x1d[:, :].rearrange("(a p) m -> a p m", a=NCH, p=128)[oc], xs,
                            waits=[("V", v_res[oc])])

        g_q2 = {}
        d_w2tr = 0
        sl_order = [0, 1, 2, 3, 6, 7, 4, 5]
        qi = 0
        for sl in sl_order:
            for dt in range(8):
                w = [("V", v_sg)]
                if qi >= 2:
                    w.append(("G", g_q2[qi - 2]))
                d_g = dma(FA[:, WS + 512 * (qi % 2):WS + 512 * (qi % 2) + 512],
                          w2[128 * dt:128 * (dt + 1), 512 * sl:512 * (sl + 1)], waits=w)
                src = FA[:, WS + 512 * (qi % 2):WS + 512 * (qi % 2) + 512]
                t1 = BF[:, SP_PREP:SP_PREP + 512]
                t2 = BF[:, SP_PREP + 512:SP_PREP + 1024]
                emit("G", lambda e, o=t1, i=src: e.tensor_scalar(o, i, SG2, 0.5, ALU.mult, ALU.is_ge),
                     [("D", d_g), ("V", v_sg), ("D", d_w2tr)])
                emit("G", lambda e, o=t2, i=src: e.tensor_scalar(o, i, SG2, -0.5, ALU.mult, ALU.is_le))
                g_q2[qi] = emit("G", lambda e, o=t1, a=t1, b=t2: e.tensor_sub(o, a, b), inc=True)
                wtr = [("G", g_q2[qi]), ("T", t_pv_last)]
                if sl in (4, 5):
                    wtr.append(("T", t_pp[NCH - 1]))
                oap = BF[:, w2base(4 * sl):w2base(4 * sl) + 4096].rearrange(
                    "p (b r) -> p b r", b=4, r=1024)[:, :, 128 * dt:128 * dt + 128]
                d_w2tr = dma(oap, t1, waits=wtr, transpose=True)
                qi += 1

        if KPH < 5:
            finish()
            return nc
        ln2_gate_t = t_pp[NCH - 1]
        ln2_gate_v = v_res[NCH - 1]
        v_ap2 = None

        def ln_phase2():
            d_x = [0] * NCH
            v_st = [0] * NCH
            for c in range(NCH):
                w = [("D", d_x1w[c])]
                if c >= 2:
                    w.append(("V", v_st[c - 2]))
                d_x[c] = dma(FA[:, G0 + 2048 * (c % 2):G0 + 2048 * (c % 2) + 2048],
                             x1d[:, :].rearrange("(a p) m -> a p m", a=NCH, p=128)[c], waits=w)
                for g in range(4):
                    xsrc = FA[:, G0 + 2048 * (c % 2) + 512 * g:G0 + 2048 * (c % 2) + 512 * (g + 1)]
                    exb = BF[:, SP_EX:SP_EX + 512]
                    exq = BF[:, SP_EX + 512:SP_EX + 1024]
                    wv_ = [("D", d_x[c]), ("T", ln2_gate_t)]
                    if c > 0 or g > 0:
                        wv_.append(("T", cnt["T"]))
                    emit("V", lambda e, o=exb, i=xsrc: e.tensor_copy(o, i), wv_)
                    v_c = emit("V", lambda e, o=exq, i=exb: e.tensor_mul(o, i, i), inc=True)
                    emit("T", lambda e, o=PS[0:1, 512 * g:512 * g + 512], l=ONES_BF, r=exb,
                         s=(c == 0), p=(c == NCH - 1):
                         e.matmul(o, l, r, start=s, stop=p), [("V", v_c), ("V", ln2_gate_v)])
                    emit("T", lambda e, o=PS[0:1, 2048 + 512 * g:2048 + 512 * g + 512],
                         l=ONES_BF, r=exq, s=(c == 0), p=(c == NCH - 1):
                         e.matmul(o, l, r, start=s, stop=p), [("V", v_c)], inc=True)
                v_st[c] = cnt["V"]
            t_stats = cnt["T"]
            slotA = FA[0:1, 0:2048]
            slotB = FA[0:1, 2048:4096]
            emit("V", lambda e: e.tensor_scalar(slotA, PS[0:1, 0:2048], 1.0 / C, None, ALU.mult),
                 [("T", t_stats)])
            emit("V", lambda e: e.tensor_mul(slotB, slotA, slotA))
            v_var = emit("V", lambda e: e.scalar_tensor_tensor(
                slotB, PS[0:1, 2048:4096], 1.0 / C, slotB, ALU.mult, ALU.subtract),
                [("T", t_stats)], inc=True)
            emit("V", lambda e: e.tensor_scalar(slotB, slotB, EPS_LN, None, ALU.add))
            s_sd = emit("S", lambda e: e.activation(slotB, slotB, AF.Sqrt),
                        [("V", v_var)], inc=True)
            emit("V", lambda e: e.reciprocal(slotB, slotB), [("S", s_sd)])
            v_negm = emit("V", lambda e: e.tensor_scalar(
                slotA, slotA, -1.0, None, ALU.mult), inc=True)
            t_bc = 0
            for g in range(4):
                emit("T", lambda e, o=PS[:, 512 * g:512 * g + 512],
                     r=FA[0:1, 2048 + 512 * g:2048 + 512 * (g + 1)]:
                     e.matmul(o, ONER128, r, start=True, stop=True), [("V", v_negm)])
                t_bc = emit("T", lambda e, o=PS[:, 2048 + 512 * g:2048 + 512 * g + 512],
                            r=FA[0:1, 512 * g:512 * (g + 1)]:
                            e.matmul(o, ONER128, r, start=True, stop=True),
                            [("V", v_negm)], inc=(g == 3))
            v_ap = [0] * NCH
            for c in range(NCH):
                w = [("T", t_bc)]
                if c >= 2:
                    w.append(("V", v_ap[c - 2]))
                d_a = dma(FA[:, G0 + 2048 * (c % 2):G0 + 2048 * (c % 2) + 2048],
                          x1d[:, :].rearrange("(a p) m -> a p m", a=NCH, p=128)[c], waits=w)
                xs = FA[:, G0 + 2048 * (c % 2):G0 + 2048 * (c % 2) + 2048]
                emit("V", lambda e, x=xs: e.tensor_add(x, x, PS[:, 2048:4096]),
                     [("D", d_a), ("T", t_bc)])
                v_ap[c] = emit("V", lambda e, x=xs, o=BF[:, XH + 2048 * c:XH + 2048 * (c + 1)]:
                               e.tensor_mul(o, x, PS[:, 0:2048]), inc=True)
            return v_ap

        v_ap2 = ln_phase2()

        v_q1 = {}
        d_w1tr = 0
        g_last = cnt["G"]
        for fc in range(32):
            w = [("V", v_sg)]
            if fc >= 2:
                w.append(("V", v_q1[fc - 2]))
            else:
                w.append(("G", g_last))
            d_g = dma(FA[:, WS + 1024 * (fc % 2):WS + 1024 * (fc % 2) + 1024],
                      w1[128 * fc:128 * (fc + 1), :], waits=w)
            src = FA[:, WS + 1024 * (fc % 2):WS + 1024 * (fc % 2) + 1024]
            t1 = BF[:, SP_PREP:SP_PREP + 1024]
            t2 = BF[:, SP_EX:SP_EX + 1024]
            wq_ = [("D", d_g), ("V", v_sg), ("V", v_ap2[NCH - 1]),
                   ("G", g_last), ("D", d_w2tr), ("D", d_w1tr)]
            emit("V", lambda e, o=t1, i=src: e.tensor_scalar(o, i, SG1, 0.5, ALU.mult, ALU.is_ge), wq_)
            emit("V", lambda e, o=t2, i=src: e.tensor_scalar(o, i, SG1, -0.5, ALU.mult, ALU.is_le))
            v_q1[fc] = emit("V", lambda e, o=t1, a=t1, b=t2: e.tensor_sub(o, a, b), inc=True)
            oap = BF[:, QQ:QQ + 32768].rearrange("p (b r) -> p b r", b=8, r=4096)[
                :, :, 128 * fc:128 * fc + 128]
            wtr = [("V", v_q1[fc]), ("T", t_pv_last)]
            d_w1tr = dma(oap, t1, waits=wtr, transpose=True)

        if KPH < 6:
            finish()
            return nc
        t_mm1 = {}
        s_relu = {}
        t_mm2 = {}
        v_out = [0] * 8
        d_out = [0] * 8
        for tp in range(8):
            w = [("T", t_selbc[NCH - 1])]
            if tp >= 2:
                w.append(("V", v_out[tp - 2]))
            if tp < 2:
                w.append(("V", v_ap2[NCH - 1]))
                w.append(("D", d_x1w[NCH - 1]))
            d_x1p = dma(FA[:, G0 + 2048 * (tp % 2):G0 + 2048 * (tp % 2) + 2048].rearrange(
                "p (a m) -> p a m", a=8, m=256),
                x1d[:, :].rearrange("(a p) m -> p a m", a=NCH, p=128)[:, :, 256 * tp:256 * (tp + 1)],
                waits=w)
            for fc in range(32):
                hcol = 512 * (fc % 2)
                wmm1 = [("D", d_w1tr), ("V", v_ap2[NCH - 1])]
                if (tp, fc) == (0, 0):
                    wmm1.append(("V", v_ap2[NCH - 1]))
                prev = (tp, fc - 2) if fc >= 2 else ((tp - 1, 30 + fc) if tp >= 1 else None)
                if prev in s_relu:
                    wmm1.append(("S", s_relu[prev]))
                for cc in range(NCH):
                    t_mm1[(tp, fc)] = emit("T", lambda e, o=PS[:, hcol:hcol + 256],
                                           l=BF[:, QQ + 4096 * cc + 128 * fc:QQ + 4096 * cc + 128 * fc + 128],
                                           r=BF[:, XH + 2048 * cc + 256 * tp:XH + 2048 * cc + 256 * (tp + 1)],
                                           s=(cc == 0), p=(cc == NCH - 1): e.matmul(o, l, r, start=s, stop=p),
                                           wmm1, inc=(cc == NCH - 1))
                    wmm1 = []
                s_relu[(tp, fc)] = emit("S", lambda e, o=BF[:, SP_H + 256 * (fc % 2):SP_H + 256 * (fc % 2) + 256],
                                        i=PS[:, hcol:hcol + 256], b=FA[:, B1o + fc:B1o + fc + 1]:
                                        e.activation(o, i, AF.Relu, bias=b),
                                        [("T", t_mm1[(tp, fc)])], inc=True)
                wmm2 = [("S", s_relu[(tp, fc)]), ("D", d_w2tr)]
                if fc == 0 and tp >= 1:
                    wmm2.append(("V", v_out[tp - 1]))
                for dc in range(NCH):
                    st_ = (fc == 0 and dc % 2 == 0)
                    t_mm2[(tp, fc)] = emit("T", lambda e, o=PS[:, 2048 + 256 * dc:2048 + 256 * (dc + 1)],
                                           l=BF[:, w2base(fc) + 128 * dc:w2base(fc) + 128 * dc + 128],
                                           r=BF[:, SP_H + 256 * (fc % 2):SP_H + 256 * (fc % 2) + 256],
                                           s=st_, p=(fc == 31 and dc % 2 == 1):
                                           e.matmul(o, l, r, start=s, stop=p),
                                           wmm2, inc=(dc == NCH - 1))
                    wmm2 = []
            for dc in range(NCH):
                xs = FA[:, G0 + 2048 * (tp % 2) + 256 * dc:G0 + 2048 * (tp % 2) + 256 * (dc + 1)]
                v_out[tp] = emit("V", lambda e, x=xs, i=PS[:, 2048 + 256 * dc:2048 + 256 * (dc + 1)],
                                 b=FA[:, B2o + dc:B2o + dc + 1]:
                                 e.scalar_tensor_tensor(x, i, b, x, ALU.add, ALU.add),
                                 [("T", t_mm2[(tp, 31)]), ("D", d_x1p)], inc=(dc == NCH - 1))
            d_out[tp] = dma(out_fm[:, :].rearrange("(a p) m -> p a m", a=NCH, p=128)[
                :, :, 256 * tp:256 * (tp + 1)],
                FA[:, G0 + 2048 * (tp % 2):G0 + 2048 * (tp % 2) + 2048].rearrange(
                    "p (a m) -> p a m", a=8, m=256),
                waits=[("V", v_out[tp])])

        finish()
    return nc


def _perm():
    perm = np.zeros(C, dtype=np.int64)
    for g in range(C):
        chk, p = g // 128, g % 128
        h = 2 * chk + p // 64
        lp = p % 64
        quad = lp // 32
        r = lp % 32
        if r < 16:
            i_idx, par = r + 16 * quad, 0
        else:
            i_idx, par = (r - 16) + 16 * quad, 1
        perm[g] = 64 * h + 2 * i_idx + par
    return perm


def _rope_tables():
    i = np.arange(32, dtype=np.float32)
    theta = (np.float32(1.0) / (np.float32(10000.0) ** (np.float32(2.0) * i / np.float32(64)))).astype(np.float32)
    t = np.arange(T, dtype=np.float32)[:, None]
    ang = (t * theta[None, :]).astype(np.float32)
    cos = np.cos(ang.astype(np.float64)).astype(np.float32)
    sin = np.sin(ang.astype(np.float64)).astype(np.float32)
    C1 = np.zeros((128, T), np.float32)
    C2 = np.zeros((128, T), np.float32)
    for p in range(128):
        lp = p % 64
        quad = lp // 32
        r = lp % 32
        if r < 16:
            i_idx, is_even = r + 16 * quad, True
        else:
            i_idx, is_even = (r - 16) + 16 * quad, False
        C1[p] = cos[:, i_idx]
        C2[p] = (-sin[:, i_idx]) if is_even else sin[:, i_idx]
    return C1, C2


def _consts_pack(b1, b2, bq_p, bk_p, bp):
    A = np.zeros((128, CONST_COLS), np.float32)
    for c2 in range(NCH):
        for j in range(128):
            h = 2 * c2 + (1 if j >= 64 else 0)
            A[h, 128 * c2 + j] = 1.0
    for p in range(128):
        A[p, 1024:1024 + p] = NEG
    A[:, 1152:1184] = b1.reshape(32, 128).T
    A[:, 1184:1192] = b2.reshape(8, 128).T
    A[:, 1192:1200] = bq_p.reshape(8, 128).T
    A[:, 1200:1208] = bk_p.reshape(8, 128).T
    A[:, 1208:1216] = bp.reshape(8, 128).T
    return A


_NC_CACHE = {}


def kernel(L0_attn_x, L0_attn_wq, L0_attn_bq, L0_attn_wk, L0_attn_bk,
           L0_attn_wv, L0_attn_bv, L0_attn_wp, L0_attn_bp,
           L0_ffn_W1, L0_ffn_b1, L0_ffn_W2, L0_ffn_b2):
    x = np.asarray(L0_attn_x, dtype=np.float32)
    perm = _perm()
    wq_p = np.ascontiguousarray(np.asarray(L0_attn_wq, np.float32)[perm])
    wk_p = np.ascontiguousarray(np.asarray(L0_attn_wk, np.float32)[perm])
    bq_p = np.ascontiguousarray(np.asarray(L0_attn_bq, np.float32)[perm])
    bk_p = np.ascontiguousarray(np.asarray(L0_attn_bk, np.float32)[perm])
    wv = np.ascontiguousarray(np.asarray(L0_attn_wv, np.float32))
    wp = np.ascontiguousarray(np.asarray(L0_attn_wp, np.float32))
    w1 = np.ascontiguousarray(np.asarray(L0_ffn_W1, np.float32))
    w2 = np.ascontiguousarray(np.asarray(L0_ffn_W2, np.float32))
    C1, C2 = _rope_tables()
    consts = _consts_pack(np.asarray(L0_ffn_b1, np.float32), np.asarray(L0_ffn_b2, np.float32),
                          bq_p, bk_p, np.asarray(L0_attn_bp, np.float32))
    bvr = np.ascontiguousarray(np.asarray(L0_attn_bv, np.float32).reshape(1, C))

    if "nc" not in _NC_CACHE:
        _NC_CACHE["nc"] = build_nc()
    nc = _NC_CACHE["nc"]

    in_maps = []
    for b in range(8):
        in_maps.append({
            "xfm": np.ascontiguousarray(x[b].T),
            "wq": wq_p, "wk": wk_p, "wv": wv, "wp": wp,
            "w1": w1, "w2": w2,
            "cr1": C1, "cr2": C2, "consts": consts, "bvr": bvr,
        })
    res = run_bass_kernel_spmd(nc, in_maps, core_ids=list(range(8)))
    out = np.stack([np.asarray(r["out_fm"]).T for r in res.results], axis=0)
    return out.astype(np.float32)


# version 04
# speedup vs baseline: 1.3472x; 1.3472x over previous
import os
import numpy as np
from contextlib import ExitStack

import concourse.bass as bass
import concourse.mybir as mybir
from concourse.bass_utils import run_bass_kernel_spmd

F32 = mybir.dt.float32
BF16 = mybir.dt.bfloat16
AF = mybir.ActivationFunctionType
ALU = mybir.AluOpType
AX = mybir.AxisListType

T = 2048
C = 1024
H = 16
DH = 64
FF = 4096
NCH = 8
NTB = 16
NEG = -1.0e10
EPS_LN = 1e-5
EPS_Q = 1e-5

XH = 0
QQ = 16384
KK = 32768
VV = 49152
WT = 65792
SC = 73984
WB = 78080
TPc = 80128
SP = 82176
SP_H = SP
SP_PREP = SP + 512
SP_ONE = SP + 1536
SP_EX = SP + 1552
NBF = SP + 1552 + 1024

def w2base(fc):
    if fc < 16:
        return VV + 1024 * fc
    if fc < 24:
        return WT + 1024 * (fc - 16)
    if fc < 28:
        return SC + 1024 * (fc - 24)
    return WB + 1024 * (fc - 28)

G0 = 0
WS = 4096
SELo = 6144
MTRIo = 7168
B1o = 7296
B2o = 7328
BQo = 7336
BKo = 7344
BPo = 7352
GACCo = 7360
ONECOLo = 7424
GT = 7425
BVo = 7456
ONERo = 8480
NFA = 8992

CONST_COLS = 1232


_SCHED = None


def _semname(sm):
    return getattr(sm, "name", None) if sm is not None else None


def build_nc():
    nc = bass.Bass("TRN2", target_bir_lowering=False, detect_race_conditions=False)
    xfm = nc.dram_tensor("xfm", [C, T], F32, kind="ExternalInput").ap()
    wq = nc.dram_tensor("wq", [C, C], F32, kind="ExternalInput").ap()
    wk = nc.dram_tensor("wk", [C, C], F32, kind="ExternalInput").ap()
    wv = nc.dram_tensor("wv", [C, C], F32, kind="ExternalInput").ap()
    wp = nc.dram_tensor("wp", [C, C], F32, kind="ExternalInput").ap()
    w1 = nc.dram_tensor("w1", [FF, C], F32, kind="ExternalInput").ap()
    w2 = nc.dram_tensor("w2", [C, FF], F32, kind="ExternalInput").ap()
    cr1 = nc.dram_tensor("cr1", [128, T], F32, kind="ExternalInput").ap()
    cr2 = nc.dram_tensor("cr2", [128, T], F32, kind="ExternalInput").ap()
    consts = nc.dram_tensor("consts", [128, CONST_COLS], F32, kind="ExternalInput").ap()
    bvr = nc.dram_tensor("bvr", [1, C], F32, kind="ExternalInput").ap()
    out_fm = nc.dram_tensor("out_fm", [C, T], F32, kind="ExternalOutput").ap()
    x1d = nc.dram_tensor("x1d", [C, T], F32, kind="Internal").ap()
    sums_d = nc.dram_tensor("sums_d", [H, T], F32, kind="Internal").ap()

    with ExitStack() as st:
        BFt = st.enter_context(nc.sbuf_tensor("BF", [128, NBF], BF16))
        FAt = st.enter_context(nc.sbuf_tensor("FA", [128, NFA], F32))
        PSt = st.enter_context(nc.psum_tensor("PS", [128, 4096], F32))
        sD = st.enter_context(nc.semaphore("sD"))
        sT = st.enter_context(nc.semaphore("sT"))
        sV = st.enter_context(nc.semaphore("sV"))
        sS = st.enter_context(nc.semaphore("sS"))
        sG = st.enter_context(nc.semaphore("sG"))
        block = st.enter_context(nc.Block())

        BF = BFt.ap()
        FA = FAt.ap()
        PS = PSt.ap()

        ops = {"sync": [], "tensor": [], "vector": [], "scalar": [], "gpsimd": []}
        cnt = {"D": 0, "T": 0, "V": 0, "S": 0, "G": 0}
        sem = {"D": sD, "T": sT, "V": sV, "S": sS, "G": sG}
        seen = {e: {} for e in ops}
        ENG_OF = {"D": "sync", "T": "tensor", "V": "vector", "S": "scalar", "G": "gpsimd"}

        def emit(ch, fn, waits=(), inc=False):
            eng = ENG_OF[ch]
            w2 = []
            for (sname, v) in waits:
                if v <= 0:
                    continue
                if sname == ch:
                    continue
                if v > seen[eng].get(sname, 0):
                    seen[eng][sname] = v
                    w2.append((sname, v))
            amt = 0
            if inc:
                amt = 16 if ch == "D" else 1
                cnt[ch] += amt
            ops[eng].append((fn, w2, sem[ch] if inc else None, amt))
            return cnt[ch]

        def dma(out_ap, in_ap, waits=(), transpose=False):
            return emit(
                "D",
                lambda e, o=out_ap, i=in_ap, tr=transpose: e.dma_start(out=o, in_=i, transpose=tr),
                waits, inc=True)

        def bf3(off, span, b, r):
            return BF[:, off:off + span].rearrange("p (b r) -> p b r", b=b, r=r)

        KPH = int(os.environ.get("KPH", "9"))

        def finish():
            final_d = cnt["D"]
            ops["sync"].append((lambda e: None, [("D", final_d)], None, 0))
            import kernel as _km
            _km._SCHED = {"ops": {e: [(w, _semname(sm), amt)
                                      for (fn, w, sm, amt) in ops[e]] for e in ops}}

            def emit_block(eng_name, e):
                for (fn, waits, s_, amt) in ops[eng_name]:
                    for (sname, v) in waits:
                        e.wait_ge(sem[sname], v)
                    inst = fn(e)
                    if s_ is not None:
                        inst.then_inc(s_, amt)

            @block.sync
            def _(e):
                emit_block("sync", e)

            @block.tensor
            def _(e):
                emit_block("tensor", e)

            @block.vector
            def _(e):
                emit_block("vector", e)

            @block.scalar
            def _(e):
                emit_block("scalar", e)

            @block.gpsimd
            def _(e):
                emit_block("gpsimd", e)

        d_const = dma(FA[:, SELo:SELo + CONST_COLS], consts)
        d_bv = dma(FA[0:1, BVo:BVo + C], bvr)
        v_ones = emit("V", lambda e: e.memset(FA[0:1, ONERo:ONERo + 512], 1.0), inc=True)
        v_ones = emit("V", lambda e: e.memset(FA[:, ONECOLo:ONECOLo + 1], 1.0), inc=True)
        v_ones = emit("V", lambda e: e.memset(BF[:, SP_ONE:SP_ONE + 16], 1.0), inc=True)
        vone_ap = BF[:, VV:VV + 16640].rearrange("p (t h d) -> p t h d", t=16, h=16, d=65)[:, :, :, 64:65]
        v_ones = emit("V", lambda e: e.memset(vone_ap, 1.0), inc=True)

        ONES_BF = BF[:, SP_ONE:SP_ONE + 1]
        ONER128 = FA[0:1, ONERo:ONERo + 128]

        def ln_phase(src_dram, first):
            t_before = cnt["T"]
            d_x = [0] * NCH
            v_st = [0] * NCH
            for c in range(NCH):
                w = []
                if c >= 2:
                    w.append(("V", v_st[c - 2]))
                d_x[c] = dma(FA[:, G0 + 2048 * (c % 2):G0 + 2048 * (c % 2) + 2048],
                             src_dram[:, :].rearrange("(a p) m -> a p m", a=NCH, p=128)[c],
                             waits=w)
                for g in range(4):
                    xsrc = FA[:, G0 + 2048 * (c % 2) + 512 * g:G0 + 2048 * (c % 2) + 512 * (g + 1)]
                    exb = BF[:, SP_EX:SP_EX + 512]
                    exq = BF[:, SP_EX + 512:SP_EX + 1024]
                    wv_ = [("D", d_x[c])]
                    if c > 0 or g > 0:
                        wv_.append(("T", cnt["T"]))
                    emit("V", lambda e, o=exb, i=xsrc: e.tensor_copy(o, i), wv_)
                    v_c = emit("V", lambda e, o=exq, i=exb: e.tensor_mul(o, i, i), inc=True)
                    emit("T", lambda e, o=PS[0:1, 512 * g:512 * g + 512], l=ONES_BF, r=exb,
                         s=(c == 0), p=(c == NCH - 1):
                         e.matmul(o, l, r, start=s, stop=p), [("V", v_c)])
                    t_c = emit("T", lambda e, o=PS[0:1, 2048 + 512 * g:2048 + 512 * g + 512],
                               l=ONES_BF, r=exq, s=(c == 0), p=(c == NCH - 1):
                               e.matmul(o, l, r, start=s, stop=p), [("V", v_c)], inc=True)
                v_st[c] = cnt["V"]
            t_stats = cnt["T"]
            slotA = FA[0:1, 0:2048]
            slotB = FA[0:1, 2048:4096]
            emit("V", lambda e: e.tensor_scalar(slotA, PS[0:1, 0:2048], 1.0 / C, None, ALU.mult),
                 [("T", t_stats)])
            emit("V", lambda e: e.tensor_mul(slotB, slotA, slotA))
            v_var = emit("V", lambda e: e.scalar_tensor_tensor(
                slotB, PS[0:1, 2048:4096], 1.0 / C, slotB, ALU.mult, ALU.subtract),
                [("T", t_stats)], inc=True)
            emit("V", lambda e: e.tensor_scalar(slotB, slotB, EPS_LN, None, ALU.add))
            s_sd = emit("S", lambda e: e.activation(slotB, slotB, AF.Sqrt),
                        [("V", v_var)], inc=True)
            emit("V", lambda e: e.reciprocal(slotB, slotB), [("S", s_sd)])
            v_negm = emit("V", lambda e: e.tensor_scalar(
                slotA, slotA, -1.0, None, ALU.mult), inc=True)
            t_bc = 0
            for g in range(4):
                emit("T", lambda e, o=PS[:, 512 * g:512 * g + 512],
                     r=FA[0:1, 2048 + 512 * g:2048 + 512 * (g + 1)]:
                     e.matmul(o, ONER128, r, start=True, stop=True), [("V", v_negm)])
                t_bc = emit("T", lambda e, o=PS[:, 2048 + 512 * g:2048 + 512 * g + 512],
                            r=FA[0:1, 512 * g:512 * (g + 1)]:
                            e.matmul(o, ONER128, r, start=True, stop=True),
                            [("V", v_negm)], inc=(g == 3))
            v_ap = [0] * NCH
            for c in range(NCH):
                w = [("T", t_bc)]
                if c >= 2:
                    w.append(("V", v_ap[c - 2]))
                d_a = dma(FA[:, G0 + 2048 * (c % 2):G0 + 2048 * (c % 2) + 2048],
                          src_dram[:, :].rearrange("(a p) m -> a p m", a=NCH, p=128)[c],
                          waits=w)
                xs = FA[:, G0 + 2048 * (c % 2):G0 + 2048 * (c % 2) + 2048]
                emit("V", lambda e, x=xs: e.tensor_add(x, x, PS[:, 2048:4096]),
                     [("D", d_a), ("T", t_bc)])
                v_ap[c] = emit("V", lambda e, x=xs, o=BF[:, XH + 2048 * c:XH + 2048 * (c + 1)]:
                               e.tensor_mul(o, x, PS[:, 0:2048]), inc=True)
            return v_ap

        v_ap1 = ln_phase(xfm, True)
        if KPH < 2:
            finish()
            return nc

        d_c1 = dma(FA[:, 0:2048], cr1, waits=[("V", v_ap1[6])])
        d_c2 = dma(FA[:, 2048:4096], cr2, waits=[("V", v_ap1[7])])

        prev_cast = [0]

        def prep_weight(wdram, wait_wt_free_t):
            v_cast = [0] * 8
            d_tr = [0] * 8
            for ot in range(8):
                w = []
                if ot >= 2:
                    w.append(("V", v_cast[ot - 2]))
                else:
                    w.append(("V", prev_cast[0]))
                d_w = dma(FA[:, WS + 1024 * (ot % 2):WS + 1024 * (ot % 2) + 1024],
                          wdram[128 * ot:128 * (ot + 1), :], waits=w)
                wc = [("D", d_w)]
                if ot >= 2:
                    wc.append(("D", d_tr[ot - 2]))
                else:
                    wc.append(("D", prev_cast[1] if len(prev_cast) > 1 else 0))
                v_cast[ot] = emit("V", lambda e, o=BF[:, WB + 1024 * (ot % 2):WB + 1024 * (ot % 2) + 1024],
                                  i=FA[:, WS + 1024 * (ot % 2):WS + 1024 * (ot % 2) + 1024]:
                                  e.tensor_copy(o, i), wc, inc=True)
                w = [("V", v_cast[ot])]
                if ot == 0 and wait_wt_free_t:
                    w.append(("T", wait_wt_free_t))
                oap = bf3(WT, 8192, 8, 1024)[:, :, 128 * ot:128 * ot + 128]
                d_tr[ot] = dma(oap, BF[:, WB + 1024 * (ot % 2):WB + 1024 * (ot % 2) + 1024],
                               waits=w, transpose=True)
            prev_cast[0] = v_cast[7]
            if len(prev_cast) > 1:
                prev_cast[1] = d_tr[7]
            else:
                prev_cast.append(d_tr[7])
            return d_tr[7]

        def fm_proj(dst_off, bias_off, d_wt, extra_first_waits):
            v_rot = [0] * NCH
            s_qb = [0] * NCH
            t_oc = [0] * NCH
            for oc in range(NCH):
                pcol = (oc % 2) * 2048
                w = [("D", d_wt)]
                if oc == 0:
                    w += extra_first_waits
                if oc >= 2:
                    w.append(("S", s_qb[oc - 2]))
                for cc in range(NCH):
                    for g in range(4):
                        last = (cc == NCH - 1 and g == 3)
                        t_oc[oc] = emit("T", lambda e,
                                        o=PS[:, pcol + 512 * g:pcol + 512 * (g + 1)],
                                        l=BF[:, WT + 1024 * cc + 128 * oc:WT + 1024 * cc + 128 * oc + 128],
                                        r=BF[:, XH + 2048 * cc + 512 * g:XH + 2048 * cc + 512 * (g + 1)],
                                        s=(cc == 0), p=(cc == NCH - 1):
                                        e.matmul(o, l, r, start=s, stop=p),
                                        w, inc=last)
                    w = []
                qb = BF[:, SC:SC + 2048]
                qs = BF[:, SC + 2048:SC + 4096]
                wS = [("T", t_oc[oc])]
                if oc >= 1:
                    wS.append(("V", v_rot[oc - 1]))
                s_qb[oc] = emit("S", lambda e, o=qb, i=PS[:, pcol:pcol + 2048],
                                b=FA[:, bias_off + oc:bias_off + oc + 1]:
                                e.activation(o, i, AF.Identity, bias=b), wS, inc=True)
                emit("V", lambda e, o=qs, i=qb:
                     e.stream_shuffle(o, i, mask=[(j + 16) % 32 for j in range(32)]),
                     [("S", s_qb[oc])])
                emit("V", lambda e, o=qb, i=qb: e.tensor_mul(o, i, FA[:, 0:2048]),
                     [("D", d_c1)])
                emit("V", lambda e, o=qs, i=qs: e.tensor_mul(o, i, FA[:, 2048:4096]),
                     [("D", d_c2)])
                v_rot[oc] = emit("V", lambda e, o=BF[:, dst_off + 2048 * oc:dst_off + 2048 * (oc + 1)],
                                 a=qb, b=qs: e.tensor_add(o, a, b), inc=True)
            return t_oc[NCH - 1], v_rot, s_qb

        d_wkT = prep_weight(wk, 0)
        t_kproj, v_rotk, s_qbk = fm_proj(KK, BKo, d_wkT, [("V", v_ap1[7])])
        d_wvT = prep_weight(wv, t_kproj)
        t_vp = [0] * NTB
        v_vev = [0] * NTB
        for tb in range(NTB):
            pcol = 2048 + 1024 * (tb % 2)
            w = [("D", d_wvT)]
            if tb < 2:
                w.append(("S", s_qbk[NCH - 1]))
            if tb >= 2:
                w.append(("V", v_vev[tb - 2]))
            for cc in range(NCH):
                for g in range(2):
                    emit("T", lambda e, o=PS[:, pcol + 512 * g:pcol + 512 * (g + 1)],
                         l=BF[:, XH + 2048 * cc + 128 * tb:XH + 2048 * cc + 128 * tb + 128],
                         r=BF[:, WT + 1024 * cc + 512 * g:WT + 1024 * cc + 512 * (g + 1)],
                         s=(cc == 0): e.matmul(o, l, r, start=s, stop=False), w)
                w = []
            for g in range(2):
                t_vp[tb] = emit("T", lambda e, o=PS[:, pcol + 512 * g:pcol + 512 * (g + 1)],
                                l=ONER128, r=FA[0:1, BVo + 512 * g:BVo + 512 * (g + 1)]:
                                e.matmul(o, l, r, start=False, stop=True),
                                [("D", d_bv)], inc=(g == 1))
            oap = BF[:, VV + 1040 * tb:VV + 1040 * tb + 1040].rearrange(
                "p (h d) -> p h d", h=16, d=65)[:, :, 0:64]
            iap = PS[:, pcol:pcol + 1024].rearrange("p (h d) -> p h d", h=16, d=64)
            v_vev[tb] = emit("V", lambda e, o=oap, i=iap: e.tensor_copy(o, i),
                             [("T", t_vp[tb])], inc=True)
        d_wqT = prep_weight(wq, t_vp[NTB - 1])
        t_qproj, v_rotq, s_qbq = fm_proj(
            QQ, BQo, d_wqT, [("V", v_vev[NTB - 1]), ("S", s_qbk[NCH - 1])])

        if KPH < 3:
            finish()
            return nc
        d_wpT = prep_weight(wp, t_qproj)

        gstate = {"gi": 0, "v_gprev": [prev_cast[0], prev_cast[0]], "v_gred": 0}

        def gamma_unit():
            gi = gstate["gi"]
            if gi >= 64:
                return
            w = [("V", gstate["v_gprev"][gi % 2])]
            if gi < 64 - 32:
                iap = w1[128 * gi:128 * (gi + 1), :]
            else:
                tt = gi - 32
                iap = w2[:, :].rearrange("(a p) (b m) -> a b p m", p=128, m=1024)[tt // 4, tt % 4]
            d_g = dma(FA[:, WS + 1024 * (gi % 2):WS + 1024 * (gi % 2) + 1024], iap, waits=w)
            v = emit("V", lambda e, o=FA[:, GACCo + gi:GACCo + gi + 1],
                     i=FA[:, WS + 1024 * (gi % 2):WS + 1024 * (gi % 2) + 1024]:
                     e.tensor_reduce(o, i, AX.X, ALU.add, apply_absolute_value=True),
                     [("D", d_g)], inc=True)
            gstate["v_gprev"][gi % 2] = v
            gstate["v_gred"] = v
            gstate["gi"] = gi + 1

        pc = 0
        s_exp = {}
        t_pv_last = 0
        v_yev = [0] * H
        v_sum = [0] * H
        d_sumst = [0] * H
        pending_pv = [None]

        def flush_pv():
            nonlocal_t = pending_pv[0]
            if nonlocal_t is not None:
                nonlocal_t()
                pending_pv[0] = None

        for h in range(H):
            ch = h // 2
            pb = 64 * (h % 2)
            for _ in range(4):
                gamma_unit()
            for i in range(NTB):
                q0 = 128 * i
                Nq = T - q0
                pieces = [(q0, min(1024, Nq))]
                if Nq > 1024:
                    pieces.append((q0 + 1024, Nq - 1024))
                for pi, (qoff, plen) in enumerate(pieces):
                    par = pc % 2
                    stc = 1024 * par
                    wT_ = []
                    if pc >= 2:
                        wT_.append(("S", s_exp[pc - 2]))
                    if h == 0 and i == 0 and pi == 0:
                        wT_.append(("V", v_rotq[NCH - 1]))
                        wT_.append(("V", v_rotk[NCH - 1]))
                    t_kq = 0
                    ng = (plen + 511) // 512
                    for g in range(ng):
                        gw = min(512, plen - 512 * g)
                        t_kq = emit("T", lambda e, o=PS[:, stc + 512 * g:stc + 512 * g + gw],
                                    l=BF[pb:pb + 64, KK + 2048 * ch + q0:KK + 2048 * ch + q0 + 128],
                                    r=BF[pb:pb + 64, QQ + 2048 * ch + qoff + 512 * g:
                                         QQ + 2048 * ch + qoff + 512 * g + gw]:
                                    e.matmul(o, l, r, start=True, stop=True),
                                    wT_, inc=(g == ng - 1))
                        wT_ = []
                    wS_ = [("T", t_kq)]
                    if pi == 0:
                        v_m = emit("V", lambda e, o=PS[:, stc:stc + 128]:
                                   e.tensor_add(o, o, FA[:, MTRIo:MTRIo + 128]),
                                   [("T", t_kq), ("D", d_const)], inc=True)
                        wS_ = [("V", v_m)]
                    s_exp[pc] = emit("S", lambda e, o=BF[:, TPc + 1024 * par:TPc + 1024 * par + plen],
                                     i=PS[:, stc:stc + plen]:
                                     e.activation(o, i, AF.Exp, scale=0.125), wS_, inc=True)

                    def make_pv(pc_=pc, par_=par, qoff_=qoff, plen_=plen, i_=i, h_=h, pi_=pi):
                        def do():
                            nonlocal t_pv_last
                            wPV = [("S", s_exp[pc_])]
                            if i_ == 0 and pi_ == 0 and h_ > 0:
                                wPV.append(("V", v_sum[h_ - 1]))
                            off = 0
                            while off < plen_:
                                gw = min(512 - ((qoff_ + off) % 512), plen_ - off)
                                if pi_ == 0 and off == 0:
                                    gw = min(gw, 128)
                                stp = (pi_ == 0 and off == 0 and (i_ % 4) == 3)
                                st_ = (i_ == 0 and ((qoff_ + off) % 512) == 0)
                                t_pv_last = emit("T", lambda e,
                                                 o=PS[0:65, 2048 + qoff_ + off:2048 + qoff_ + off + gw],
                                                 l=BF[:, VV + 1040 * i_ + 65 * h_:VV + 1040 * i_ + 65 * h_ + 65],
                                                 r=BF[:, TPc + 1024 * par_ + off:TPc + 1024 * par_ + off + gw],
                                                 s=st_, p=stp:
                                                 e.matmul(o, l, r, start=s, stop=p),
                                                 wPV, inc=(off + gw >= plen_))
                                wPV = []
                                off += gw
                        return do

                    flush_pv()
                    pending_pv[0] = make_pv()
                    pc += 1
            flush_pv()
            wY = [("T", t_pv_last)]
            if h == 0:
                wY.append(("T", t_qproj))
            v_yev[h] = emit("V", lambda e, o=BF[pb:pb + 64, XH + 2048 * ch:XH + 2048 * (ch + 1)],
                            i=PS[0:64, 2048:4096]: e.tensor_copy(o, i), wY, inc=True)
            wSm = []
            if h >= 1:
                wSm.append(("D", d_sumst[h - 1]))
            v_sum[h] = emit("V", lambda e, o=FA[0:1, 0:2048], i=PS[64:65, 2048:4096]:
                            e.tensor_copy(o, i), wSm, inc=True)
            d_sumst[h] = dma(sums_d[h:h + 1, :], FA[0:1, 0:2048], waits=[("V", v_sum[h])])

        d_sumld = dma(FA[0:16, 0:2048], sums_d, waits=[("D", d_sumst[H - 1])])
        v_inv = emit("V", lambda e: e.reciprocal(FA[0:16, 0:2048], FA[0:16, 0:2048]),
                     [("D", d_sumld)], inc=True)
        t_selbc = [0] * NCH
        v_ynorm = [0] * NCH
        for c2 in range(NCH):
            pcol = (c2 % 2) * 2048
            w = [("V", v_inv), ("D", d_const)]
            if c2 >= 2:
                w.append(("V", v_ynorm[c2 - 2]))
            for g in range(4):
                t_selbc[c2] = emit("T", lambda e, o=PS[:, pcol + 512 * g:pcol + 512 * (g + 1)],
                                   l=FA[0:16, SELo + 128 * c2:SELo + 128 * (c2 + 1)],
                                   r=FA[0:16, 512 * g:512 * (g + 1)]:
                                   e.matmul(o, l, r, start=True, stop=True), w, inc=(g == 3))
                w = []
            v_ynorm[c2] = emit("V", lambda e, o=BF[:, XH + 2048 * c2:XH + 2048 * (c2 + 1)],
                               i=PS[:, pcol:pcol + 2048]:
                               e.tensor_mul(o, o, i),
                               [("T", t_selbc[c2]), ("V", v_yev[2 * c2 + 1])], inc=True)

        if KPH < 4:
            finish()
            return nc
        while gstate["gi"] < 64:
            gamma_unit()
        emit("V", lambda e: e.tensor_reduce(FA[:, GT:GT + 1], FA[:, GACCo:GACCo + 32], AX.X, ALU.add),
             [("V", gstate["v_gred"])])
        v_g2 = emit("V", lambda e: e.tensor_reduce(FA[:, GT + 1:GT + 2], FA[:, GACCo + 32:GACCo + 64],
                                                   AX.X, ALU.add), inc=True)
        emit("T", lambda e: e.matmul(PS[0:1, 0:1], FA[:, ONECOLo:ONECOLo + 1], FA[:, GT:GT + 1],
                                     start=True, stop=True),
             [("V", v_g2), ("V", v_ynorm[NCH - 1])])
        t_gm = emit("T", lambda e: e.matmul(PS[0:1, 1:2], FA[:, ONECOLo:ONECOLo + 1],
                                            FA[:, GT + 1:GT + 2], start=True, stop=True),
                    [("V", v_g2)], inc=True)
        emit("V", lambda e: e.tensor_scalar(FA[0:1, GT + 4:GT + 5], PS[0:1, 0:1],
                                            1.0 / (FF * C), EPS_Q, ALU.mult, ALU.add),
             [("T", t_gm)])
        emit("V", lambda e: e.tensor_scalar(FA[0:1, GT + 5:GT + 6], PS[0:1, 1:2],
                                            1.0 / (FF * C), EPS_Q, ALU.mult, ALU.add))
        emit("V", lambda e: e.reciprocal(FA[0:1, GT + 4:GT + 5], FA[0:1, GT + 4:GT + 5]))
        v_gi = emit("V", lambda e: e.reciprocal(FA[0:1, GT + 5:GT + 6], FA[0:1, GT + 5:GT + 6]),
                    inc=True)
        emit("T", lambda e: e.matmul(PS[:, 4:5], ONER128, FA[0:1, GT + 4:GT + 5],
                                     start=True, stop=True), [("V", v_gi)])
        t_gb = emit("T", lambda e: e.matmul(PS[:, 5:6], ONER128, FA[0:1, GT + 5:GT + 6],
                                            start=True, stop=True), inc=True)
        emit("V", lambda e: e.tensor_copy(FA[:, GT + 2:GT + 3], PS[:, 4:5]), [("T", t_gb)])
        v_sg = emit("V", lambda e: e.tensor_copy(FA[:, GT + 3:GT + 4], PS[:, 5:6]), inc=True)
        SG1 = FA[:, GT + 2:GT + 3]
        SG2 = FA[:, GT + 3:GT + 4]

        t_pp = [0] * NCH
        v_res = [0] * NCH
        d_x1w = [0] * NCH
        for oc in range(NCH):
            pcol = (oc % 2) * 2048
            w = [("D", d_wpT), ("V", v_sg), ("V", v_ynorm[NCH - 1])]
            if oc >= 2:
                w.append(("V", v_res[oc - 2]))
            for cc in range(NCH):
                for g in range(4):
                    t_pp[oc] = emit("T", lambda e, o=PS[:, pcol + 512 * g:pcol + 512 * (g + 1)],
                                    l=BF[:, WT + 1024 * cc + 128 * oc:WT + 1024 * cc + 128 * oc + 128],
                                    r=BF[:, XH + 2048 * cc + 512 * g:XH + 2048 * cc + 512 * (g + 1)],
                                    s=(cc == 0), p=(cc == NCH - 1): e.matmul(o, l, r, start=s, stop=p),
                                    w, inc=(cc == NCH - 1 and g == 3))
                    w = []
            wD = [("T", t_selbc[NCH - 1]), ("V", v_ynorm[NCH - 1])]
            if oc >= 2:
                wD.append(("V", v_res[oc - 2]))
            d_xr = dma(FA[:, G0 + 2048 * (oc % 2):G0 + 2048 * (oc % 2) + 2048],
                       xfm[:, :].rearrange("(a p) m -> a p m", a=NCH, p=128)[oc], waits=wD)
            xs = FA[:, G0 + 2048 * (oc % 2):G0 + 2048 * (oc % 2) + 2048]
            v_res[oc] = emit("V", lambda e, x=xs, i=PS[:, pcol:pcol + 2048],
                             b=FA[:, BPo + oc:BPo + oc + 1]:
                             e.scalar_tensor_tensor(x, i, b, x, ALU.add, ALU.add),
                             [("T", t_pp[oc]), ("D", d_xr)], inc=True)
            d_x1w[oc] = dma(x1d[:, :].rearrange("(a p) m -> a p m", a=NCH, p=128)[oc], xs,
                            waits=[("V", v_res[oc])])

        v_q2 = {}
        d_w2tr = 0
        d_w2rdy = [0] * 32
        sl_order = [0, 1, 2, 3, 6, 7, 4, 5]
        qi = 0
        for sl in sl_order:
            for dt in range(8):
                w = [("V", v_sg)]
                if qi >= 2:
                    w.append(("V", v_q2[qi - 2]))
                d_g = dma(FA[:, WS + 512 * (qi % 2):WS + 512 * (qi % 2) + 512],
                          w2[128 * dt:128 * (dt + 1), 512 * sl:512 * (sl + 1)], waits=w)
                src = FA[:, WS + 512 * (qi % 2):WS + 512 * (qi % 2) + 512]
                t1 = BF[:, SP_PREP:SP_PREP + 512]
                t2 = BF[:, SP_PREP + 512:SP_PREP + 1024]
                emit("V", lambda e, o=t1, i=src: e.tensor_scalar(o, i, SG2, 0.5, ALU.mult, ALU.is_ge),
                     [("D", d_g), ("V", v_sg), ("D", d_w2tr)])
                emit("V", lambda e, o=t2, i=src: e.tensor_scalar(o, i, SG2, -0.5, ALU.mult, ALU.is_le))
                v_q2[qi] = emit("V", lambda e, o=t1, a=t1, b=t2: e.tensor_sub(o, a, b), inc=True)
                wtr = [("V", v_q2[qi]), ("T", t_pv_last)]
                if sl in (4, 5):
                    wtr.append(("T", t_pp[NCH - 1]))
                oap = BF[:, w2base(4 * sl):w2base(4 * sl) + 4096].rearrange(
                    "p (b r) -> p b r", b=4, r=1024)[:, :, 128 * dt:128 * dt + 128]
                d_w2tr = dma(oap, t1, waits=wtr, transpose=True)
                if dt == 7:
                    for b_ in range(4):
                        d_w2rdy[4 * sl + b_] = d_w2tr
                qi += 1

        if KPH < 5:
            finish()
            return nc
        ln2_gate_t = t_pp[NCH - 1]
        ln2_gate_v = v_res[NCH - 1]
        v_ap2 = None

        def ln_phase2():
            d_x = [0] * NCH
            v_st = [0] * NCH
            for c in range(NCH):
                w = [("D", d_x1w[c])]
                if c >= 2:
                    w.append(("V", v_st[c - 2]))
                d_x[c] = dma(FA[:, G0 + 2048 * (c % 2):G0 + 2048 * (c % 2) + 2048],
                             x1d[:, :].rearrange("(a p) m -> a p m", a=NCH, p=128)[c], waits=w)
                for g in range(4):
                    xsrc = FA[:, G0 + 2048 * (c % 2) + 512 * g:G0 + 2048 * (c % 2) + 512 * (g + 1)]
                    exb = BF[:, SP_EX:SP_EX + 512]
                    exq = BF[:, SP_EX + 512:SP_EX + 1024]
                    wv_ = [("D", d_x[c]), ("T", ln2_gate_t)]
                    if c > 0 or g > 0:
                        wv_.append(("T", cnt["T"]))
                    emit("V", lambda e, o=exb, i=xsrc: e.tensor_copy(o, i), wv_)
                    v_c = emit("V", lambda e, o=exq, i=exb: e.tensor_mul(o, i, i), inc=True)
                    emit("T", lambda e, o=PS[0:1, 512 * g:512 * g + 512], l=ONES_BF, r=exb,
                         s=(c == 0), p=(c == NCH - 1):
                         e.matmul(o, l, r, start=s, stop=p), [("V", v_c), ("V", ln2_gate_v)])
                    emit("T", lambda e, o=PS[0:1, 2048 + 512 * g:2048 + 512 * g + 512],
                         l=ONES_BF, r=exq, s=(c == 0), p=(c == NCH - 1):
                         e.matmul(o, l, r, start=s, stop=p), [("V", v_c)], inc=True)
                v_st[c] = cnt["V"]
            t_stats = cnt["T"]
            slotA = FA[0:1, 0:2048]
            slotB = FA[0:1, 2048:4096]
            emit("V", lambda e: e.tensor_scalar(slotA, PS[0:1, 0:2048], 1.0 / C, None, ALU.mult),
                 [("T", t_stats)])
            emit("V", lambda e: e.tensor_mul(slotB, slotA, slotA))
            v_var = emit("V", lambda e: e.scalar_tensor_tensor(
                slotB, PS[0:1, 2048:4096], 1.0 / C, slotB, ALU.mult, ALU.subtract),
                [("T", t_stats)], inc=True)
            emit("V", lambda e: e.tensor_scalar(slotB, slotB, EPS_LN, None, ALU.add))
            s_sd = emit("S", lambda e: e.activation(slotB, slotB, AF.Sqrt),
                        [("V", v_var)], inc=True)
            emit("V", lambda e: e.reciprocal(slotB, slotB), [("S", s_sd)])
            v_negm = emit("V", lambda e: e.tensor_scalar(
                slotA, slotA, -1.0, None, ALU.mult), inc=True)
            t_bc = 0
            for g in range(4):
                emit("T", lambda e, o=PS[:, 512 * g:512 * g + 512],
                     r=FA[0:1, 2048 + 512 * g:2048 + 512 * (g + 1)]:
                     e.matmul(o, ONER128, r, start=True, stop=True), [("V", v_negm)])
                t_bc = emit("T", lambda e, o=PS[:, 2048 + 512 * g:2048 + 512 * g + 512],
                            r=FA[0:1, 512 * g:512 * (g + 1)]:
                            e.matmul(o, ONER128, r, start=True, stop=True),
                            [("V", v_negm)], inc=(g == 3))
            v_ap = [0] * NCH
            for c in range(NCH):
                w = [("T", t_bc)]
                if c >= 2:
                    w.append(("V", v_ap[c - 2]))
                d_a = dma(FA[:, G0 + 2048 * (c % 2):G0 + 2048 * (c % 2) + 2048],
                          x1d[:, :].rearrange("(a p) m -> a p m", a=NCH, p=128)[c], waits=w)
                xs = FA[:, G0 + 2048 * (c % 2):G0 + 2048 * (c % 2) + 2048]
                emit("V", lambda e, x=xs: e.tensor_add(x, x, PS[:, 2048:4096]),
                     [("D", d_a), ("T", t_bc)])
                v_ap[c] = emit("V", lambda e, x=xs, o=BF[:, XH + 2048 * c:XH + 2048 * (c + 1)]:
                               e.tensor_mul(o, x, PS[:, 0:2048]), inc=True)
            return v_ap

        v_ap2 = ln_phase2()

        v_q1 = {}
        d_w1tr = 0
        d_w1rdy = [0] * 32
        for fc in range(32):
            w = [("V", v_sg)]
            if fc >= 2:
                w.append(("V", v_q1[fc - 2]))
            else:
                w.append(("V", v_q2[qi - 1]))
            d_g = dma(FA[:, WS + 1024 * (fc % 2):WS + 1024 * (fc % 2) + 1024],
                      w1[128 * fc:128 * (fc + 1), :], waits=w)
            src = FA[:, WS + 1024 * (fc % 2):WS + 1024 * (fc % 2) + 1024]
            t1 = BF[:, SP_PREP:SP_PREP + 1024]
            t2 = BF[:, SP_EX:SP_EX + 1024]
            wq_ = [("D", d_g), ("V", v_sg), ("D", d_w2tr), ("D", d_w1tr)]
            emit("V", lambda e, o=t1, i=src: e.tensor_scalar(o, i, SG1, 0.5, ALU.mult, ALU.is_ge), wq_)
            emit("V", lambda e, o=t2, i=src: e.tensor_scalar(o, i, SG1, -0.5, ALU.mult, ALU.is_le))
            v_q1[fc] = emit("V", lambda e, o=t1, a=t1, b=t2: e.tensor_sub(o, a, b), inc=True)
            oap = BF[:, QQ:QQ + 32768].rearrange("p (b r) -> p b r", b=8, r=4096)[
                :, :, 128 * fc:128 * fc + 128]
            wtr = [("V", v_q1[fc]), ("T", t_pv_last)]
            d_w1tr = dma(oap, t1, waits=wtr, transpose=True)
            d_w1rdy[fc] = d_w1tr

        if KPH < 6:
            finish()
            return nc
        t_mm1 = {}
        s_relu = {}
        t_mm2 = {}
        v_out = [0] * 8
        d_out = [0] * 8
        for tp in range(8):
            w = [("T", t_selbc[NCH - 1])]
            if tp >= 2:
                w.append(("V", v_out[tp - 2]))
            if tp < 2:
                w.append(("V", v_ap2[NCH - 1]))
                w.append(("D", d_x1w[NCH - 1]))
            d_x1p = dma(FA[:, G0 + 2048 * (tp % 2):G0 + 2048 * (tp % 2) + 2048].rearrange(
                "p (a m) -> p a m", a=8, m=256),
                x1d[:, :].rearrange("(a p) m -> p a m", a=NCH, p=128)[:, :, 256 * tp:256 * (tp + 1)],
                waits=w)
            for fc in range(32):
                hcol = 512 * (fc % 2)
                wmm1 = [("D", d_w1rdy[fc]), ("V", v_ap2[NCH - 1])]
                prev = (tp, fc - 2) if fc >= 2 else ((tp - 1, 30 + fc) if tp >= 1 else None)
                if prev in s_relu:
                    wmm1.append(("S", s_relu[prev]))
                for cc in range(NCH):
                    t_mm1[(tp, fc)] = emit("T", lambda e, o=PS[:, hcol:hcol + 256],
                                           l=BF[:, QQ + 4096 * cc + 128 * fc:QQ + 4096 * cc + 128 * fc + 128],
                                           r=BF[:, XH + 2048 * cc + 256 * tp:XH + 2048 * cc + 256 * (tp + 1)],
                                           s=(cc == 0), p=(cc == NCH - 1): e.matmul(o, l, r, start=s, stop=p),
                                           wmm1, inc=(cc == NCH - 1))
                    wmm1 = []
                s_relu[(tp, fc)] = emit("S", lambda e, o=BF[:, SP_H + 256 * (fc % 2):SP_H + 256 * (fc % 2) + 256],
                                        i=PS[:, hcol:hcol + 256], b=FA[:, B1o + fc:B1o + fc + 1]:
                                        e.activation(o, i, AF.Relu, bias=b),
                                        [("T", t_mm1[(tp, fc)])], inc=True)
                wmm2 = [("S", s_relu[(tp, fc)]), ("D", d_w2rdy[fc])]
                if fc == 0 and tp >= 1:
                    wmm2.append(("V", v_out[tp - 1]))
                for dc in range(NCH):
                    st_ = (fc == 0 and dc % 2 == 0)
                    t_mm2[(tp, fc)] = emit("T", lambda e, o=PS[:, 2048 + 256 * dc:2048 + 256 * (dc + 1)],
                                           l=BF[:, w2base(fc) + 128 * dc:w2base(fc) + 128 * dc + 128],
                                           r=BF[:, SP_H + 256 * (fc % 2):SP_H + 256 * (fc % 2) + 256],
                                           s=st_, p=(fc == 31 and dc % 2 == 1):
                                           e.matmul(o, l, r, start=s, stop=p),
                                           wmm2, inc=(dc == NCH - 1))
                    wmm2 = []
            for dc in range(NCH):
                xs = FA[:, G0 + 2048 * (tp % 2) + 256 * dc:G0 + 2048 * (tp % 2) + 256 * (dc + 1)]
                v_out[tp] = emit("V", lambda e, x=xs, i=PS[:, 2048 + 256 * dc:2048 + 256 * (dc + 1)],
                                 b=FA[:, B2o + dc:B2o + dc + 1]:
                                 e.scalar_tensor_tensor(x, i, b, x, ALU.add, ALU.add),
                                 [("T", t_mm2[(tp, 31)]), ("D", d_x1p)], inc=(dc == NCH - 1))
            d_out[tp] = dma(out_fm[:, :].rearrange("(a p) m -> p a m", a=NCH, p=128)[
                :, :, 256 * tp:256 * (tp + 1)],
                FA[:, G0 + 2048 * (tp % 2):G0 + 2048 * (tp % 2) + 2048].rearrange(
                    "p (a m) -> p a m", a=8, m=256),
                waits=[("V", v_out[tp])])

        finish()
    return nc


def _perm():
    perm = np.zeros(C, dtype=np.int64)
    for g in range(C):
        chk, p = g // 128, g % 128
        h = 2 * chk + p // 64
        lp = p % 64
        quad = lp // 32
        r = lp % 32
        if r < 16:
            i_idx, par = r + 16 * quad, 0
        else:
            i_idx, par = (r - 16) + 16 * quad, 1
        perm[g] = 64 * h + 2 * i_idx + par
    return perm


def _rope_tables():
    i = np.arange(32, dtype=np.float32)
    theta = (np.float32(1.0) / (np.float32(10000.0) ** (np.float32(2.0) * i / np.float32(64)))).astype(np.float32)
    t = np.arange(T, dtype=np.float32)[:, None]
    ang = (t * theta[None, :]).astype(np.float32)
    cos = np.cos(ang.astype(np.float64)).astype(np.float32)
    sin = np.sin(ang.astype(np.float64)).astype(np.float32)
    C1 = np.zeros((128, T), np.float32)
    C2 = np.zeros((128, T), np.float32)
    for p in range(128):
        lp = p % 64
        quad = lp // 32
        r = lp % 32
        if r < 16:
            i_idx, is_even = r + 16 * quad, True
        else:
            i_idx, is_even = (r - 16) + 16 * quad, False
        C1[p] = cos[:, i_idx]
        C2[p] = (-sin[:, i_idx]) if is_even else sin[:, i_idx]
    return C1, C2


def _consts_pack(b1, b2, bq_p, bk_p, bp):
    A = np.zeros((128, CONST_COLS), np.float32)
    for c2 in range(NCH):
        for j in range(128):
            h = 2 * c2 + (1 if j >= 64 else 0)
            A[h, 128 * c2 + j] = 1.0
    for p in range(128):
        A[p, 1024:1024 + p] = NEG
    A[:, 1152:1184] = b1.reshape(32, 128).T
    A[:, 1184:1192] = b2.reshape(8, 128).T
    A[:, 1192:1200] = bq_p.reshape(8, 128).T
    A[:, 1200:1208] = bk_p.reshape(8, 128).T
    A[:, 1208:1216] = bp.reshape(8, 128).T
    return A


_NC_CACHE = {}


def kernel(L0_attn_x, L0_attn_wq, L0_attn_bq, L0_attn_wk, L0_attn_bk,
           L0_attn_wv, L0_attn_bv, L0_attn_wp, L0_attn_bp,
           L0_ffn_W1, L0_ffn_b1, L0_ffn_W2, L0_ffn_b2):
    x = np.asarray(L0_attn_x, dtype=np.float32)
    perm = _perm()
    wq_p = np.ascontiguousarray(np.asarray(L0_attn_wq, np.float32)[perm])
    wk_p = np.ascontiguousarray(np.asarray(L0_attn_wk, np.float32)[perm])
    bq_p = np.ascontiguousarray(np.asarray(L0_attn_bq, np.float32)[perm])
    bk_p = np.ascontiguousarray(np.asarray(L0_attn_bk, np.float32)[perm])
    wv = np.ascontiguousarray(np.asarray(L0_attn_wv, np.float32))
    wp = np.ascontiguousarray(np.asarray(L0_attn_wp, np.float32))
    w1 = np.ascontiguousarray(np.asarray(L0_ffn_W1, np.float32))
    w2 = np.ascontiguousarray(np.asarray(L0_ffn_W2, np.float32))
    C1, C2 = _rope_tables()
    consts = _consts_pack(np.asarray(L0_ffn_b1, np.float32), np.asarray(L0_ffn_b2, np.float32),
                          bq_p, bk_p, np.asarray(L0_attn_bp, np.float32))
    bvr = np.ascontiguousarray(np.asarray(L0_attn_bv, np.float32).reshape(1, C))

    if "nc" not in _NC_CACHE:
        _NC_CACHE["nc"] = build_nc()
    nc = _NC_CACHE["nc"]

    in_maps = []
    for b in range(8):
        in_maps.append({
            "xfm": np.ascontiguousarray(x[b].T),
            "wq": wq_p, "wk": wk_p, "wv": wv, "wp": wp,
            "w1": w1, "w2": w2,
            "cr1": C1, "cr2": C2, "consts": consts, "bvr": bvr,
        })
    res = run_bass_kernel_spmd(nc, in_maps, core_ids=list(range(8)))
    out = np.stack([np.asarray(r["out_fm"]).T for r in res.results], axis=0)
    return out.astype(np.float32)


# version 05
# speedup vs baseline: 1.4761x; 1.0957x over previous
import os
import numpy as np
from contextlib import ExitStack

import concourse.bass as bass
import concourse.mybir as mybir
from concourse.bass_utils import run_bass_kernel_spmd

F32 = mybir.dt.float32
BF16 = mybir.dt.bfloat16
AF = mybir.ActivationFunctionType
ALU = mybir.AluOpType
AX = mybir.AxisListType

T = 2048
C = 1024
H = 16
DH = 64
FF = 4096
NCH = 8
NTB = 16
NEG = -1.0e10
EPS_LN = 1e-5
EPS_Q = 1e-5

XH = 0
QQ = 16384
KK = 32768
VV = 49152
WT = 65792
SC = 73984
WB = 78080
TPc = 80128
SP = 82176
SP_H = SP
SP_PREP = SP + 512
SP_ONE = SP + 1536
SP_EX = SP + 1552
NBF = SP + 1552 + 1024

def w2base(fc):
    if fc < 16:
        return VV + 1024 * fc
    if fc < 24:
        return WT + 1024 * (fc - 16)
    if fc < 28:
        return SC + 1024 * (fc - 24)
    return WB + 1024 * (fc - 28)

G0 = 0
WS = 4096
SELo = 6144
MTRIo = 7168
B1o = 7296
B2o = 7328
BQo = 7336
BKo = 7344
BPo = 7352
GACCo = 7360
ONECOLo = 7424
GT = 7425
BVo = 7456
ONERo = 8480
NFA = 8992

CONST_COLS = 1232


_SCHED = None


def _semname(sm):
    return getattr(sm, "name", None) if sm is not None else None


def build_nc():
    nc = bass.Bass("TRN2", target_bir_lowering=False, detect_race_conditions=False)
    xfm = nc.dram_tensor("xfm", [C, T], F32, kind="ExternalInput").ap()
    wq = nc.dram_tensor("wq", [C, C], F32, kind="ExternalInput").ap()
    wk = nc.dram_tensor("wk", [C, C], F32, kind="ExternalInput").ap()
    wv = nc.dram_tensor("wv", [C, C], F32, kind="ExternalInput").ap()
    wp = nc.dram_tensor("wp", [C, C], F32, kind="ExternalInput").ap()
    w1 = nc.dram_tensor("w1", [FF, C], F32, kind="ExternalInput").ap()
    w2 = nc.dram_tensor("w2", [C, FF], F32, kind="ExternalInput").ap()
    cr1 = nc.dram_tensor("cr1", [128, T], F32, kind="ExternalInput").ap()
    cr2 = nc.dram_tensor("cr2", [128, T], F32, kind="ExternalInput").ap()
    consts = nc.dram_tensor("consts", [128, CONST_COLS], F32, kind="ExternalInput").ap()
    bvr = nc.dram_tensor("bvr", [1, C], F32, kind="ExternalInput").ap()
    out_fm = nc.dram_tensor("out_fm", [C, T], F32, kind="ExternalOutput").ap()
    x1d = nc.dram_tensor("x1d", [C, T], F32, kind="Internal").ap()
    sums_d = nc.dram_tensor("sums_d", [H, T], F32, kind="Internal").ap()

    with ExitStack() as st:
        BFt = st.enter_context(nc.sbuf_tensor("BF", [128, NBF], BF16))
        FAt = st.enter_context(nc.sbuf_tensor("FA", [128, NFA], F32))
        PSt = st.enter_context(nc.psum_tensor("PS", [128, 4096], F32))
        sD = st.enter_context(nc.semaphore("sD"))
        sT = st.enter_context(nc.semaphore("sT"))
        sV = st.enter_context(nc.semaphore("sV"))
        sS = st.enter_context(nc.semaphore("sS"))
        sG = st.enter_context(nc.semaphore("sG"))
        sA = st.enter_context(nc.semaphore("sA"))
        block = st.enter_context(nc.Block())

        BF = BFt.ap()
        FA = FAt.ap()
        PS = PSt.ap()

        ops = {"sync": [], "tensor": [], "vector": [], "scalar": [], "gpsimd": []}
        cnt = {"D": 0, "T": 0, "V": 0, "S": 0, "G": 0, "A": 0}
        sem = {"D": sD, "T": sT, "V": sV, "S": sS, "G": sG, "A": sA}
        seen = {e: {} for e in ops}
        ENG_OF = {"D": "sync", "T": "tensor", "V": "vector", "S": "scalar",
                  "G": "gpsimd", "A": "scalar"}

        def emit(ch, fn, waits=(), inc=False):
            eng = ENG_OF[ch]
            w2 = []
            for (sname, v) in waits:
                if v <= 0:
                    continue
                if ENG_OF.get(sname, sname) == eng and sname == ch:
                    continue
                if v > seen[eng].get(sname, 0):
                    seen[eng][sname] = v
                    w2.append((sname, v))
            amt = 0
            if inc:
                amt = 16 if ch in ("D", "A") else 1
                cnt[ch] += amt
            ops[eng].append((fn, w2, sem[ch] if inc else None, amt))
            return cnt[ch]

        def dma(out_ap, in_ap, waits=(), transpose=False, ch="D"):
            return emit(
                ch,
                lambda e, o=out_ap, i=in_ap, tr=transpose: e.dma_start(out=o, in_=i, transpose=tr),
                waits, inc=True)

        def bf3(off, span, b, r):
            return BF[:, off:off + span].rearrange("p (b r) -> p b r", b=b, r=r)

        KPH = int(os.environ.get("KPH", "9"))

        def finish():
            final_d = cnt["D"]
            ops["sync"].append((lambda e: None, [("D", final_d)], None, 0))
            import kernel as _km
            _km._SCHED = {"ops": {e: [(w, _semname(sm), amt)
                                      for (fn, w, sm, amt) in ops[e]] for e in ops}}

            def emit_block(eng_name, e):
                for (fn, waits, s_, amt) in ops[eng_name]:
                    for (sname, v) in waits:
                        e.wait_ge(sem[sname], v)
                    inst = fn(e)
                    if s_ is not None:
                        inst.then_inc(s_, amt)

            @block.sync
            def _(e):
                emit_block("sync", e)

            @block.tensor
            def _(e):
                emit_block("tensor", e)

            @block.vector
            def _(e):
                emit_block("vector", e)

            @block.scalar
            def _(e):
                emit_block("scalar", e)

            @block.gpsimd
            def _(e):
                emit_block("gpsimd", e)

        d_const = dma(FA[:, SELo:SELo + CONST_COLS], consts)
        d_bv = dma(FA[0:1, BVo:BVo + C], bvr)
        v_ones = emit("V", lambda e: e.memset(FA[0:1, ONERo:ONERo + 512], 1.0), inc=True)
        v_ones = emit("V", lambda e: e.memset(FA[:, ONECOLo:ONECOLo + 1], 1.0), inc=True)
        v_ones = emit("V", lambda e: e.memset(BF[:, SP_ONE:SP_ONE + 16], 1.0), inc=True)
        vone_ap = BF[:, VV:VV + 16640].rearrange("p (t h d) -> p t h d", t=16, h=16, d=65)[:, :, :, 64:65]
        v_ones = emit("V", lambda e: e.memset(vone_ap, 1.0), inc=True)

        ONES_BF = BF[:, SP_ONE:SP_ONE + 1]
        ONER128 = FA[0:1, ONERo:ONERo + 128]

        def ln_phase(src_dram, first):
            t_before = cnt["T"]
            d_x = [0] * NCH
            v_st = [0] * NCH
            for c in range(NCH):
                w = []
                if c >= 2:
                    w.append(("V", v_st[c - 2]))
                d_x[c] = dma(FA[:, G0 + 2048 * (c % 2):G0 + 2048 * (c % 2) + 2048],
                             src_dram[:, :].rearrange("(a p) m -> a p m", a=NCH, p=128)[c],
                             waits=w)
                for g in range(4):
                    xsrc = FA[:, G0 + 2048 * (c % 2) + 512 * g:G0 + 2048 * (c % 2) + 512 * (g + 1)]
                    exb = BF[:, SP_EX:SP_EX + 512]
                    exq = BF[:, SP_EX + 512:SP_EX + 1024]
                    wv_ = [("D", d_x[c])]
                    if c > 0 or g > 0:
                        wv_.append(("T", cnt["T"]))
                    emit("V", lambda e, o=exb, i=xsrc: e.tensor_copy(o, i), wv_)
                    v_c = emit("V", lambda e, o=exq, i=exb: e.tensor_mul(o, i, i), inc=True)
                    emit("T", lambda e, o=PS[0:1, 512 * g:512 * g + 512], l=ONES_BF, r=exb,
                         s=(c == 0), p=(c == NCH - 1):
                         e.matmul(o, l, r, start=s, stop=p), [("V", v_c)])
                    t_c = emit("T", lambda e, o=PS[0:1, 2048 + 512 * g:2048 + 512 * g + 512],
                               l=ONES_BF, r=exq, s=(c == 0), p=(c == NCH - 1):
                               e.matmul(o, l, r, start=s, stop=p), [("V", v_c)], inc=True)
                v_st[c] = cnt["V"]
            t_stats = cnt["T"]
            slotA = FA[0:1, 0:2048]
            slotB = FA[0:1, 2048:4096]
            emit("V", lambda e: e.tensor_scalar(slotA, PS[0:1, 0:2048], 1.0 / C, None, ALU.mult),
                 [("T", t_stats)])
            emit("V", lambda e: e.tensor_mul(slotB, slotA, slotA))
            v_var = emit("V", lambda e: e.scalar_tensor_tensor(
                slotB, PS[0:1, 2048:4096], 1.0 / C, slotB, ALU.mult, ALU.subtract),
                [("T", t_stats)], inc=True)
            emit("V", lambda e: e.tensor_scalar(slotB, slotB, EPS_LN, None, ALU.add))
            s_sd = emit("S", lambda e: e.activation(slotB, slotB, AF.Sqrt),
                        [("V", v_var)], inc=True)
            emit("V", lambda e: e.reciprocal(slotB, slotB), [("S", s_sd)])
            v_negm = emit("V", lambda e: e.tensor_scalar(
                slotA, slotA, -1.0, None, ALU.mult), inc=True)
            t_bc = 0
            for g in range(4):
                emit("T", lambda e, o=PS[:, 512 * g:512 * g + 512],
                     r=FA[0:1, 2048 + 512 * g:2048 + 512 * (g + 1)]:
                     e.matmul(o, ONER128, r, start=True, stop=True), [("V", v_negm)])
                t_bc = emit("T", lambda e, o=PS[:, 2048 + 512 * g:2048 + 512 * g + 512],
                            r=FA[0:1, 512 * g:512 * (g + 1)]:
                            e.matmul(o, ONER128, r, start=True, stop=True),
                            [("V", v_negm)], inc=(g == 3))
            v_ap = [0] * NCH
            for c in range(NCH):
                w = [("T", t_bc)]
                if c >= 2:
                    w.append(("V", v_ap[c - 2]))
                d_a = dma(FA[:, G0 + 2048 * (c % 2):G0 + 2048 * (c % 2) + 2048],
                          src_dram[:, :].rearrange("(a p) m -> a p m", a=NCH, p=128)[c],
                          waits=w)
                xs = FA[:, G0 + 2048 * (c % 2):G0 + 2048 * (c % 2) + 2048]
                emit("V", lambda e, x=xs: e.tensor_add(x, x, PS[:, 2048:4096]),
                     [("D", d_a), ("T", t_bc)])
                v_ap[c] = emit("V", lambda e, x=xs, o=BF[:, XH + 2048 * c:XH + 2048 * (c + 1)]:
                               e.tensor_mul(o, x, PS[:, 0:2048]), inc=True)
            return v_ap

        v_ap1 = ln_phase(xfm, True)
        if KPH < 2:
            finish()
            return nc

        d_c1 = dma(FA[:, 0:2048], cr1, waits=[("V", v_ap1[6])])
        d_c2 = dma(FA[:, 2048:4096], cr2, waits=[("V", v_ap1[7])])

        prev_cast = [0]

        def prep_weight(wdram, wait_wt_free_t):
            v_cast = [0] * 8
            d_tr = [0] * 8
            for ot in range(8):
                w = []
                if ot >= 2:
                    w.append(("V", v_cast[ot - 2]))
                else:
                    w.append(("V", prev_cast[0]))
                d_w = dma(FA[:, WS + 1024 * (ot % 2):WS + 1024 * (ot % 2) + 1024],
                          wdram[128 * ot:128 * (ot + 1), :], waits=w, ch="A")
                wc = [("A", d_w)]
                if ot >= 2:
                    wc.append(("D", d_tr[ot - 2]))
                else:
                    wc.append(("D", prev_cast[1] if len(prev_cast) > 1 else 0))
                v_cast[ot] = emit("V", lambda e, o=BF[:, WB + 1024 * (ot % 2):WB + 1024 * (ot % 2) + 1024],
                                  i=FA[:, WS + 1024 * (ot % 2):WS + 1024 * (ot % 2) + 1024]:
                                  e.tensor_copy(o, i), wc, inc=True)
                w = [("V", v_cast[ot])]
                if ot == 0 and wait_wt_free_t:
                    w.append(("T", wait_wt_free_t))
                oap = bf3(WT, 8192, 8, 1024)[:, :, 128 * ot:128 * ot + 128]
                d_tr[ot] = dma(oap, BF[:, WB + 1024 * (ot % 2):WB + 1024 * (ot % 2) + 1024],
                               waits=w, transpose=True)
            prev_cast[0] = v_cast[7]
            if len(prev_cast) > 1:
                prev_cast[1] = d_tr[7]
            else:
                prev_cast.append(d_tr[7])
            return d_tr[7]

        def fm_proj(dst_off, bias_off, d_wt, extra_first_waits):
            v_rot = [0] * NCH
            s_qb = [0] * NCH
            t_oc = [0] * NCH
            for oc in range(NCH):
                pcol = (oc % 2) * 2048
                w = [("D", d_wt)]
                if oc == 0:
                    w += extra_first_waits
                if oc >= 2:
                    w.append(("S", s_qb[oc - 2]))
                for cc in range(NCH):
                    for g in range(4):
                        last = (cc == NCH - 1 and g == 3)
                        t_oc[oc] = emit("T", lambda e,
                                        o=PS[:, pcol + 512 * g:pcol + 512 * (g + 1)],
                                        l=BF[:, WT + 1024 * cc + 128 * oc:WT + 1024 * cc + 128 * oc + 128],
                                        r=BF[:, XH + 2048 * cc + 512 * g:XH + 2048 * cc + 512 * (g + 1)],
                                        s=(cc == 0), p=(cc == NCH - 1):
                                        e.matmul(o, l, r, start=s, stop=p),
                                        w, inc=last)
                    w = []
                qb = BF[:, SC:SC + 2048]
                qs = BF[:, SC + 2048:SC + 4096]
                wS = [("T", t_oc[oc])]
                if oc >= 1:
                    wS.append(("V", v_rot[oc - 1]))
                s_qb[oc] = emit("S", lambda e, o=qb, i=PS[:, pcol:pcol + 2048],
                                b=FA[:, bias_off + oc:bias_off + oc + 1]:
                                e.activation(o, i, AF.Identity, bias=b), wS, inc=True)
                emit("V", lambda e, o=qs, i=qb:
                     e.stream_shuffle(o, i, mask=[(j + 16) % 32 for j in range(32)]),
                     [("S", s_qb[oc])])
                emit("V", lambda e, o=qb, i=qb: e.tensor_mul(o, i, FA[:, 0:2048]),
                     [("D", d_c1)])
                emit("V", lambda e, o=qs, i=qs: e.tensor_mul(o, i, FA[:, 2048:4096]),
                     [("D", d_c2)])
                v_rot[oc] = emit("V", lambda e, o=BF[:, dst_off + 2048 * oc:dst_off + 2048 * (oc + 1)],
                                 a=qb, b=qs: e.tensor_add(o, a, b), inc=True)
            return t_oc[NCH - 1], v_rot, s_qb

        d_wkT = prep_weight(wk, 0)
        t_kproj, v_rotk, s_qbk = fm_proj(KK, BKo, d_wkT, [("V", v_ap1[7])])
        d_wvT = prep_weight(wv, t_kproj)
        t_vp = [0] * NTB
        v_vev = [0] * NTB
        for tb in range(NTB):
            pcol = 2048 + 1024 * (tb % 2)
            w = [("D", d_wvT)]
            if tb < 2:
                w.append(("S", s_qbk[NCH - 1]))
            if tb >= 2:
                w.append(("V", v_vev[tb - 2]))
            for cc in range(NCH):
                for g in range(2):
                    emit("T", lambda e, o=PS[:, pcol + 512 * g:pcol + 512 * (g + 1)],
                         l=BF[:, XH + 2048 * cc + 128 * tb:XH + 2048 * cc + 128 * tb + 128],
                         r=BF[:, WT + 1024 * cc + 512 * g:WT + 1024 * cc + 512 * (g + 1)],
                         s=(cc == 0): e.matmul(o, l, r, start=s, stop=False), w)
                w = []
            for g in range(2):
                t_vp[tb] = emit("T", lambda e, o=PS[:, pcol + 512 * g:pcol + 512 * (g + 1)],
                                l=ONER128, r=FA[0:1, BVo + 512 * g:BVo + 512 * (g + 1)]:
                                e.matmul(o, l, r, start=False, stop=True),
                                [("D", d_bv)], inc=(g == 1))
            oap = BF[:, VV + 1040 * tb:VV + 1040 * tb + 1040].rearrange(
                "p (h d) -> p h d", h=16, d=65)[:, :, 0:64]
            iap = PS[:, pcol:pcol + 1024].rearrange("p (h d) -> p h d", h=16, d=64)
            v_vev[tb] = emit("V", lambda e, o=oap, i=iap: e.tensor_copy(o, i),
                             [("T", t_vp[tb])], inc=True)
        d_wqT = prep_weight(wq, t_vp[NTB - 1])
        t_qproj, v_rotq, s_qbq = fm_proj(
            QQ, BQo, d_wqT, [("V", v_vev[NTB - 1]), ("S", s_qbk[NCH - 1])])

        if KPH < 3:
            finish()
            return nc
        d_wpT = prep_weight(wp, t_qproj)

        gstate = {"gi": 0, "v_gprev": [prev_cast[0], prev_cast[0]], "v_gred": 0}

        def gamma_unit():
            gi = gstate["gi"]
            if gi >= 64:
                return
            w = [("V", gstate["v_gprev"][gi % 2])]
            if gi < 64 - 32:
                iap = w1[128 * gi:128 * (gi + 1), :]
            else:
                tt = gi - 32
                iap = w2[:, :].rearrange("(a p) (b m) -> a b p m", p=128, m=1024)[tt // 4, tt % 4]
            d_g = dma(FA[:, WS + 1024 * (gi % 2):WS + 1024 * (gi % 2) + 1024], iap, waits=w)
            v = emit("V", lambda e, o=FA[:, GACCo + gi:GACCo + gi + 1],
                     i=FA[:, WS + 1024 * (gi % 2):WS + 1024 * (gi % 2) + 1024]:
                     e.tensor_reduce(o, i, AX.X, ALU.add, apply_absolute_value=True),
                     [("D", d_g)], inc=True)
            gstate["v_gprev"][gi % 2] = v
            gstate["v_gred"] = v
            gstate["gi"] = gi + 1

        pc = 0
        s_exp = {}
        t_pv_last = 0
        v_yev = [0] * H
        v_sum = [0] * H
        d_sumst = [0] * H
        pending_pv = [None]

        def flush_pv():
            nonlocal_t = pending_pv[0]
            if nonlocal_t is not None:
                nonlocal_t()
                pending_pv[0] = None

        for h in range(H):
            ch = h // 2
            pb = 64 * (h % 2)
            for _ in range(4):
                gamma_unit()
            for i in range(NTB):
                q0 = 128 * i
                Nq = T - q0
                pieces = [(q0, min(1024, Nq))]
                if Nq > 1024:
                    pieces.append((q0 + 1024, Nq - 1024))
                for pi, (qoff, plen) in enumerate(pieces):
                    par = pc % 2
                    stc = 1024 * par
                    wT_ = []
                    if pc >= 2:
                        wT_.append(("S", s_exp[pc - 2]))
                    if h == 0 and i == 0 and pi == 0:
                        wT_.append(("V", v_rotq[NCH - 1]))
                        wT_.append(("V", v_rotk[NCH - 1]))
                    t_kq = 0
                    ng = (plen + 511) // 512
                    for g in range(ng):
                        gw = min(512, plen - 512 * g)
                        t_kq = emit("T", lambda e, o=PS[:, stc + 512 * g:stc + 512 * g + gw],
                                    l=BF[pb:pb + 64, KK + 2048 * ch + q0:KK + 2048 * ch + q0 + 128],
                                    r=BF[pb:pb + 64, QQ + 2048 * ch + qoff + 512 * g:
                                         QQ + 2048 * ch + qoff + 512 * g + gw]:
                                    e.matmul(o, l, r, start=True, stop=True),
                                    wT_, inc=(g == ng - 1))
                        wT_ = []
                    wS_ = [("T", t_kq)]
                    if pi == 0:
                        v_m = emit("V", lambda e, o=PS[:, stc:stc + 128]:
                                   e.tensor_add(o, o, FA[:, MTRIo:MTRIo + 128]),
                                   [("T", t_kq), ("D", d_const)], inc=True)
                        wS_ = [("V", v_m)]
                    s_exp[pc] = emit("S", lambda e, o=BF[:, TPc + 1024 * par:TPc + 1024 * par + plen],
                                     i=PS[:, stc:stc + plen]:
                                     e.activation(o, i, AF.Exp, scale=0.125), wS_, inc=True)

                    def make_pv(pc_=pc, par_=par, qoff_=qoff, plen_=plen, i_=i, h_=h, pi_=pi):
                        def do():
                            nonlocal t_pv_last
                            wPV = [("S", s_exp[pc_])]
                            if i_ == 0 and pi_ == 0 and h_ > 0:
                                wPV.append(("V", v_sum[h_ - 1]))
                            off = 0
                            while off < plen_:
                                gw = min(512 - ((qoff_ + off) % 512), plen_ - off)
                                if pi_ == 0 and off == 0:
                                    gw = min(gw, 128)
                                stp = (pi_ == 0 and off == 0 and (i_ % 4) == 3)
                                st_ = (i_ == 0 and ((qoff_ + off) % 512) == 0)
                                t_pv_last = emit("T", lambda e,
                                                 o=PS[0:65, 2048 + qoff_ + off:2048 + qoff_ + off + gw],
                                                 l=BF[:, VV + 1040 * i_ + 65 * h_:VV + 1040 * i_ + 65 * h_ + 65],
                                                 r=BF[:, TPc + 1024 * par_ + off:TPc + 1024 * par_ + off + gw],
                                                 s=st_, p=stp:
                                                 e.matmul(o, l, r, start=s, stop=p),
                                                 wPV, inc=(off + gw >= plen_))
                                wPV = []
                                off += gw
                        return do

                    flush_pv()
                    pending_pv[0] = make_pv()
                    pc += 1
            flush_pv()
            wY = [("T", t_pv_last)]
            if h == 0:
                wY.append(("T", t_qproj))
            v_yev[h] = emit("V", lambda e, o=BF[pb:pb + 64, XH + 2048 * ch:XH + 2048 * (ch + 1)],
                            i=PS[0:64, 2048:4096]: e.tensor_copy(o, i), wY, inc=True)
            wSm = []
            if h >= 1:
                wSm.append(("D", d_sumst[h - 1]))
            v_sum[h] = emit("V", lambda e, o=FA[0:1, 0:2048], i=PS[64:65, 2048:4096]:
                            e.tensor_copy(o, i), wSm, inc=True)
            d_sumst[h] = dma(sums_d[h:h + 1, :], FA[0:1, 0:2048], waits=[("V", v_sum[h])])

        d_sumld = dma(FA[0:16, 0:2048], sums_d, waits=[("D", d_sumst[H - 1])])
        v_inv = emit("V", lambda e: e.reciprocal(FA[0:16, 0:2048], FA[0:16, 0:2048]),
                     [("D", d_sumld)], inc=True)
        t_selbc = [0] * NCH
        v_ynorm = [0] * NCH
        for c2 in range(NCH):
            pcol = (c2 % 2) * 2048
            w = [("V", v_inv), ("D", d_const)]
            if c2 >= 2:
                w.append(("V", v_ynorm[c2 - 2]))
            for g in range(4):
                t_selbc[c2] = emit("T", lambda e, o=PS[:, pcol + 512 * g:pcol + 512 * (g + 1)],
                                   l=FA[0:16, SELo + 128 * c2:SELo + 128 * (c2 + 1)],
                                   r=FA[0:16, 512 * g:512 * (g + 1)]:
                                   e.matmul(o, l, r, start=True, stop=True), w, inc=(g == 3))
                w = []
            v_ynorm[c2] = emit("V", lambda e, o=BF[:, XH + 2048 * c2:XH + 2048 * (c2 + 1)],
                               i=PS[:, pcol:pcol + 2048]:
                               e.tensor_mul(o, o, i),
                               [("T", t_selbc[c2]), ("V", v_yev[2 * c2 + 1])], inc=True)

        if KPH < 4:
            finish()
            return nc
        while gstate["gi"] < 64:
            gamma_unit()
        emit("V", lambda e: e.tensor_reduce(FA[:, GT:GT + 1], FA[:, GACCo:GACCo + 32], AX.X, ALU.add),
             [("V", gstate["v_gred"])])
        v_g2 = emit("V", lambda e: e.tensor_reduce(FA[:, GT + 1:GT + 2], FA[:, GACCo + 32:GACCo + 64],
                                                   AX.X, ALU.add), inc=True)
        emit("T", lambda e: e.matmul(PS[0:1, 0:1], FA[:, ONECOLo:ONECOLo + 1], FA[:, GT:GT + 1],
                                     start=True, stop=True),
             [("V", v_g2), ("V", v_ynorm[NCH - 1])])
        t_gm = emit("T", lambda e: e.matmul(PS[0:1, 1:2], FA[:, ONECOLo:ONECOLo + 1],
                                            FA[:, GT + 1:GT + 2], start=True, stop=True),
                    [("V", v_g2)], inc=True)
        emit("V", lambda e: e.tensor_scalar(FA[0:1, GT + 4:GT + 5], PS[0:1, 0:1],
                                            1.0 / (FF * C), EPS_Q, ALU.mult, ALU.add),
             [("T", t_gm)])
        emit("V", lambda e: e.tensor_scalar(FA[0:1, GT + 5:GT + 6], PS[0:1, 1:2],
                                            1.0 / (FF * C), EPS_Q, ALU.mult, ALU.add))
        emit("V", lambda e: e.reciprocal(FA[0:1, GT + 4:GT + 5], FA[0:1, GT + 4:GT + 5]))
        v_gi = emit("V", lambda e: e.reciprocal(FA[0:1, GT + 5:GT + 6], FA[0:1, GT + 5:GT + 6]),
                    inc=True)
        emit("T", lambda e: e.matmul(PS[:, 4:5], ONER128, FA[0:1, GT + 4:GT + 5],
                                     start=True, stop=True), [("V", v_gi)])
        t_gb = emit("T", lambda e: e.matmul(PS[:, 5:6], ONER128, FA[0:1, GT + 5:GT + 6],
                                            start=True, stop=True), inc=True)
        emit("V", lambda e: e.tensor_copy(FA[:, GT + 2:GT + 3], PS[:, 4:5]), [("T", t_gb)])
        v_sg = emit("V", lambda e: e.tensor_copy(FA[:, GT + 3:GT + 4], PS[:, 5:6]), inc=True)
        SG1 = FA[:, GT + 2:GT + 3]
        SG2 = FA[:, GT + 3:GT + 4]

        t_pp = [0] * NCH
        v_res = [0] * NCH
        d_x1w = [0] * NCH
        for oc in range(NCH):
            pcol = (oc % 2) * 2048
            w = [("D", d_wpT), ("V", v_sg), ("V", v_ynorm[NCH - 1])]
            if oc >= 2:
                w.append(("V", v_res[oc - 2]))
            for cc in range(NCH):
                for g in range(4):
                    t_pp[oc] = emit("T", lambda e, o=PS[:, pcol + 512 * g:pcol + 512 * (g + 1)],
                                    l=BF[:, WT + 1024 * cc + 128 * oc:WT + 1024 * cc + 128 * oc + 128],
                                    r=BF[:, XH + 2048 * cc + 512 * g:XH + 2048 * cc + 512 * (g + 1)],
                                    s=(cc == 0), p=(cc == NCH - 1): e.matmul(o, l, r, start=s, stop=p),
                                    w, inc=(cc == NCH - 1 and g == 3))
                    w = []
            wD = [("T", t_selbc[NCH - 1]), ("V", v_ynorm[NCH - 1])]
            if oc >= 2:
                wD.append(("V", v_res[oc - 2]))
            d_xr = dma(FA[:, G0 + 2048 * (oc % 2):G0 + 2048 * (oc % 2) + 2048],
                       xfm[:, :].rearrange("(a p) m -> a p m", a=NCH, p=128)[oc], waits=wD)
            xs = FA[:, G0 + 2048 * (oc % 2):G0 + 2048 * (oc % 2) + 2048]
            v_res[oc] = emit("V", lambda e, x=xs, i=PS[:, pcol:pcol + 2048],
                             b=FA[:, BPo + oc:BPo + oc + 1]:
                             e.scalar_tensor_tensor(x, i, b, x, ALU.add, ALU.add),
                             [("T", t_pp[oc]), ("D", d_xr)], inc=True)
            d_x1w[oc] = dma(x1d[:, :].rearrange("(a p) m -> a p m", a=NCH, p=128)[oc], xs,
                            waits=[("V", v_res[oc])])

        v_q2 = {}
        d_w2tr = 0
        d_w2rdy = [0] * 32
        sl_order = [0, 1, 2, 3, 6, 7, 4, 5]
        qi = 0
        for sl in sl_order:
            for dt in range(8):
                w = [("V", v_sg)]
                if qi >= 2:
                    w.append(("V", v_q2[qi - 2]))
                d_g = dma(FA[:, WS + 512 * (qi % 2):WS + 512 * (qi % 2) + 512],
                          w2[128 * dt:128 * (dt + 1), 512 * sl:512 * (sl + 1)], waits=w, ch="A")
                src = FA[:, WS + 512 * (qi % 2):WS + 512 * (qi % 2) + 512]
                t1 = BF[:, SP_PREP:SP_PREP + 512]
                t2 = BF[:, SP_PREP + 512:SP_PREP + 1024]
                emit("V", lambda e, o=t1, i=src: e.tensor_scalar(o, i, SG2, 0.5, ALU.mult, ALU.is_ge),
                     [("A", d_g), ("V", v_sg), ("D", d_w2tr)])
                emit("V", lambda e, o=t2, i=src: e.tensor_scalar(o, i, SG2, -0.5, ALU.mult, ALU.is_le))
                v_q2[qi] = emit("V", lambda e, o=t1, a=t1, b=t2: e.tensor_sub(o, a, b), inc=True)
                wtr = [("V", v_q2[qi]), ("T", t_pv_last)]
                if sl in (4, 5):
                    wtr.append(("T", t_pp[NCH - 1]))
                oap = BF[:, w2base(4 * sl):w2base(4 * sl) + 4096].rearrange(
                    "p (b r) -> p b r", b=4, r=1024)[:, :, 128 * dt:128 * dt + 128]
                d_w2tr = dma(oap, t1, waits=wtr, transpose=True)
                if dt == 7:
                    for b_ in range(4):
                        d_w2rdy[4 * sl + b_] = d_w2tr
                qi += 1

        if KPH < 5:
            finish()
            return nc
        ln2_gate_t = t_pp[NCH - 1]
        ln2_gate_v = v_res[NCH - 1]
        v_ap2 = None

        def ln_phase2():
            d_x = [0] * NCH
            v_st = [0] * NCH
            for c in range(NCH):
                w = [("D", d_x1w[c])]
                if c >= 2:
                    w.append(("V", v_st[c - 2]))
                d_x[c] = dma(FA[:, G0 + 2048 * (c % 2):G0 + 2048 * (c % 2) + 2048],
                             x1d[:, :].rearrange("(a p) m -> a p m", a=NCH, p=128)[c], waits=w)
                for g in range(4):
                    xsrc = FA[:, G0 + 2048 * (c % 2) + 512 * g:G0 + 2048 * (c % 2) + 512 * (g + 1)]
                    exb = BF[:, SP_EX:SP_EX + 512]
                    exq = BF[:, SP_EX + 512:SP_EX + 1024]
                    wv_ = [("D", d_x[c]), ("T", ln2_gate_t)]
                    if c > 0 or g > 0:
                        wv_.append(("T", cnt["T"]))
                    emit("V", lambda e, o=exb, i=xsrc: e.tensor_copy(o, i), wv_)
                    v_c = emit("V", lambda e, o=exq, i=exb: e.tensor_mul(o, i, i), inc=True)
                    emit("T", lambda e, o=PS[0:1, 512 * g:512 * g + 512], l=ONES_BF, r=exb,
                         s=(c == 0), p=(c == NCH - 1):
                         e.matmul(o, l, r, start=s, stop=p), [("V", v_c), ("V", ln2_gate_v)])
                    emit("T", lambda e, o=PS[0:1, 2048 + 512 * g:2048 + 512 * g + 512],
                         l=ONES_BF, r=exq, s=(c == 0), p=(c == NCH - 1):
                         e.matmul(o, l, r, start=s, stop=p), [("V", v_c)], inc=True)
                v_st[c] = cnt["V"]
            t_stats = cnt["T"]
            slotA = FA[0:1, 0:2048]
            slotB = FA[0:1, 2048:4096]
            emit("V", lambda e: e.tensor_scalar(slotA, PS[0:1, 0:2048], 1.0 / C, None, ALU.mult),
                 [("T", t_stats)])
            emit("V", lambda e: e.tensor_mul(slotB, slotA, slotA))
            v_var = emit("V", lambda e: e.scalar_tensor_tensor(
                slotB, PS[0:1, 2048:4096], 1.0 / C, slotB, ALU.mult, ALU.subtract),
                [("T", t_stats)], inc=True)
            emit("V", lambda e: e.tensor_scalar(slotB, slotB, EPS_LN, None, ALU.add))
            s_sd = emit("S", lambda e: e.activation(slotB, slotB, AF.Sqrt),
                        [("V", v_var)], inc=True)
            emit("V", lambda e: e.reciprocal(slotB, slotB), [("S", s_sd)])
            v_negm = emit("V", lambda e: e.tensor_scalar(
                slotA, slotA, -1.0, None, ALU.mult), inc=True)
            t_bc = 0
            for g in range(4):
                emit("T", lambda e, o=PS[:, 512 * g:512 * g + 512],
                     r=FA[0:1, 2048 + 512 * g:2048 + 512 * (g + 1)]:
                     e.matmul(o, ONER128, r, start=True, stop=True), [("V", v_negm)])
                t_bc = emit("T", lambda e, o=PS[:, 2048 + 512 * g:2048 + 512 * g + 512],
                            r=FA[0:1, 512 * g:512 * (g + 1)]:
                            e.matmul(o, ONER128, r, start=True, stop=True),
                            [("V", v_negm)], inc=(g == 3))
            v_ap = [0] * NCH
            for c in range(NCH):
                w = [("T", t_bc)]
                if c >= 2:
                    w.append(("V", v_ap[c - 2]))
                d_a = dma(FA[:, G0 + 2048 * (c % 2):G0 + 2048 * (c % 2) + 2048],
                          x1d[:, :].rearrange("(a p) m -> a p m", a=NCH, p=128)[c], waits=w)
                xs = FA[:, G0 + 2048 * (c % 2):G0 + 2048 * (c % 2) + 2048]
                emit("V", lambda e, x=xs: e.tensor_add(x, x, PS[:, 2048:4096]),
                     [("D", d_a), ("T", t_bc)])
                v_ap[c] = emit("V", lambda e, x=xs, o=BF[:, XH + 2048 * c:XH + 2048 * (c + 1)]:
                               e.tensor_mul(o, x, PS[:, 0:2048]), inc=True)
            return v_ap

        v_ap2 = ln_phase2()

        v_q1 = {}
        d_w1tr = 0
        d_w1rdy = [0] * 32
        for fc in range(32):
            w = [("V", v_sg)]
            if fc >= 2:
                w.append(("V", v_q1[fc - 2]))
            else:
                w.append(("V", v_q2[qi - 1]))
            d_g = dma(FA[:, WS + 1024 * (fc % 2):WS + 1024 * (fc % 2) + 1024],
                      w1[128 * fc:128 * (fc + 1), :], waits=w, ch="A")
            src = FA[:, WS + 1024 * (fc % 2):WS + 1024 * (fc % 2) + 1024]
            t1 = BF[:, SP_PREP:SP_PREP + 1024]
            t2 = BF[:, SP_EX:SP_EX + 1024]
            wq_ = [("A", d_g), ("V", v_sg), ("D", d_w2tr), ("D", d_w1tr)]
            emit("V", lambda e, o=t1, i=src: e.tensor_scalar(o, i, SG1, 0.5, ALU.mult, ALU.is_ge), wq_)
            emit("V", lambda e, o=t2, i=src: e.tensor_scalar(o, i, SG1, -0.5, ALU.mult, ALU.is_le))
            v_q1[fc] = emit("V", lambda e, o=t1, a=t1, b=t2: e.tensor_sub(o, a, b), inc=True)
            oap = BF[:, QQ:QQ + 32768].rearrange("p (b r) -> p b r", b=8, r=4096)[
                :, :, 128 * fc:128 * fc + 128]
            wtr = [("V", v_q1[fc]), ("T", t_pv_last)]
            d_w1tr = dma(oap, t1, waits=wtr, transpose=True)
            d_w1rdy[fc] = d_w1tr

        if KPH < 6:
            finish()
            return nc
        t_mm1 = {}
        s_relu = {}
        t_mm2 = {}
        v_out = [0] * 8
        d_out = [0] * 8
        pending_mm2 = [None]

        def flush_mm2():
            if pending_mm2[0] is not None:
                pending_mm2[0]()
                pending_mm2[0] = None

        for tp in range(8):
            w = [("T", t_selbc[NCH - 1])]
            if tp >= 2:
                w.append(("V", v_out[tp - 2]))
            if tp < 2:
                w.append(("V", v_ap2[NCH - 1]))
                w.append(("D", d_x1w[NCH - 1]))
            d_x1p = dma(FA[:, G0 + 2048 * (tp % 2):G0 + 2048 * (tp % 2) + 2048].rearrange(
                "p (a m) -> p a m", a=8, m=256),
                x1d[:, :].rearrange("(a p) m -> p a m", a=NCH, p=128)[:, :, 256 * tp:256 * (tp + 1)],
                waits=w)
            for fc in range(32):
                hcol = 512 * (fc % 2)
                wmm1 = [("D", d_w1rdy[fc]), ("V", v_ap2[NCH - 1])]
                prev = (tp, fc - 2) if fc >= 2 else ((tp - 1, 30 + fc) if tp >= 1 else None)
                if prev in s_relu:
                    wmm1.append(("S", s_relu[prev]))
                for cc in range(NCH):
                    t_mm1[(tp, fc)] = emit("T", lambda e, o=PS[:, hcol:hcol + 256],
                                           l=BF[:, QQ + 4096 * cc + 128 * fc:QQ + 4096 * cc + 128 * fc + 128],
                                           r=BF[:, XH + 2048 * cc + 256 * tp:XH + 2048 * cc + 256 * (tp + 1)],
                                           s=(cc == 0), p=(cc == NCH - 1): e.matmul(o, l, r, start=s, stop=p),
                                           wmm1, inc=(cc == NCH - 1))
                    wmm1 = []
                s_relu[(tp, fc)] = emit("S", lambda e, o=BF[:, SP_H + 256 * (fc % 2):SP_H + 256 * (fc % 2) + 256],
                                        i=PS[:, hcol:hcol + 256], b=FA[:, B1o + fc:B1o + fc + 1]:
                                        e.activation(o, i, AF.Relu, bias=b),
                                        [("T", t_mm1[(tp, fc)])], inc=True)
                def make_mm2(tp_=tp, fc_=fc):
                    def do():
                        wmm2 = [("S", s_relu[(tp_, fc_)]), ("D", d_w2rdy[fc_])]
                        if fc_ == 0 and tp_ >= 1:
                            wmm2.append(("V", v_out[tp_ - 1]))
                        for dc in range(NCH):
                            st_ = (fc_ == 0 and dc % 2 == 0)
                            t_mm2[(tp_, fc_)] = emit(
                                "T", lambda e, o=PS[:, 2048 + 256 * dc:2048 + 256 * (dc + 1)],
                                l=BF[:, w2base(fc_) + 128 * dc:w2base(fc_) + 128 * dc + 128],
                                r=BF[:, SP_H + 256 * (fc_ % 2):SP_H + 256 * (fc_ % 2) + 256],
                                s=st_, p=(fc_ == 31 and dc % 2 == 1):
                                e.matmul(o, l, r, start=s, stop=p),
                                wmm2, inc=(dc == NCH - 1))
                            wmm2[:] = []
                    return do

                flush_mm2()
                pending_mm2[0] = make_mm2()
            flush_mm2()
            for dc in range(NCH):
                xs = FA[:, G0 + 2048 * (tp % 2) + 256 * dc:G0 + 2048 * (tp % 2) + 256 * (dc + 1)]
                v_out[tp] = emit("V", lambda e, x=xs, i=PS[:, 2048 + 256 * dc:2048 + 256 * (dc + 1)],
                                 b=FA[:, B2o + dc:B2o + dc + 1]:
                                 e.scalar_tensor_tensor(x, i, b, x, ALU.add, ALU.add),
                                 [("T", t_mm2[(tp, 31)]), ("D", d_x1p)], inc=(dc == NCH - 1))
            d_out[tp] = dma(out_fm[:, :].rearrange("(a p) m -> p a m", a=NCH, p=128)[
                :, :, 256 * tp:256 * (tp + 1)],
                FA[:, G0 + 2048 * (tp % 2):G0 + 2048 * (tp % 2) + 2048].rearrange(
                    "p (a m) -> p a m", a=8, m=256),
                waits=[("V", v_out[tp])])

        finish()
    return nc


def _perm():
    perm = np.zeros(C, dtype=np.int64)
    for g in range(C):
        chk, p = g // 128, g % 128
        h = 2 * chk + p // 64
        lp = p % 64
        quad = lp // 32
        r = lp % 32
        if r < 16:
            i_idx, par = r + 16 * quad, 0
        else:
            i_idx, par = (r - 16) + 16 * quad, 1
        perm[g] = 64 * h + 2 * i_idx + par
    return perm


def _rope_tables():
    i = np.arange(32, dtype=np.float32)
    theta = (np.float32(1.0) / (np.float32(10000.0) ** (np.float32(2.0) * i / np.float32(64)))).astype(np.float32)
    t = np.arange(T, dtype=np.float32)[:, None]
    ang = (t * theta[None, :]).astype(np.float32)
    cos = np.cos(ang.astype(np.float64)).astype(np.float32)
    sin = np.sin(ang.astype(np.float64)).astype(np.float32)
    C1 = np.zeros((128, T), np.float32)
    C2 = np.zeros((128, T), np.float32)
    for p in range(128):
        lp = p % 64
        quad = lp // 32
        r = lp % 32
        if r < 16:
            i_idx, is_even = r + 16 * quad, True
        else:
            i_idx, is_even = (r - 16) + 16 * quad, False
        C1[p] = cos[:, i_idx]
        C2[p] = (-sin[:, i_idx]) if is_even else sin[:, i_idx]
    return C1, C2


def _consts_pack(b1, b2, bq_p, bk_p, bp):
    A = np.zeros((128, CONST_COLS), np.float32)
    for c2 in range(NCH):
        for j in range(128):
            h = 2 * c2 + (1 if j >= 64 else 0)
            A[h, 128 * c2 + j] = 1.0
    for p in range(128):
        A[p, 1024:1024 + p] = NEG
    A[:, 1152:1184] = b1.reshape(32, 128).T
    A[:, 1184:1192] = b2.reshape(8, 128).T
    A[:, 1192:1200] = bq_p.reshape(8, 128).T
    A[:, 1200:1208] = bk_p.reshape(8, 128).T
    A[:, 1208:1216] = bp.reshape(8, 128).T
    return A


_NC_CACHE = {}


def kernel(L0_attn_x, L0_attn_wq, L0_attn_bq, L0_attn_wk, L0_attn_bk,
           L0_attn_wv, L0_attn_bv, L0_attn_wp, L0_attn_bp,
           L0_ffn_W1, L0_ffn_b1, L0_ffn_W2, L0_ffn_b2):
    x = np.asarray(L0_attn_x, dtype=np.float32)
    perm = _perm()
    wq_p = np.ascontiguousarray(np.asarray(L0_attn_wq, np.float32)[perm])
    wk_p = np.ascontiguousarray(np.asarray(L0_attn_wk, np.float32)[perm])
    bq_p = np.ascontiguousarray(np.asarray(L0_attn_bq, np.float32)[perm])
    bk_p = np.ascontiguousarray(np.asarray(L0_attn_bk, np.float32)[perm])
    wv = np.ascontiguousarray(np.asarray(L0_attn_wv, np.float32))
    wp = np.ascontiguousarray(np.asarray(L0_attn_wp, np.float32))
    w1 = np.ascontiguousarray(np.asarray(L0_ffn_W1, np.float32))
    w2 = np.ascontiguousarray(np.asarray(L0_ffn_W2, np.float32))
    C1, C2 = _rope_tables()
    consts = _consts_pack(np.asarray(L0_ffn_b1, np.float32), np.asarray(L0_ffn_b2, np.float32),
                          bq_p, bk_p, np.asarray(L0_attn_bp, np.float32))
    bvr = np.ascontiguousarray(np.asarray(L0_attn_bv, np.float32).reshape(1, C))

    if "nc" not in _NC_CACHE:
        _NC_CACHE["nc"] = build_nc()
    nc = _NC_CACHE["nc"]

    in_maps = []
    for b in range(8):
        in_maps.append({
            "xfm": np.ascontiguousarray(x[b].T),
            "wq": wq_p, "wk": wk_p, "wv": wv, "wp": wp,
            "w1": w1, "w2": w2,
            "cr1": C1, "cr2": C2, "consts": consts, "bvr": bvr,
        })
    res = run_bass_kernel_spmd(nc, in_maps, core_ids=list(range(8)))
    out = np.stack([np.asarray(r["out_fm"]).T for r in res.results], axis=0)
    return out.astype(np.float32)


# version 09
# speedup vs baseline: 1.6773x; 1.1363x over previous
import os
import numpy as np
import ml_dtypes
from contextlib import ExitStack

import concourse.bass as bass
import concourse.mybir as mybir
from concourse.bass_utils import run_bass_kernel_spmd

F32 = mybir.dt.float32
BF16 = mybir.dt.bfloat16
AF = mybir.ActivationFunctionType
ALU = mybir.AluOpType
AX = mybir.AxisListType

T = 2048
C = 1024
H = 16
DH = 64
FF = 4096
NCH = 8
NTB = 16
NEG = -1.0e10
EPS_LN = 1e-5
EPS_Q = 1e-5

XH = 0
QQ = 16384
KK = 32768
VV = 49152
WT = 65792
SC = 73984
WB = 78080
TPc = 80128
SP = 82176
SP_H = SP
SP_PREP = SP + 512
SP_ONE = SP + 1536
SP_EX = SP + 1552
SP_I = SP + 2576
SP_T2 = SP + 2832
NBF = SP + 2832 + 1024

def w2base(fc):
    if fc < 16:
        return VV + 1024 * fc
    if fc < 24:
        return WT + 1024 * (fc - 16)
    if fc < 28:
        return SC + 1024 * (fc - 24)
    return WB + 1024 * (fc - 28)

G0 = 0
WS = 4096
SELo = 6144
MTRIo = 7168
B1o = 7296
B2o = 7328
BQo = 7336
BKo = 7344
BPo = 7352
GACCo = 7360
ONECOLo = 7424
GT = 7425
BVo = 7456
ONERo = 8480
NFA = 8992

CONST_COLS = 1232


_SCHED = None


def _semname(sm):
    return getattr(sm, "name", None) if sm is not None else None


def build_nc():
    nc = bass.Bass("TRN2", target_bir_lowering=False, detect_race_conditions=False)
    xfm = nc.dram_tensor("xfm", [C, T], F32, kind="ExternalInput").ap()
    wq = nc.dram_tensor("wq", [C, C], F32, kind="ExternalInput").ap()
    wk = nc.dram_tensor("wk", [C, C], F32, kind="ExternalInput").ap()
    wv = nc.dram_tensor("wv", [C, C], F32, kind="ExternalInput").ap()
    wp = nc.dram_tensor("wp", [C, C], F32, kind="ExternalInput").ap()
    w1 = nc.dram_tensor("w1", [FF, C], F32, kind="ExternalInput").ap()
    w2 = nc.dram_tensor("w2", [C, FF], F32, kind="ExternalInput").ap()
    cr1 = nc.dram_tensor("cr1", [128, T], F32, kind="ExternalInput").ap()
    cr2 = nc.dram_tensor("cr2", [128, T], F32, kind="ExternalInput").ap()
    consts = nc.dram_tensor("consts", [128, CONST_COLS], F32, kind="ExternalInput").ap()
    bvr = nc.dram_tensor("bvr", [1, C], F32, kind="ExternalInput").ap()
    cbf = nc.dram_tensor("cbf", [128, 256], BF16, kind="ExternalInput").ap()
    out_fm = nc.dram_tensor("out_fm", [C, T], F32, kind="ExternalOutput").ap()
    x1d = nc.dram_tensor("x1d", [C, T], F32, kind="Internal").ap()
    sums_d = nc.dram_tensor("sums_d", [H, T], F32, kind="Internal").ap()

    with ExitStack() as st:
        BFt = st.enter_context(nc.sbuf_tensor("BF", [128, NBF], BF16))
        FAt = st.enter_context(nc.sbuf_tensor("FA", [128, NFA], F32))
        PSt = st.enter_context(nc.psum_tensor("PS", [128, 4096], F32))
        sD = st.enter_context(nc.semaphore("sD"))
        sT = st.enter_context(nc.semaphore("sT"))
        sV = st.enter_context(nc.semaphore("sV"))
        sS = st.enter_context(nc.semaphore("sS"))
        sG = st.enter_context(nc.semaphore("sG"))
        sA = st.enter_context(nc.semaphore("sA"))
        block = st.enter_context(nc.Block())

        BF = BFt.ap()
        FA = FAt.ap()
        PS = PSt.ap()

        ops = {"sync": [], "tensor": [], "vector": [], "scalar": [], "gpsimd": []}
        cnt = {"D": 0, "T": 0, "V": 0, "S": 0, "G": 0, "A": 0}
        sem = {"D": sD, "T": sT, "V": sV, "S": sS, "G": sG, "A": sA}
        seen = {e: {} for e in ops}
        ENG_OF = {"D": "sync", "T": "tensor", "V": "vector", "S": "scalar",
                  "G": "gpsimd", "A": "scalar"}

        def emit(ch, fn, waits=(), inc=False):
            eng = ENG_OF[ch]
            w2 = []
            for (sname, v) in waits:
                if v <= 0:
                    continue
                if sname == ch and ch not in ("D", "A"):
                    continue

                if v > seen[eng].get(sname, 0):
                    seen[eng][sname] = v
                    w2.append((sname, v))
            amt = 0
            if inc:
                amt = 16 if ch in ("D", "A") else 1
                cnt[ch] += amt
            ops[eng].append((fn, w2, sem[ch] if inc else None, amt))
            return cnt[ch]

        last_tr = [0]

        def dma(out_ap, in_ap, waits=(), transpose=False, ch="D"):
            w = list(waits)
            if transpose:
                w.append(("D", cnt["D"]))
                w.append(("A", cnt["A"]))
            elif last_tr[0] > 0:
                w.append(("D", last_tr[0]))
            d = emit(
                ch,
                lambda e, o=out_ap, i=in_ap, tr=transpose: e.dma_start(out=o, in_=i, transpose=tr),
                w, inc=True)
            if transpose:
                last_tr[0] = d
            return d

        def bf3(off, span, b, r):
            return BF[:, off:off + span].rearrange("p (b r) -> p b r", b=b, r=r)

        PREP_SLOT = [SP_PREP, SP_EX]

        class Prep:
            def __init__(self):
                self.qi = 0
                self.v_q = {}
                self.d_tr = {}
                self.onready = {}
                self.pending = []

            def _emit_load(self, src_ap, extra_ld_waits):
                qi = self.qi + len(self.pending)
                par = qi % 2
                w = list(extra_ld_waits)
                if qi >= 2:
                    w.append(("V", self.v_q[qi - 2]))
                d_ld = dma(FA[:, WS + 1024 * par:WS + 1024 * par + 1024], src_ap, waits=w)
                return d_ld, par

            def push(self, src_ap, kind, sg, oap, extra_tr_waits=(), extra_ld_waits=(),
                     on_ready=None):
                uidx = self.qi + len(self.pending)
                if on_ready is not None:
                    self.onready[uidx] = on_ready
                d_ld, par = self._emit_load(src_ap, extra_ld_waits)
                self.pending.append((d_ld, par, kind, sg, oap, list(extra_tr_waits)))
                if len(self.pending) > 1:
                    self._pop()

            def _pop(self):
                d_ld, par, kind, sg, oap, extra_tr = self.pending.pop(0)
                qi = self.qi
                src = FA[:, WS + 1024 * par:WS + 1024 * par + 1024]
                t1 = BF[:, PREP_SLOT[par]:PREP_SLOT[par] + 1024]
                t2 = BF[:, SP_T2:SP_T2 + 1024]
                wq_ = [("D", d_ld)]
                if qi >= 2:
                    wq_.append(("D", self.d_tr[qi - 2]))
                if kind == "cast":
                    self.v_q[qi] = emit("V", lambda e, o=t1, i=src: e.tensor_copy(o, i),
                                        wq_, inc=True)
                else:
                    emit("V", lambda e, o=t1, i=src, g=sg:
                         e.tensor_scalar(o, i, g, 0.5, ALU.mult, ALU.is_ge), wq_)
                    emit("V", lambda e, o=t2, i=src, g=sg:
                         e.tensor_scalar(o, i, g, -0.5, ALU.mult, ALU.is_le))
                    self.v_q[qi] = emit("V", lambda e, o=t1, a=t1, b=t2:
                                        e.tensor_sub(o, a, b), inc=True)
                d_tr = dma(oap, t1, waits=[("V", self.v_q[qi])] + extra_tr, transpose=True)
                self.d_tr[qi] = d_tr
                if qi in self.onready:
                    self.onready.pop(qi)(d_tr)
                self.qi += 1
                return d_tr

            def flush(self):
                d = 0
                while self.pending:
                    d = self._pop()
                return d

        prep = Prep()

        KPH = int(os.environ.get("KPH", "9"))

        def finish():
            final_d = cnt["D"]
            ops["sync"].append((lambda e: None, [("D", final_d)], None, 0))
            import kernel as _km
            _km._SCHED = {"ops": {e: [(w, _semname(sm), amt)
                                      for (fn, w, sm, amt) in ops[e]] for e in ops}}

            def emit_block(eng_name, e):
                for (fn, waits, s_, amt) in ops[eng_name]:
                    for (sname, v) in waits:
                        e.wait_ge(sem[sname], v)
                    inst = fn(e)
                    if s_ is not None:
                        inst.then_inc(s_, amt)

            @block.sync
            def _(e):
                emit_block("sync", e)

            @block.tensor
            def _(e):
                emit_block("tensor", e)

            @block.vector
            def _(e):
                emit_block("vector", e)

            @block.scalar
            def _(e):
                emit_block("scalar", e)

            @block.gpsimd
            def _(e):
                emit_block("gpsimd", e)

        d_const = dma(FA[:, SELo:SELo + CONST_COLS], consts)
        d_bv = dma(FA[0:1, BVo:BVo + C], bvr)
        d_cbf = dma(BF[:, SP_I:SP_I + 256], cbf)
        v_ones = emit("V", lambda e: e.memset(FA[0:1, ONERo:ONERo + 512], 1.0), inc=True)
        v_ones = emit("V", lambda e: e.memset(FA[:, ONECOLo:ONECOLo + 1], 1.0), inc=True)
        v_ones = emit("V", lambda e: e.memset(BF[:, SP_ONE:SP_ONE + 16], 1.0), inc=True)
        vone_ap = BF[:, VV:VV + 16640].rearrange("p (t h d) -> p t h d", t=16, h=16, d=65)[:, :, :, 64:65]
        v_ones = emit("V", lambda e: e.memset(vone_ap, 1.0), inc=True)

        ONES_BF = BF[:, SP_ONE:SP_ONE + 1]
        ONER128 = FA[0:1, ONERo:ONERo + 128]

        def ln_phase(src_dram, first):
            t_before = cnt["T"]
            d_x = [0] * NCH
            v_st = [0] * NCH
            for c in range(NCH):
                w = []
                if c >= 2:
                    w.append(("V", v_st[c - 2]))
                d_x[c] = dma(FA[:, G0 + 2048 * (c % 2):G0 + 2048 * (c % 2) + 2048],
                             src_dram[:, :].rearrange("(a p) m -> a p m", a=NCH, p=128)[c],
                             waits=w)
                for g in range(4):
                    xsrc = FA[:, G0 + 2048 * (c % 2) + 512 * g:G0 + 2048 * (c % 2) + 512 * (g + 1)]
                    exb = BF[:, SP_EX:SP_EX + 512]
                    exq = BF[:, SP_EX + 512:SP_EX + 1024]
                    wv_ = [("D", d_x[c])]
                    if c > 0 or g > 0:
                        wv_.append(("T", cnt["T"]))
                    emit("V", lambda e, o=exb, i=xsrc: e.tensor_copy(o, i), wv_)
                    v_c = emit("V", lambda e, o=exq, i=exb: e.tensor_mul(o, i, i), inc=True)
                    emit("T", lambda e, o=PS[0:1, 512 * g:512 * g + 512], l=ONES_BF, r=exb,
                         s=(c == 0), p=(c == NCH - 1):
                         e.matmul(o, l, r, start=s, stop=p), [("V", v_c)])
                    t_c = emit("T", lambda e, o=PS[0:1, 2048 + 512 * g:2048 + 512 * g + 512],
                               l=ONES_BF, r=exq, s=(c == 0), p=(c == NCH - 1):
                               e.matmul(o, l, r, start=s, stop=p), [("V", v_c)], inc=True)
                v_st[c] = cnt["V"]
            t_stats = cnt["T"]
            slotA = FA[0:1, 0:2048]
            slotB = FA[0:1, 2048:4096]
            emit("V", lambda e: e.tensor_scalar(slotA, PS[0:1, 0:2048], 1.0 / C, None, ALU.mult),
                 [("T", t_stats)])
            emit("V", lambda e: e.tensor_mul(slotB, slotA, slotA))
            v_var = emit("V", lambda e: e.scalar_tensor_tensor(
                slotB, PS[0:1, 2048:4096], 1.0 / C, slotB, ALU.mult, ALU.subtract),
                [("T", t_stats)], inc=True)
            emit("V", lambda e: e.tensor_scalar(slotB, slotB, EPS_LN, None, ALU.add))
            s_sd = emit("S", lambda e: e.activation(slotB, slotB, AF.Sqrt),
                        [("V", v_var)], inc=True)
            emit("V", lambda e: e.reciprocal(slotB, slotB), [("S", s_sd)])
            v_negm = emit("V", lambda e: e.tensor_scalar(
                slotA, slotA, -1.0, None, ALU.mult), inc=True)
            t_bc = 0
            for g in range(4):
                emit("T", lambda e, o=PS[:, 512 * g:512 * g + 512],
                     r=FA[0:1, 2048 + 512 * g:2048 + 512 * (g + 1)]:
                     e.matmul(o, ONER128, r, start=True, stop=True), [("V", v_negm)])
                t_bc = emit("T", lambda e, o=PS[:, 2048 + 512 * g:2048 + 512 * g + 512],
                            r=FA[0:1, 512 * g:512 * (g + 1)]:
                            e.matmul(o, ONER128, r, start=True, stop=True),
                            [("V", v_negm)], inc=(g == 3))
            v_ap = [0] * NCH
            for c in range(NCH):
                w = [("T", t_bc)]
                if c >= 2:
                    w.append(("V", v_ap[c - 2]))
                d_a = dma(FA[:, G0 + 2048 * (c % 2):G0 + 2048 * (c % 2) + 2048],
                          src_dram[:, :].rearrange("(a p) m -> a p m", a=NCH, p=128)[c],
                          waits=w)
                xs = FA[:, G0 + 2048 * (c % 2):G0 + 2048 * (c % 2) + 2048]
                emit("V", lambda e, x=xs: e.tensor_add(x, x, PS[:, 2048:4096]),
                     [("D", d_a), ("T", t_bc)])
                v_ap[c] = emit("V", lambda e, x=xs, o=BF[:, XH + 2048 * c:XH + 2048 * (c + 1)]:
                               e.tensor_mul(o, x, PS[:, 0:2048]), inc=True)
            return v_ap

        v_ap1 = ln_phase(xfm, True)
        if KPH < 2:
            finish()
            return nc

        d_c1 = dma(FA[:, 0:2048], cr1, waits=[("V", v_ap1[6])])
        d_c2 = dma(FA[:, 2048:4096], cr2, waits=[("V", v_ap1[7])])

        def prep_weight(wdram, wait_wt_free_t):
            extra_tr = [("T", wait_wt_free_t)] if wait_wt_free_t else []
            for ot in range(8):
                oap = BF[:, WT:WT + 8192].rearrange(
                    "p (b r) -> p b r", b=8, r=1024)[:, :, 128 * ot:128 * ot + 128]
                prep.push(wdram[128 * ot:128 * (ot + 1), :],
                          "cast", None, oap, extra_tr_waits=extra_tr)
            return prep.flush()

        def fm_proj(dst_off, bias_off, d_wt, extra_first_waits):
            v_rot = [0] * NCH
            s_qb = [0] * NCH
            t_oc = [0] * NCH
            for oc in range(NCH):
                pcol = (oc % 2) * 2048
                w = [("D", d_wt)]
                if oc == 0:
                    w += extra_first_waits
                if oc >= 2:
                    w.append(("S", s_qb[oc - 2]))
                for cc in range(NCH):
                    for g in range(4):
                        last = (cc == NCH - 1 and g == 3)
                        t_oc[oc] = emit("T", lambda e,
                                        o=PS[:, pcol + 512 * g:pcol + 512 * (g + 1)],
                                        l=BF[:, WT + 1024 * cc + 128 * oc:WT + 1024 * cc + 128 * oc + 128],
                                        r=BF[:, XH + 2048 * cc + 512 * g:XH + 2048 * cc + 512 * (g + 1)],
                                        s=(cc == 0), p=(cc == NCH - 1):
                                        e.matmul(o, l, r, start=s, stop=p),
                                        w, inc=last)
                    w = []
                qb = BF[:, SC:SC + 2048]
                qs = BF[:, SC + 2048:SC + 4096]
                wS = [("T", t_oc[oc])]
                if oc >= 1:
                    wS.append(("V", v_rot[oc - 1]))
                s_qb[oc] = emit("S", lambda e, o=qb, i=PS[:, pcol:pcol + 2048],
                                b=FA[:, bias_off + oc:bias_off + oc + 1]:
                                e.activation(o, i, AF.Identity, bias=b), wS, inc=True)
                emit("V", lambda e, o=qs, i=qb:
                     e.stream_shuffle(o, i, mask=[(j + 16) % 32 for j in range(32)]),
                     [("S", s_qb[oc])])
                emit("V", lambda e, o=qb, i=qb: e.tensor_mul(o, i, FA[:, 0:2048]),
                     [("D", d_c1)])
                emit("V", lambda e, o=qs, i=qs: e.tensor_mul(o, i, FA[:, 2048:4096]),
                     [("D", d_c2)])
                v_rot[oc] = emit("V", lambda e, o=BF[:, dst_off + 2048 * oc:dst_off + 2048 * (oc + 1)],
                                 a=qb, b=qs: e.tensor_add(o, a, b), inc=True)
            return t_oc[NCH - 1], v_rot, s_qb

        d_wkT = prep_weight(wk, 0)
        t_kproj, v_rotk, s_qbk = fm_proj(KK, BKo, d_wkT, [("V", v_ap1[7])])
        d_wvT = prep_weight(wv, t_kproj)
        t_vp = [0] * NTB
        v_vev = [0] * NTB
        for tb in range(NTB):
            pcol = 2048 + 1024 * (tb % 2)
            w = [("D", d_wvT)]
            if tb < 2:
                w.append(("S", s_qbk[NCH - 1]))
            if tb >= 2:
                w.append(("V", v_vev[tb - 2]))
            for cc in range(NCH):
                for g in range(2):
                    emit("T", lambda e, o=PS[:, pcol + 512 * g:pcol + 512 * (g + 1)],
                         l=BF[:, XH + 2048 * cc + 128 * tb:XH + 2048 * cc + 128 * tb + 128],
                         r=BF[:, WT + 1024 * cc + 512 * g:WT + 1024 * cc + 512 * (g + 1)],
                         s=(cc == 0): e.matmul(o, l, r, start=s, stop=False), w)
                w = []
            for g in range(2):
                t_vp[tb] = emit("T", lambda e, o=PS[:, pcol + 512 * g:pcol + 512 * (g + 1)],
                                l=ONER128, r=FA[0:1, BVo + 512 * g:BVo + 512 * (g + 1)]:
                                e.matmul(o, l, r, start=False, stop=True),
                                [("D", d_bv)], inc=(g == 1))
            oap = BF[:, VV + 1040 * tb:VV + 1040 * tb + 1040].rearrange(
                "p (h d) -> p h d", h=16, d=65)[:, :, 0:64]
            iap = PS[:, pcol:pcol + 1024].rearrange("p (h d) -> p h d", h=16, d=64)
            v_vev[tb] = emit("V", lambda e, o=oap, i=iap: e.tensor_copy(o, i),
                             [("T", t_vp[tb])], inc=True)
        d_wqT = prep_weight(wq, t_vp[NTB - 1])
        t_qproj, v_rotq, s_qbq = fm_proj(
            QQ, BQo, d_wqT, [("V", v_vev[NTB - 1]), ("S", s_qbk[NCH - 1])])

        if KPH < 3:
            finish()
            return nc
        d_wpT = prep_weight(wp, t_qproj)

        _vafter_preps = cnt["V"]
        gstate = {"gi": 0, "v_gprev": [_vafter_preps, _vafter_preps], "v_gred": 0}

        def gamma_unit():
            gi = gstate["gi"]
            if gi >= 64:
                return
            w = [("V", gstate["v_gprev"][gi % 2])]
            if gi < 64 - 32:
                iap = w1[128 * gi:128 * (gi + 1), :]
            else:
                tt = gi - 32
                iap = w2[:, :].rearrange("(a p) (b m) -> a b p m", p=128, m=1024)[tt // 4, tt % 4]
            d_g = dma(FA[:, WS + 1024 * (gi % 2):WS + 1024 * (gi % 2) + 1024], iap, waits=w)
            v = emit("V", lambda e, o=FA[:, GACCo + gi:GACCo + gi + 1],
                     i=FA[:, WS + 1024 * (gi % 2):WS + 1024 * (gi % 2) + 1024]:
                     e.tensor_reduce(o, i, AX.X, ALU.add, apply_absolute_value=True),
                     [("D", d_g)], inc=True)
            gstate["v_gprev"][gi % 2] = v
            gstate["v_gred"] = v
            gstate["gi"] = gi + 1

        pc = 0
        s_exp = {}
        t_pv_last = 0
        v_yev = [0] * H
        v_sum = [0] * H
        d_sumst = [0] * H
        pending_pv = [None]

        def flush_pv():
            nonlocal_t = pending_pv[0]
            if nonlocal_t is not None:
                nonlocal_t()
                pending_pv[0] = None

        for h in range(H):
            ch = h // 2
            pb = 64 * (h % 2)
            for _ in range(4):
                gamma_unit()
            for i in range(NTB):
                q0 = 128 * i
                Nq = T - q0
                pieces = [(q0, min(1024, Nq))]
                if Nq > 1024:
                    pieces.append((q0 + 1024, Nq - 1024))
                for pi, (qoff, plen) in enumerate(pieces):
                    par = pc % 2
                    stc = 1024 * par
                    wT_ = []
                    if pc >= 2:
                        wT_.append(("S", s_exp[pc - 2]))
                    if h == 0 and i == 0 and pi == 0:
                        wT_.append(("V", v_rotq[NCH - 1]))
                        wT_.append(("V", v_rotk[NCH - 1]))
                    t_kq = 0
                    ng = (plen + 511) // 512
                    for g in range(ng):
                        gw = min(512, plen - 512 * g)
                        diag = (pi == 0 and g == 0)
                        t_kq = emit("T", lambda e, o=PS[:, stc + 512 * g:stc + 512 * g + gw],
                                    l=BF[pb:pb + 64, KK + 2048 * ch + q0:KK + 2048 * ch + q0 + 128],
                                    r=BF[pb:pb + 64, QQ + 2048 * ch + qoff + 512 * g:
                                         QQ + 2048 * ch + qoff + 512 * g + gw],
                                    p=(not diag):
                                    e.matmul(o, l, r, start=True, stop=p),
                                    wT_, inc=(g == ng - 1 and not diag))
                        wT_ = []
                        if diag:
                            t_kq = emit("T", lambda e, o=PS[:, stc:stc + 128],
                                        l=BF[:, SP_I:SP_I + 128],
                                        r=BF[:, SP_I + 128:SP_I + 256]:
                                        e.matmul(o, l, r, start=False, stop=True),
                                        [("D", d_cbf)], inc=(g == ng - 1))
                    s_exp[pc] = emit("S", lambda e, o=BF[:, TPc + 1024 * par:TPc + 1024 * par + plen],
                                     i=PS[:, stc:stc + plen]:
                                     e.activation(o, i, AF.Exp, scale=0.125),
                                     [("T", t_kq)], inc=True)

                    def make_pv(pc_=pc, par_=par, qoff_=qoff, plen_=plen, i_=i, h_=h, pi_=pi):
                        def do():
                            nonlocal t_pv_last
                            wPV = [("S", s_exp[pc_])]
                            if i_ == 0 and pi_ == 0 and h_ > 0:
                                wPV.append(("V", v_sum[h_ - 1]))
                            off = 0
                            while off < plen_:
                                gw = min(512 - ((qoff_ + off) % 512), plen_ - off)
                                if pi_ == 0 and off == 0:
                                    gw = min(gw, 128)
                                stp = (pi_ == 0 and off == 0 and (i_ % 4) == 3)
                                st_ = (i_ == 0 and ((qoff_ + off) % 512) == 0)
                                t_pv_last = emit("T", lambda e,
                                                 o=PS[0:65, 2048 + qoff_ + off:2048 + qoff_ + off + gw],
                                                 l=BF[:, VV + 1040 * i_ + 65 * h_:VV + 1040 * i_ + 65 * h_ + 65],
                                                 r=BF[:, TPc + 1024 * par_ + off:TPc + 1024 * par_ + off + gw],
                                                 s=st_, p=stp:
                                                 e.matmul(o, l, r, start=s, stop=p),
                                                 wPV, inc=(off + gw >= plen_))
                                wPV = []
                                off += gw
                        return do

                    flush_pv()
                    pending_pv[0] = make_pv()
                    pc += 1
            flush_pv()
            wY = [("T", t_pv_last)]
            if h == 0:
                wY.append(("T", t_qproj))
            v_yev[h] = emit("V", lambda e, o=BF[pb:pb + 64, XH + 2048 * ch:XH + 2048 * (ch + 1)],
                            i=PS[0:64, 2048:4096]: e.tensor_copy(o, i), wY, inc=True)
            wSm = []
            if h >= 1:
                wSm.append(("D", d_sumst[h - 1]))
            v_sum[h] = emit("V", lambda e, o=FA[0:1, 0:2048], i=PS[64:65, 2048:4096]:
                            e.tensor_copy(o, i), wSm, inc=True)
            d_sumst[h] = dma(sums_d[h:h + 1, :], FA[0:1, 0:2048], waits=[("V", v_sum[h])])

        d_sumld = dma(FA[0:16, 0:2048], sums_d, waits=[("D", d_sumst[H - 1])])
        v_inv = emit("V", lambda e: e.reciprocal(FA[0:16, 0:2048], FA[0:16, 0:2048]),
                     [("D", d_sumld)], inc=True)
        t_selbc = [0] * NCH
        v_ynorm = [0] * NCH
        for c2 in range(NCH):
            pcol = (c2 % 2) * 2048
            w = [("V", v_inv), ("D", d_const)]
            if c2 >= 2:
                w.append(("V", v_ynorm[c2 - 2]))
            for g in range(4):
                t_selbc[c2] = emit("T", lambda e, o=PS[:, pcol + 512 * g:pcol + 512 * (g + 1)],
                                   l=FA[0:16, SELo + 128 * c2:SELo + 128 * (c2 + 1)],
                                   r=FA[0:16, 512 * g:512 * (g + 1)]:
                                   e.matmul(o, l, r, start=True, stop=True), w, inc=(g == 3))
                w = []
            v_ynorm[c2] = emit("V", lambda e, o=BF[:, XH + 2048 * c2:XH + 2048 * (c2 + 1)],
                               i=PS[:, pcol:pcol + 2048]:
                               e.tensor_mul(o, o, i),
                               [("T", t_selbc[c2]), ("V", v_yev[2 * c2 + 1])], inc=True)

        if KPH < 4:
            finish()
            return nc
        while gstate["gi"] < 64:
            gamma_unit()
        emit("V", lambda e: e.tensor_reduce(FA[:, GT:GT + 1], FA[:, GACCo:GACCo + 32], AX.X, ALU.add),
             [("V", gstate["v_gred"])])
        v_g2 = emit("V", lambda e: e.tensor_reduce(FA[:, GT + 1:GT + 2], FA[:, GACCo + 32:GACCo + 64],
                                                   AX.X, ALU.add), inc=True)
        emit("T", lambda e: e.matmul(PS[0:1, 0:1], FA[:, ONECOLo:ONECOLo + 1], FA[:, GT:GT + 1],
                                     start=True, stop=True),
             [("V", v_g2), ("V", v_ynorm[NCH - 1])])
        t_gm = emit("T", lambda e: e.matmul(PS[0:1, 1:2], FA[:, ONECOLo:ONECOLo + 1],
                                            FA[:, GT + 1:GT + 2], start=True, stop=True),
                    [("V", v_g2)], inc=True)
        emit("V", lambda e: e.tensor_scalar(FA[0:1, GT + 4:GT + 5], PS[0:1, 0:1],
                                            1.0 / (FF * C), EPS_Q, ALU.mult, ALU.add),
             [("T", t_gm)])
        emit("V", lambda e: e.tensor_scalar(FA[0:1, GT + 5:GT + 6], PS[0:1, 1:2],
                                            1.0 / (FF * C), EPS_Q, ALU.mult, ALU.add))
        emit("V", lambda e: e.reciprocal(FA[0:1, GT + 4:GT + 5], FA[0:1, GT + 4:GT + 5]))
        v_gi = emit("V", lambda e: e.reciprocal(FA[0:1, GT + 5:GT + 6], FA[0:1, GT + 5:GT + 6]),
                    inc=True)
        emit("T", lambda e: e.matmul(PS[:, 4:5], ONER128, FA[0:1, GT + 4:GT + 5],
                                     start=True, stop=True), [("V", v_gi)])
        t_gb = emit("T", lambda e: e.matmul(PS[:, 5:6], ONER128, FA[0:1, GT + 5:GT + 6],
                                            start=True, stop=True), inc=True)
        emit("V", lambda e: e.tensor_copy(FA[:, GT + 2:GT + 3], PS[:, 4:5]), [("T", t_gb)])
        v_sg = emit("V", lambda e: e.tensor_copy(FA[:, GT + 3:GT + 4], PS[:, 5:6]), inc=True)
        SG1 = FA[:, GT + 2:GT + 3]
        SG2 = FA[:, GT + 3:GT + 4]

        t_pp = [0] * NCH
        v_res = [0] * NCH
        d_x1w = [0] * NCH
        for oc in range(NCH):
            pcol = (oc % 2) * 2048
            w = [("D", d_wpT), ("V", v_sg), ("V", v_ynorm[NCH - 1])]
            if oc >= 2:
                w.append(("V", v_res[oc - 2]))
            for cc in range(NCH):
                for g in range(4):
                    t_pp[oc] = emit("T", lambda e, o=PS[:, pcol + 512 * g:pcol + 512 * (g + 1)],
                                    l=BF[:, WT + 1024 * cc + 128 * oc:WT + 1024 * cc + 128 * oc + 128],
                                    r=BF[:, XH + 2048 * cc + 512 * g:XH + 2048 * cc + 512 * (g + 1)],
                                    s=(cc == 0), p=(cc == NCH - 1): e.matmul(o, l, r, start=s, stop=p),
                                    w, inc=(cc == NCH - 1 and g == 3))
                    w = []
            wD = [("T", t_selbc[NCH - 1]), ("V", v_ynorm[NCH - 1])]
            if oc >= 2:
                wD.append(("V", v_res[oc - 2]))
            d_xr = dma(FA[:, G0 + 2048 * (oc % 2):G0 + 2048 * (oc % 2) + 2048],
                       xfm[:, :].rearrange("(a p) m -> a p m", a=NCH, p=128)[oc], waits=wD)
            xs = FA[:, G0 + 2048 * (oc % 2):G0 + 2048 * (oc % 2) + 2048]
            v_res[oc] = emit("V", lambda e, x=xs, i=PS[:, pcol:pcol + 2048],
                             b=FA[:, BPo + oc:BPo + oc + 1]:
                             e.scalar_tensor_tensor(x, i, b, x, ALU.add, ALU.add),
                             [("T", t_pp[oc]), ("D", d_xr)], inc=True)
            d_x1w[oc] = dma(x1d[:, :].rearrange("(a p) m -> a p m", a=NCH, p=128)[oc], xs,
                            waits=[("V", v_res[oc])])

        d_w2rdy = [0] * 32
        d_w1rdy = [0] * 32

        def mk_rdy2(sl):
            def f(d):
                for b_ in range(8):
                    d_w2rdy[8 * sl + b_] = d
            return f

        sl_order = [0, 1, 3, 2]
        first_w2 = [("V", gstate["v_gred"])]
        for sl in sl_order:
            for dt in range(8):
                base = w2base(8 * sl)
                oap = BF[:, base:base + 8192].rearrange(
                    "p (b r) -> p b r", b=8, r=1024)[:, :, 128 * dt:128 * dt + 128]
                ex = [("T", t_pv_last)]
                if sl == 2:
                    ex.append(("T", t_pp[NCH - 1]))
                prep.push(w2[128 * dt:128 * (dt + 1), 1024 * sl:1024 * (sl + 1)],
                          "quant", SG2, oap, extra_tr_waits=ex,
                          extra_ld_waits=first_w2,
                          on_ready=mk_rdy2(sl) if dt == 7 else None)
                first_w2 = []

        ln2_gate_t = t_pp[NCH - 1]
        ln2_gate_v = v_res[NCH - 1]
        v_ap2 = None

        def ln_phase2():
            d_x = [0] * NCH
            v_st = [0] * NCH
            for c in range(NCH):
                w = [("D", d_x1w[c])]
                if c >= 2:
                    w.append(("V", v_st[c - 2]))
                d_x[c] = dma(FA[:, G0 + 2048 * (c % 2):G0 + 2048 * (c % 2) + 2048],
                             x1d[:, :].rearrange("(a p) m -> a p m", a=NCH, p=128)[c], waits=w)
                for g in range(4):
                    xsrc = FA[:, G0 + 2048 * (c % 2) + 512 * g:G0 + 2048 * (c % 2) + 512 * (g + 1)]
                    exb = BF[:, SP_EX:SP_EX + 512]
                    exq = BF[:, SP_EX + 512:SP_EX + 1024]
                    wv_ = [("D", d_x[c]), ("T", ln2_gate_t)]
                    if c > 0 or g > 0:
                        wv_.append(("T", cnt["T"]))
                    emit("V", lambda e, o=exb, i=xsrc: e.tensor_copy(o, i), wv_)
                    v_c = emit("V", lambda e, o=exq, i=exb: e.tensor_mul(o, i, i), inc=True)
                    emit("T", lambda e, o=PS[0:1, 512 * g:512 * g + 512], l=ONES_BF, r=exb,
                         s=(c == 0), p=(c == NCH - 1):
                         e.matmul(o, l, r, start=s, stop=p), [("V", v_c), ("V", ln2_gate_v)])
                    emit("T", lambda e, o=PS[0:1, 2048 + 512 * g:2048 + 512 * g + 512],
                         l=ONES_BF, r=exq, s=(c == 0), p=(c == NCH - 1):
                         e.matmul(o, l, r, start=s, stop=p), [("V", v_c)], inc=True)
                v_st[c] = cnt["V"]
            t_stats = cnt["T"]
            slotA = FA[0:1, 0:2048]
            slotB = FA[0:1, 2048:4096]
            emit("V", lambda e: e.tensor_scalar(slotA, PS[0:1, 0:2048], 1.0 / C, None, ALU.mult),
                 [("T", t_stats)])
            emit("V", lambda e: e.tensor_mul(slotB, slotA, slotA))
            v_var = emit("V", lambda e: e.scalar_tensor_tensor(
                slotB, PS[0:1, 2048:4096], 1.0 / C, slotB, ALU.mult, ALU.subtract),
                [("T", t_stats)], inc=True)
            emit("V", lambda e: e.tensor_scalar(slotB, slotB, EPS_LN, None, ALU.add))
            s_sd = emit("S", lambda e: e.activation(slotB, slotB, AF.Sqrt),
                        [("V", v_var)], inc=True)
            emit("V", lambda e: e.reciprocal(slotB, slotB), [("S", s_sd)])
            v_negm = emit("V", lambda e: e.tensor_scalar(
                slotA, slotA, -1.0, None, ALU.mult), inc=True)
            t_bc = 0
            for g in range(4):
                emit("T", lambda e, o=PS[:, 512 * g:512 * g + 512],
                     r=FA[0:1, 2048 + 512 * g:2048 + 512 * (g + 1)]:
                     e.matmul(o, ONER128, r, start=True, stop=True), [("V", v_negm)])
                t_bc = emit("T", lambda e, o=PS[:, 2048 + 512 * g:2048 + 512 * g + 512],
                            r=FA[0:1, 512 * g:512 * (g + 1)]:
                            e.matmul(o, ONER128, r, start=True, stop=True),
                            [("V", v_negm)], inc=(g == 3))
            v_ap = [0] * NCH
            for c in range(NCH):
                w = [("T", t_bc)]
                if c >= 2:
                    w.append(("V", v_ap[c - 2]))
                d_a = dma(FA[:, G0 + 2048 * (c % 2):G0 + 2048 * (c % 2) + 2048],
                          x1d[:, :].rearrange("(a p) m -> a p m", a=NCH, p=128)[c], waits=w)
                xs = FA[:, G0 + 2048 * (c % 2):G0 + 2048 * (c % 2) + 2048]
                emit("V", lambda e, x=xs: e.tensor_add(x, x, PS[:, 2048:4096]),
                     [("D", d_a), ("T", t_bc)])
                v_ap[c] = emit("V", lambda e, x=xs, o=BF[:, XH + 2048 * c:XH + 2048 * (c + 1)]:
                               e.tensor_mul(o, x, PS[:, 0:2048]), inc=True)
            return v_ap

        v_ap2 = ln_phase2()

        def mk_rdy1(fc):
            def f(d):
                d_w1rdy[fc] = d
            return f

        for fc in range(32):
            oap = BF[:, QQ:QQ + 32768].rearrange(
                "p (b r) -> p b r", b=8, r=4096)[:, :, 128 * fc:128 * fc + 128]
            prep.push(w1[128 * fc:128 * (fc + 1), :],
                      "quant", SG1, oap, extra_tr_waits=[("T", t_pv_last)],
                      on_ready=mk_rdy1(fc))
        prep.flush()

        t_mm1 = {}
        s_relu = {}
        t_mm2 = {}
        v_out = [0] * 8
        d_out = [0] * 8
        pending_mm2 = [None]

        def flush_mm2():
            if pending_mm2[0] is not None:
                pending_mm2[0]()
                pending_mm2[0] = None

        for tp in range(8):
            w = [("T", t_selbc[NCH - 1])]
            if tp >= 2:
                w.append(("V", v_out[tp - 2]))
            if tp < 2:
                w.append(("V", v_ap2[NCH - 1]))
                w.append(("D", d_x1w[NCH - 1]))
            d_x1p = dma(FA[:, G0 + 2048 * (tp % 2):G0 + 2048 * (tp % 2) + 2048].rearrange(
                "p (a m) -> p a m", a=8, m=256),
                x1d[:, :].rearrange("(a p) m -> p a m", a=NCH, p=128)[:, :, 256 * tp:256 * (tp + 1)],
                waits=w)
            for fc in range(32):
                hcol = 512 * (fc % 2)
                wmm1 = [("D", d_w1rdy[fc]), ("V", v_ap2[NCH - 1])]
                prev = (tp, fc - 2) if fc >= 2 else ((tp - 1, 30 + fc) if tp >= 1 else None)
                if prev in s_relu:
                    wmm1.append(("S", s_relu[prev]))
                for cc in range(NCH):
                    t_mm1[(tp, fc)] = emit("T", lambda e, o=PS[:, hcol:hcol + 256],
                                           l=BF[:, QQ + 4096 * cc + 128 * fc:QQ + 4096 * cc + 128 * fc + 128],
                                           r=BF[:, XH + 2048 * cc + 256 * tp:XH + 2048 * cc + 256 * (tp + 1)],
                                           s=(cc == 0), p=(cc == NCH - 1): e.matmul(o, l, r, start=s, stop=p),
                                           wmm1, inc=(cc == NCH - 1))
                    wmm1 = []
                s_relu[(tp, fc)] = emit("S", lambda e, o=BF[:, SP_H + 256 * (fc % 2):SP_H + 256 * (fc % 2) + 256],
                                        i=PS[:, hcol:hcol + 256], b=FA[:, B1o + fc:B1o + fc + 1]:
                                        e.activation(o, i, AF.Relu, bias=b),
                                        [("T", t_mm1[(tp, fc)])], inc=True)
                def make_mm2(tp_=tp, fc_=fc):
                    def do():
                        wmm2 = [("S", s_relu[(tp_, fc_)]), ("D", d_w2rdy[fc_])]
                        if fc_ == 0 and tp_ >= 1:
                            wmm2.append(("V", v_out[tp_ - 1]))
                        for dc in range(NCH):
                            st_ = (fc_ == 0 and dc % 2 == 0)
                            t_mm2[(tp_, fc_)] = emit(
                                "T", lambda e, o=PS[:, 2048 + 256 * dc:2048 + 256 * (dc + 1)],
                                l=BF[:, w2base(fc_) + 128 * dc:w2base(fc_) + 128 * dc + 128],
                                r=BF[:, SP_H + 256 * (fc_ % 2):SP_H + 256 * (fc_ % 2) + 256],
                                s=st_, p=(fc_ == 31 and dc % 2 == 1):
                                e.matmul(o, l, r, start=s, stop=p),
                                wmm2, inc=(dc == NCH - 1))
                            wmm2[:] = []
                    return do

                flush_mm2()
                pending_mm2[0] = make_mm2()
            flush_mm2()
            for dc in range(NCH):
                xs = FA[:, G0 + 2048 * (tp % 2) + 256 * dc:G0 + 2048 * (tp % 2) + 256 * (dc + 1)]
                v_out[tp] = emit("V", lambda e, x=xs, i=PS[:, 2048 + 256 * dc:2048 + 256 * (dc + 1)],
                                 b=FA[:, B2o + dc:B2o + dc + 1]:
                                 e.scalar_tensor_tensor(x, i, b, x, ALU.add, ALU.add),
                                 [("T", t_mm2[(tp, 31)]), ("D", d_x1p)], inc=(dc == NCH - 1))
            d_out[tp] = dma(out_fm[:, :].rearrange("(a p) m -> p a m", a=NCH, p=128)[
                :, :, 256 * tp:256 * (tp + 1)],
                FA[:, G0 + 2048 * (tp % 2):G0 + 2048 * (tp % 2) + 2048].rearrange(
                    "p (a m) -> p a m", a=8, m=256),
                waits=[("V", v_out[tp])])

        finish()
    return nc


def _perm():
    perm = np.zeros(C, dtype=np.int64)
    for g in range(C):
        chk, p = g // 128, g % 128
        h = 2 * chk + p // 64
        lp = p % 64
        quad = lp // 32
        r = lp % 32
        if r < 16:
            i_idx, par = r + 16 * quad, 0
        else:
            i_idx, par = (r - 16) + 16 * quad, 1
        perm[g] = 64 * h + 2 * i_idx + par
    return perm


def _rope_tables():
    i = np.arange(32, dtype=np.float32)
    theta = (np.float32(1.0) / (np.float32(10000.0) ** (np.float32(2.0) * i / np.float32(64)))).astype(np.float32)
    t = np.arange(T, dtype=np.float32)[:, None]
    ang = (t * theta[None, :]).astype(np.float32)
    cos = np.cos(ang.astype(np.float64)).astype(np.float32)
    sin = np.sin(ang.astype(np.float64)).astype(np.float32)
    C1 = np.zeros((128, T), np.float32)
    C2 = np.zeros((128, T), np.float32)
    for p in range(128):
        lp = p % 64
        quad = lp // 32
        r = lp % 32
        if r < 16:
            i_idx, is_even = r + 16 * quad, True
        else:
            i_idx, is_even = (r - 16) + 16 * quad, False
        C1[p] = cos[:, i_idx]
        C2[p] = (-sin[:, i_idx]) if is_even else sin[:, i_idx]
    return C1, C2


def _consts_pack(b1, b2, bq_p, bk_p, bp):
    A = np.zeros((128, CONST_COLS), np.float32)
    for c2 in range(NCH):
        for j in range(128):
            h = 2 * c2 + (1 if j >= 64 else 0)
            A[h, 128 * c2 + j] = 1.0
    for p in range(128):
        A[p, 1024:1024 + p] = NEG
    A[:, 1152:1184] = b1.reshape(32, 128).T
    A[:, 1184:1192] = b2.reshape(8, 128).T
    A[:, 1192:1200] = bq_p.reshape(8, 128).T
    A[:, 1200:1208] = bk_p.reshape(8, 128).T
    A[:, 1208:1216] = bp.reshape(8, 128).T
    return A


_NC_CACHE = {}


def kernel(L0_attn_x, L0_attn_wq, L0_attn_bq, L0_attn_wk, L0_attn_bk,
           L0_attn_wv, L0_attn_bv, L0_attn_wp, L0_attn_bp,
           L0_ffn_W1, L0_ffn_b1, L0_ffn_W2, L0_ffn_b2):
    x = np.asarray(L0_attn_x, dtype=np.float32)
    perm = _perm()
    wq_p = np.ascontiguousarray(np.asarray(L0_attn_wq, np.float32)[perm])
    wk_p = np.ascontiguousarray(np.asarray(L0_attn_wk, np.float32)[perm])
    bq_p = np.ascontiguousarray(np.asarray(L0_attn_bq, np.float32)[perm])
    bk_p = np.ascontiguousarray(np.asarray(L0_attn_bk, np.float32)[perm])
    wv = np.ascontiguousarray(np.asarray(L0_attn_wv, np.float32))
    wp = np.ascontiguousarray(np.asarray(L0_attn_wp, np.float32))
    w1 = np.ascontiguousarray(np.asarray(L0_ffn_W1, np.float32))
    w2 = np.ascontiguousarray(np.asarray(L0_ffn_W2, np.float32))
    C1, C2 = _rope_tables()
    consts = _consts_pack(np.asarray(L0_ffn_b1, np.float32), np.asarray(L0_ffn_b2, np.float32),
                          bq_p, bk_p, np.asarray(L0_attn_bp, np.float32))
    bvr = np.ascontiguousarray(np.asarray(L0_attn_bv, np.float32).reshape(1, C))
    cbf_host = np.zeros((128, 256), dtype=ml_dtypes.bfloat16)
    cbf_host[:, 0:128] = np.eye(128, dtype=np.float32).astype(ml_dtypes.bfloat16)
    mtri = np.zeros((128, 128), np.float32)
    for p in range(128):
        mtri[p, :p] = NEG
    cbf_host[:, 128:256] = mtri.astype(ml_dtypes.bfloat16)

    if "nc" not in _NC_CACHE:
        _NC_CACHE["nc"] = build_nc()
    nc = _NC_CACHE["nc"]

    in_maps = []
    for b in range(8):
        in_maps.append({
            "xfm": np.ascontiguousarray(x[b].T),
            "wq": wq_p, "wk": wk_p, "wv": wv, "wp": wp,
            "w1": w1, "w2": w2,
            "cr1": C1, "cr2": C2, "consts": consts, "bvr": bvr, "cbf": cbf_host,
        })
    res = run_bass_kernel_spmd(nc, in_maps, core_ids=list(range(8)))
    out = np.stack([np.asarray(r["out_fm"]).T for r in res.results], axis=0)
    return out.astype(np.float32)


# version 10
# speedup vs baseline: 1.6801x; 1.0017x over previous
import os
import numpy as np
import ml_dtypes
from contextlib import ExitStack

import concourse.bass as bass
import concourse.mybir as mybir
from concourse.bass_utils import run_bass_kernel_spmd

F32 = mybir.dt.float32
BF16 = mybir.dt.bfloat16
AF = mybir.ActivationFunctionType
ALU = mybir.AluOpType
AX = mybir.AxisListType

T = 2048
C = 1024
H = 16
DH = 64
FF = 4096
NCH = 8
NTB = 16
NEG = -1.0e10
EPS_LN = 1e-5
EPS_Q = 1e-5

XH = 0
QQ = 16384
KK = 32768
VV = 49152
WT = 65792
SC = 73984
WB = 78080
TPc = 80128
SP = 82176
SP_H = SP
SP_PREP = SP + 512
SP_ONE = SP + 1536
SP_EX = SP + 1552
SP_I = SP + 2576
SP_T2 = SP + 2832
NBF = SP + 2832 + 1024

def w2base(fc):
    if fc < 16:
        return VV + 1024 * fc
    if fc < 24:
        return WT + 1024 * (fc - 16)
    if fc < 28:
        return SC + 1024 * (fc - 24)
    return WB + 1024 * (fc - 28)

G0 = 0
WS = 4096
SELo = 6144
MTRIo = 7168
B1o = 7296
B2o = 7328
BQo = 7336
BKo = 7344
BPo = 7352
GACCo = 7360
ONECOLo = 7424
GT = 7425
BVo = 7456
ONERo = 8480
NFA = 8992

CONST_COLS = 1232


_SCHED = None


def _semname(sm):
    return getattr(sm, "name", None) if sm is not None else None


def build_nc():
    nc = bass.Bass("TRN2", target_bir_lowering=False, detect_race_conditions=False)
    xfm = nc.dram_tensor("xfm", [C, T], F32, kind="ExternalInput").ap()
    wq = nc.dram_tensor("wq", [C, C], F32, kind="ExternalInput").ap()
    wk = nc.dram_tensor("wk", [C, C], F32, kind="ExternalInput").ap()
    wv = nc.dram_tensor("wv", [C, C], F32, kind="ExternalInput").ap()
    wp = nc.dram_tensor("wp", [C, C], F32, kind="ExternalInput").ap()
    w1 = nc.dram_tensor("w1", [FF, C], F32, kind="ExternalInput").ap()
    w2 = nc.dram_tensor("w2", [C, FF], F32, kind="ExternalInput").ap()
    cr1 = nc.dram_tensor("cr1", [128, T], F32, kind="ExternalInput").ap()
    cr2 = nc.dram_tensor("cr2", [128, T], F32, kind="ExternalInput").ap()
    consts = nc.dram_tensor("consts", [128, CONST_COLS], F32, kind="ExternalInput").ap()
    bvr = nc.dram_tensor("bvr", [1, C], F32, kind="ExternalInput").ap()
    cbf = nc.dram_tensor("cbf", [128, 256], BF16, kind="ExternalInput").ap()
    out_fm = nc.dram_tensor("out_fm", [C, T], F32, kind="ExternalOutput").ap()
    x1d = nc.dram_tensor("x1d", [C, T], F32, kind="Internal").ap()
    sums_d = nc.dram_tensor("sums_d", [H, T], F32, kind="Internal").ap()

    with ExitStack() as st:
        BFt = st.enter_context(nc.sbuf_tensor("BF", [128, NBF], BF16))
        FAt = st.enter_context(nc.sbuf_tensor("FA", [128, NFA], F32))
        PSt = st.enter_context(nc.psum_tensor("PS", [128, 4096], F32))
        sD = st.enter_context(nc.semaphore("sD"))
        sT = st.enter_context(nc.semaphore("sT"))
        sV = st.enter_context(nc.semaphore("sV"))
        sS = st.enter_context(nc.semaphore("sS"))
        sG = st.enter_context(nc.semaphore("sG"))
        sA = st.enter_context(nc.semaphore("sA"))
        block = st.enter_context(nc.Block())

        BF = BFt.ap()
        FA = FAt.ap()
        PS = PSt.ap()

        ops = {"sync": [], "tensor": [], "vector": [], "scalar": [], "gpsimd": []}
        cnt = {"D": 0, "T": 0, "V": 0, "S": 0, "G": 0, "A": 0}
        sem = {"D": sD, "T": sT, "V": sV, "S": sS, "G": sG, "A": sA}
        seen = {e: {} for e in ops}
        ENG_OF = {"D": "sync", "T": "tensor", "V": "vector", "S": "scalar",
                  "G": "gpsimd", "A": "scalar"}

        def emit(ch, fn, waits=(), inc=False):
            eng = ENG_OF[ch]
            w2 = []
            for (sname, v) in waits:
                if v <= 0:
                    continue
                if sname == ch and ch not in ("D", "A"):
                    continue

                if v > seen[eng].get(sname, 0):
                    seen[eng][sname] = v
                    w2.append((sname, v))
            amt = 0
            if inc:
                amt = 16 if ch in ("D", "A") else 1
                cnt[ch] += amt
            ops[eng].append((fn, w2, sem[ch] if inc else None, amt))
            return cnt[ch]

        last_tr = [0]

        def dma(out_ap, in_ap, waits=(), transpose=False, ch="D"):
            w = list(waits)
            if transpose:
                w.append(("D", cnt["D"]))
                w.append(("A", cnt["A"]))
            elif last_tr[0] > 0:
                w.append(("D", last_tr[0]))
            d = emit(
                ch,
                lambda e, o=out_ap, i=in_ap, tr=transpose: e.dma_start(out=o, in_=i, transpose=tr),
                w, inc=True)
            if transpose:
                last_tr[0] = d
            return d

        def bf3(off, span, b, r):
            return BF[:, off:off + span].rearrange("p (b r) -> p b r", b=b, r=r)

        PREP_SLOT = [SP_PREP, SP_EX]

        class Prep:
            def __init__(self):
                self.qi = 0
                self.v_q = {}
                self.d_tr = {}
                self.onready = {}
                self.pending = []

            def _emit_load(self, src_ap, extra_ld_waits):
                qi = self.qi + len(self.pending)
                par = qi % 2
                w = list(extra_ld_waits)
                if qi >= 2:
                    w.append(("V", self.v_q[qi - 2]))
                d_ld = dma(FA[:, WS + 1024 * par:WS + 1024 * par + 1024], src_ap, waits=w)
                return d_ld, par

            def push(self, src_ap, kind, sg, oap, extra_tr_waits=(), extra_ld_waits=(),
                     on_ready=None):
                uidx = self.qi + len(self.pending)
                if on_ready is not None:
                    self.onready[uidx] = on_ready
                d_ld, par = self._emit_load(src_ap, extra_ld_waits)
                self.pending.append((d_ld, par, kind, sg, oap, list(extra_tr_waits)))
                if len(self.pending) > 1:
                    self._pop()

            def _pop(self):
                d_ld, par, kind, sg, oap, extra_tr = self.pending.pop(0)
                qi = self.qi
                src = FA[:, WS + 1024 * par:WS + 1024 * par + 1024]
                t1 = BF[:, PREP_SLOT[par]:PREP_SLOT[par] + 1024]
                t2 = BF[:, SP_T2:SP_T2 + 1024]
                wq_ = [("D", d_ld)]
                if qi >= 2:
                    wq_.append(("D", self.d_tr[qi - 2]))
                if kind == "cast":
                    self.v_q[qi] = emit("V", lambda e, o=t1, i=src: e.tensor_copy(o, i),
                                        wq_, inc=True)
                else:
                    emit("V", lambda e, o=t1, i=src, g=sg:
                         e.tensor_scalar(o, i, g, 0.5, ALU.mult, ALU.is_ge), wq_)
                    emit("V", lambda e, o=t2, i=src, g=sg:
                         e.tensor_scalar(o, i, g, -0.5, ALU.mult, ALU.is_le))
                    self.v_q[qi] = emit("V", lambda e, o=t1, a=t1, b=t2:
                                        e.tensor_sub(o, a, b), inc=True)
                d_tr = dma(oap, t1, waits=[("V", self.v_q[qi])] + extra_tr, transpose=True)
                self.d_tr[qi] = d_tr
                if qi in self.onready:
                    self.onready.pop(qi)(d_tr)
                self.qi += 1
                return d_tr

            def flush(self):
                d = 0
                while self.pending:
                    d = self._pop()
                return d

        prep = Prep()

        KPH = int(os.environ.get("KPH", "9"))

        def finish():
            final_d = cnt["D"]
            ops["sync"].append((lambda e: None, [("D", final_d)], None, 0))
            import kernel as _km
            _km._SCHED = {"ops": {e: [(w, _semname(sm), amt)
                                      for (fn, w, sm, amt) in ops[e]] for e in ops}}

            def emit_block(eng_name, e):
                for (fn, waits, s_, amt) in ops[eng_name]:
                    for (sname, v) in waits:
                        e.wait_ge(sem[sname], v)
                    inst = fn(e)
                    if s_ is not None:
                        inst.then_inc(s_, amt)

            @block.sync
            def _(e):
                emit_block("sync", e)

            @block.tensor
            def _(e):
                emit_block("tensor", e)

            @block.vector
            def _(e):
                emit_block("vector", e)

            @block.scalar
            def _(e):
                emit_block("scalar", e)

            @block.gpsimd
            def _(e):
                emit_block("gpsimd", e)

        d_const = dma(FA[:, SELo:SELo + CONST_COLS], consts)
        d_bv = dma(FA[0:1, BVo:BVo + C], bvr)
        d_cbf = dma(BF[:, SP_I:SP_I + 256], cbf)
        v_ones = emit("V", lambda e: e.memset(FA[0:1, ONERo:ONERo + 512], 1.0), inc=True)
        v_ones = emit("V", lambda e: e.memset(FA[:, ONECOLo:ONECOLo + 1], 1.0), inc=True)
        v_ones = emit("V", lambda e: e.memset(BF[:, SP_ONE:SP_ONE + 16], 1.0), inc=True)
        vone_ap = BF[:, VV:VV + 16640].rearrange("p (t h d) -> p t h d", t=16, h=16, d=65)[:, :, :, 64:65]
        v_ones = emit("V", lambda e: e.memset(vone_ap, 1.0), inc=True)

        ONES_BF = BF[:, SP_ONE:SP_ONE + 1]
        ONER128 = FA[0:1, ONERo:ONERo + 128]

        def ln_phase(src_dram, first):
            t_before = cnt["T"]
            d_x = [0] * NCH
            v_st = [0] * NCH
            for c in range(NCH):
                w = []
                if c >= 2:
                    w.append(("V", v_st[c - 2]))
                d_x[c] = dma(FA[:, G0 + 2048 * (c % 2):G0 + 2048 * (c % 2) + 2048],
                             src_dram[:, :].rearrange("(a p) m -> a p m", a=NCH, p=128)[c],
                             waits=w)
                for g in range(4):
                    xsrc = FA[:, G0 + 2048 * (c % 2) + 512 * g:G0 + 2048 * (c % 2) + 512 * (g + 1)]
                    exb = BF[:, SP_EX:SP_EX + 512]
                    exq = BF[:, SP_EX + 512:SP_EX + 1024]
                    wv_ = [("D", d_x[c])]
                    if c > 0 or g > 0:
                        wv_.append(("T", cnt["T"]))
                    emit("V", lambda e, o=exb, i=xsrc: e.tensor_copy(o, i), wv_)
                    v_c = emit("V", lambda e, o=exq, i=exb: e.tensor_mul(o, i, i), inc=True)
                    emit("T", lambda e, o=PS[0:1, 512 * g:512 * g + 512], l=ONES_BF, r=exb,
                         s=(c == 0), p=(c == NCH - 1):
                         e.matmul(o, l, r, start=s, stop=p), [("V", v_c)])
                    t_c = emit("T", lambda e, o=PS[0:1, 2048 + 512 * g:2048 + 512 * g + 512],
                               l=ONES_BF, r=exq, s=(c == 0), p=(c == NCH - 1):
                               e.matmul(o, l, r, start=s, stop=p), [("V", v_c)], inc=True)
                v_st[c] = cnt["V"]
            t_stats = cnt["T"]
            slotA = FA[0:1, 0:2048]
            slotB = FA[0:1, 2048:4096]
            emit("V", lambda e: e.tensor_scalar(slotA, PS[0:1, 0:2048], 1.0 / C, None, ALU.mult),
                 [("T", t_stats)])
            emit("V", lambda e: e.tensor_mul(slotB, slotA, slotA))
            v_var = emit("V", lambda e: e.scalar_tensor_tensor(
                slotB, PS[0:1, 2048:4096], 1.0 / C, slotB, ALU.mult, ALU.subtract),
                [("T", t_stats)], inc=True)
            emit("V", lambda e: e.tensor_scalar(slotB, slotB, EPS_LN, None, ALU.add))
            s_sd = emit("S", lambda e: e.activation(slotB, slotB, AF.Sqrt),
                        [("V", v_var)], inc=True)
            emit("V", lambda e: e.reciprocal(slotB, slotB), [("S", s_sd)])
            v_negm = emit("V", lambda e: e.tensor_scalar(
                slotA, slotA, -1.0, None, ALU.mult), inc=True)
            t_bc = 0
            for g in range(4):
                emit("T", lambda e, o=PS[:, 512 * g:512 * g + 512],
                     r=FA[0:1, 2048 + 512 * g:2048 + 512 * (g + 1)]:
                     e.matmul(o, ONER128, r, start=True, stop=True), [("V", v_negm)])
                t_bc = emit("T", lambda e, o=PS[:, 2048 + 512 * g:2048 + 512 * g + 512],
                            r=FA[0:1, 512 * g:512 * (g + 1)]:
                            e.matmul(o, ONER128, r, start=True, stop=True),
                            [("V", v_negm)], inc=(g == 3))
            v_ap = [0] * NCH
            for c in range(NCH):
                w = [("T", t_bc)]
                if c >= 2:
                    w.append(("V", v_ap[c - 2]))
                d_a = dma(FA[:, G0 + 2048 * (c % 2):G0 + 2048 * (c % 2) + 2048],
                          src_dram[:, :].rearrange("(a p) m -> a p m", a=NCH, p=128)[c],
                          waits=w)
                xs = FA[:, G0 + 2048 * (c % 2):G0 + 2048 * (c % 2) + 2048]
                emit("V", lambda e, x=xs: e.tensor_add(x, x, PS[:, 2048:4096]),
                     [("D", d_a), ("T", t_bc)])
                v_ap[c] = emit("V", lambda e, x=xs, o=BF[:, XH + 2048 * c:XH + 2048 * (c + 1)]:
                               e.tensor_mul(o, x, PS[:, 0:2048]), inc=True)
            return v_ap

        v_ap1 = ln_phase(xfm, True)
        if KPH < 2:
            finish()
            return nc

        d_c1 = dma(FA[:, 0:2048], cr1, waits=[("V", v_ap1[6])])
        d_c2 = dma(FA[:, 2048:4096], cr2, waits=[("V", v_ap1[7])])

        def prep_weight(wdram, wait_wt_free_t):
            extra_tr = [("T", wait_wt_free_t)] if wait_wt_free_t else []
            for ot in range(8):
                oap = BF[:, WT:WT + 8192].rearrange(
                    "p (b r) -> p b r", b=8, r=1024)[:, :, 128 * ot:128 * ot + 128]
                prep.push(wdram[128 * ot:128 * (ot + 1), :],
                          "cast", None, oap, extra_tr_waits=extra_tr)
            return prep.flush()

        def fm_proj(dst_off, bias_off, d_wt, extra_first_waits):
            v_rot = [0] * NCH
            s_qb = [0] * NCH
            t_oc = [0] * NCH
            for oc in range(NCH):
                pcol = (oc % 2) * 2048
                w = [("D", d_wt)]
                if oc == 0:
                    w += extra_first_waits
                if oc >= 2:
                    w.append(("S", s_qb[oc - 2]))
                for cc in range(NCH):
                    for g in range(4):
                        last = (cc == NCH - 1 and g == 3)
                        t_oc[oc] = emit("T", lambda e,
                                        o=PS[:, pcol + 512 * g:pcol + 512 * (g + 1)],
                                        l=BF[:, WT + 1024 * cc + 128 * oc:WT + 1024 * cc + 128 * oc + 128],
                                        r=BF[:, XH + 2048 * cc + 512 * g:XH + 2048 * cc + 512 * (g + 1)],
                                        s=(cc == 0), p=(cc == NCH - 1):
                                        e.matmul(o, l, r, start=s, stop=p),
                                        w, inc=last)
                    w = []
                qb = BF[:, SC:SC + 2048]
                qs = BF[:, SC + 2048:SC + 4096]
                wS = [("T", t_oc[oc])]
                if oc >= 1:
                    wS.append(("V", v_rot[oc - 1]))
                s_qb[oc] = emit("S", lambda e, o=qb, i=PS[:, pcol:pcol + 2048],
                                b=FA[:, bias_off + oc:bias_off + oc + 1]:
                                e.activation(o, i, AF.Identity, bias=b), wS, inc=True)
                emit("V", lambda e, o=qs, i=qb:
                     e.stream_shuffle(o, i, mask=[(j + 16) % 32 for j in range(32)]),
                     [("S", s_qb[oc])])
                emit("V", lambda e, o=qb, i=qb: e.tensor_mul(o, i, FA[:, 0:2048]),
                     [("D", d_c1)])
                emit("V", lambda e, o=qs, i=qs: e.tensor_mul(o, i, FA[:, 2048:4096]),
                     [("D", d_c2)])
                v_rot[oc] = emit("V", lambda e, o=BF[:, dst_off + 2048 * oc:dst_off + 2048 * (oc + 1)],
                                 a=qb, b=qs: e.tensor_add(o, a, b), inc=True)
            return t_oc[NCH - 1], v_rot, s_qb

        d_wkT = prep_weight(wk, 0)
        t_kproj, v_rotk, s_qbk = fm_proj(KK, BKo, d_wkT, [("V", v_ap1[7])])
        d_wvT = prep_weight(wv, t_kproj)
        t_vp = [0] * NTB
        v_vev = [0] * NTB
        for tb in range(NTB):
            pcol = 2048 + 1024 * (tb % 2)
            w = [("D", d_wvT)]
            if tb < 2:
                w.append(("S", s_qbk[NCH - 1]))
            if tb >= 2:
                w.append(("V", v_vev[tb - 2]))
            for cc in range(NCH):
                for g in range(2):
                    emit("T", lambda e, o=PS[:, pcol + 512 * g:pcol + 512 * (g + 1)],
                         l=BF[:, XH + 2048 * cc + 128 * tb:XH + 2048 * cc + 128 * tb + 128],
                         r=BF[:, WT + 1024 * cc + 512 * g:WT + 1024 * cc + 512 * (g + 1)],
                         s=(cc == 0): e.matmul(o, l, r, start=s, stop=False), w)
                w = []
            for g in range(2):
                t_vp[tb] = emit("T", lambda e, o=PS[:, pcol + 512 * g:pcol + 512 * (g + 1)],
                                l=ONER128, r=FA[0:1, BVo + 512 * g:BVo + 512 * (g + 1)]:
                                e.matmul(o, l, r, start=False, stop=True),
                                [("D", d_bv)], inc=(g == 1))
            oap = BF[:, VV + 1040 * tb:VV + 1040 * tb + 1040].rearrange(
                "p (h d) -> p h d", h=16, d=65)[:, :, 0:64]
            iap = PS[:, pcol:pcol + 1024].rearrange("p (h d) -> p h d", h=16, d=64)
            v_vev[tb] = emit("V", lambda e, o=oap, i=iap: e.tensor_copy(o, i),
                             [("T", t_vp[tb])], inc=True)
        d_wqT = prep_weight(wq, t_vp[NTB - 1])
        t_qproj, v_rotq, s_qbq = fm_proj(
            QQ, BQo, d_wqT, [("V", v_vev[NTB - 1]), ("S", s_qbk[NCH - 1])])

        if KPH < 3:
            finish()
            return nc
        d_wpT = prep_weight(wp, t_qproj)

        _vafter_preps = cnt["V"]
        gstate = {"gi": 0, "v_gprev": [_vafter_preps, _vafter_preps], "v_gred": 0}

        def gamma_unit():
            gi = gstate["gi"]
            if gi >= 64:
                return
            w = [("V", gstate["v_gprev"][gi % 2])]
            if gi < 64 - 32:
                iap = w1[128 * gi:128 * (gi + 1), :]
            else:
                tt = gi - 32
                iap = w2[:, :].rearrange("(a p) (b m) -> a b p m", p=128, m=1024)[tt // 4, tt % 4]
            d_g = dma(FA[:, WS + 1024 * (gi % 2):WS + 1024 * (gi % 2) + 1024], iap, waits=w)
            v = emit("V", lambda e, o=FA[:, GACCo + gi:GACCo + gi + 1],
                     i=FA[:, WS + 1024 * (gi % 2):WS + 1024 * (gi % 2) + 1024]:
                     e.tensor_reduce(o, i, AX.X, ALU.add, apply_absolute_value=True),
                     [("D", d_g)], inc=True)
            gstate["v_gprev"][gi % 2] = v
            gstate["v_gred"] = v
            gstate["gi"] = gi + 1

        pc = 0
        s_exp = {}
        t_pv_last = 0
        v_yev = [0] * H
        v_sum = [0] * H
        d_sumst = [0] * H
        pending_pv = [None]

        def flush_pv():
            nonlocal_t = pending_pv[0]
            if nonlocal_t is not None:
                nonlocal_t()
                pending_pv[0] = None

        for h in range(H):
            ch = h // 2
            pb = 64 * (h % 2)
            for _ in range(4):
                gamma_unit()
            for i in range(NTB):
                q0 = 128 * i
                Nq = T - q0
                pieces = [(q0, min(1024, Nq))]
                if Nq > 1024:
                    pieces.append((q0 + 1024, Nq - 1024))
                for pi, (qoff, plen) in enumerate(pieces):
                    par = pc % 2
                    stc = 1024 * par
                    wT_ = []
                    if pc >= 2:
                        wT_.append(("S", s_exp[pc - 2]))
                    if h == 0 and i == 0 and pi == 0:
                        wT_.append(("V", v_rotq[NCH - 1]))
                        wT_.append(("V", v_rotk[NCH - 1]))
                    t_kq = 0
                    ng = (plen + 511) // 512
                    for g in range(ng):
                        gw = min(512, plen - 512 * g)
                        diag = (pi == 0 and g == 0)
                        t_kq = emit("T", lambda e, o=PS[:, stc + 512 * g:stc + 512 * g + gw],
                                    l=BF[pb:pb + 64, KK + 2048 * ch + q0:KK + 2048 * ch + q0 + 128],
                                    r=BF[pb:pb + 64, QQ + 2048 * ch + qoff + 512 * g:
                                         QQ + 2048 * ch + qoff + 512 * g + gw],
                                    p=(not diag):
                                    e.matmul(o, l, r, start=True, stop=p),
                                    wT_, inc=(g == ng - 1 and not diag))
                        wT_ = []
                        if diag:
                            t_kq = emit("T", lambda e, o=PS[:, stc:stc + 128],
                                        l=BF[:, SP_I:SP_I + 128],
                                        r=BF[:, SP_I + 128:SP_I + 256]:
                                        e.matmul(o, l, r, start=False, stop=True),
                                        [("D", d_cbf)], inc=(g == ng - 1))
                    s_exp[pc] = emit("S", lambda e, o=BF[:, TPc + 1024 * par:TPc + 1024 * par + plen],
                                     i=PS[:, stc:stc + plen]:
                                     e.activation(o, i, AF.Exp, scale=0.125),
                                     [("T", t_kq)], inc=True)

                    def make_pv(pc_=pc, par_=par, qoff_=qoff, plen_=plen, i_=i, h_=h, pi_=pi):
                        def do():
                            nonlocal t_pv_last
                            wPV = [("S", s_exp[pc_])]
                            if i_ == 0 and pi_ == 0 and h_ > 0:
                                wPV.append(("V", v_sum[h_ - 1]))
                            off = 0
                            while off < plen_:
                                gw = min(512 - ((qoff_ + off) % 512), plen_ - off)
                                if pi_ == 0 and off == 0:
                                    gw = min(gw, 128)
                                stp = (pi_ == 0 and off == 0 and (i_ % 4) == 3)
                                st_ = (i_ == 0 and ((qoff_ + off) % 512) == 0)
                                t_pv_last = emit("T", lambda e,
                                                 o=PS[0:65, 2048 + qoff_ + off:2048 + qoff_ + off + gw],
                                                 l=BF[:, VV + 1040 * i_ + 65 * h_:VV + 1040 * i_ + 65 * h_ + 65],
                                                 r=BF[:, TPc + 1024 * par_ + off:TPc + 1024 * par_ + off + gw],
                                                 s=st_, p=stp:
                                                 e.matmul(o, l, r, start=s, stop=p),
                                                 wPV, inc=(off + gw >= plen_))
                                wPV = []
                                off += gw
                        return do

                    flush_pv()
                    pending_pv[0] = make_pv()
                    pc += 1
            flush_pv()
            wY = [("T", t_pv_last)]
            if h == 0:
                wY.append(("T", t_qproj))
            v_yev[h] = emit("V", lambda e, o=BF[pb:pb + 64, XH + 2048 * ch:XH + 2048 * (ch + 1)],
                            i=PS[0:64, 2048:4096]: e.tensor_copy(o, i), wY, inc=True)
            wSm = []
            if h >= 1:
                wSm.append(("D", d_sumst[h - 1]))
            v_sum[h] = emit("V", lambda e, o=FA[0:1, 0:2048], i=PS[64:65, 2048:4096]:
                            e.tensor_copy(o, i), wSm, inc=True)
            d_sumst[h] = dma(sums_d[h:h + 1, :], FA[0:1, 0:2048], waits=[("V", v_sum[h])])

        d_sumld = dma(FA[0:16, 0:2048], sums_d, waits=[("D", d_sumst[H - 1])])
        v_inv = emit("V", lambda e: e.reciprocal(FA[0:16, 0:2048], FA[0:16, 0:2048]),
                     [("D", d_sumld)], inc=True)
        t_selbc = [0] * NCH
        v_ynorm = [0] * NCH
        for c2 in range(NCH):
            pcol = (c2 % 2) * 2048
            w = [("V", v_inv), ("D", d_const)]
            if c2 >= 2:
                w.append(("V", v_ynorm[c2 - 2]))
            for g in range(4):
                t_selbc[c2] = emit("T", lambda e, o=PS[:, pcol + 512 * g:pcol + 512 * (g + 1)],
                                   l=FA[0:16, SELo + 128 * c2:SELo + 128 * (c2 + 1)],
                                   r=FA[0:16, 512 * g:512 * (g + 1)]:
                                   e.matmul(o, l, r, start=True, stop=True), w, inc=(g == 3))
                w = []
            v_ynorm[c2] = emit("V", lambda e, o=BF[:, XH + 2048 * c2:XH + 2048 * (c2 + 1)],
                               i=PS[:, pcol:pcol + 2048]:
                               e.tensor_mul(o, o, i),
                               [("T", t_selbc[c2]), ("V", v_yev[2 * c2 + 1])], inc=True)

        if KPH < 4:
            finish()
            return nc
        while gstate["gi"] < 64:
            gamma_unit()
        emit("V", lambda e: e.tensor_reduce(FA[:, GT:GT + 1], FA[:, GACCo:GACCo + 32], AX.X, ALU.add),
             [("V", gstate["v_gred"])])
        v_g2 = emit("V", lambda e: e.tensor_reduce(FA[:, GT + 1:GT + 2], FA[:, GACCo + 32:GACCo + 64],
                                                   AX.X, ALU.add), inc=True)
        emit("T", lambda e: e.matmul(PS[0:1, 0:1], FA[:, ONECOLo:ONECOLo + 1], FA[:, GT:GT + 1],
                                     start=True, stop=True),
             [("V", v_g2), ("V", v_ynorm[NCH - 1])])
        t_gm = emit("T", lambda e: e.matmul(PS[0:1, 1:2], FA[:, ONECOLo:ONECOLo + 1],
                                            FA[:, GT + 1:GT + 2], start=True, stop=True),
                    [("V", v_g2)], inc=True)
        emit("V", lambda e: e.tensor_scalar(FA[0:1, GT + 4:GT + 5], PS[0:1, 0:1],
                                            1.0 / (FF * C), EPS_Q, ALU.mult, ALU.add),
             [("T", t_gm)])
        emit("V", lambda e: e.tensor_scalar(FA[0:1, GT + 5:GT + 6], PS[0:1, 1:2],
                                            1.0 / (FF * C), EPS_Q, ALU.mult, ALU.add))
        emit("V", lambda e: e.reciprocal(FA[0:1, GT + 4:GT + 5], FA[0:1, GT + 4:GT + 5]))
        v_gi = emit("V", lambda e: e.reciprocal(FA[0:1, GT + 5:GT + 6], FA[0:1, GT + 5:GT + 6]),
                    inc=True)
        emit("T", lambda e: e.matmul(PS[:, 4:5], ONER128, FA[0:1, GT + 4:GT + 5],
                                     start=True, stop=True), [("V", v_gi)])
        t_gb = emit("T", lambda e: e.matmul(PS[:, 5:6], ONER128, FA[0:1, GT + 5:GT + 6],
                                            start=True, stop=True), inc=True)
        emit("V", lambda e: e.tensor_copy(FA[:, GT + 2:GT + 3], PS[:, 4:5]), [("T", t_gb)])
        v_sg = emit("V", lambda e: e.tensor_copy(FA[:, GT + 3:GT + 4], PS[:, 5:6]), inc=True)
        SG1 = FA[:, GT + 2:GT + 3]
        SG2 = FA[:, GT + 3:GT + 4]

        t_pp = [0] * NCH
        v_res = [0] * NCH
        d_x1w = [0] * NCH
        for oc in range(NCH):
            pcol = (oc % 2) * 2048
            w = [("D", d_wpT), ("V", v_sg), ("V", v_ynorm[NCH - 1])]
            if oc >= 2:
                w.append(("V", v_res[oc - 2]))
            for cc in range(NCH):
                for g in range(4):
                    t_pp[oc] = emit("T", lambda e, o=PS[:, pcol + 512 * g:pcol + 512 * (g + 1)],
                                    l=BF[:, WT + 1024 * cc + 128 * oc:WT + 1024 * cc + 128 * oc + 128],
                                    r=BF[:, XH + 2048 * cc + 512 * g:XH + 2048 * cc + 512 * (g + 1)],
                                    s=(cc == 0), p=(cc == NCH - 1): e.matmul(o, l, r, start=s, stop=p),
                                    w, inc=(cc == NCH - 1 and g == 3))
                    w = []
            wD = [("T", t_selbc[NCH - 1]), ("V", v_ynorm[NCH - 1])]
            if oc >= 2:
                wD.append(("V", v_res[oc - 2]))
            d_xr = dma(FA[:, G0 + 2048 * (oc % 2):G0 + 2048 * (oc % 2) + 2048],
                       xfm[:, :].rearrange("(a p) m -> a p m", a=NCH, p=128)[oc], waits=wD)
            xs = FA[:, G0 + 2048 * (oc % 2):G0 + 2048 * (oc % 2) + 2048]
            v_res[oc] = emit("V", lambda e, x=xs, i=PS[:, pcol:pcol + 2048],
                             b=FA[:, BPo + oc:BPo + oc + 1]:
                             e.scalar_tensor_tensor(x, i, b, x, ALU.add, ALU.add),
                             [("T", t_pp[oc]), ("D", d_xr)], inc=True)
            d_x1w[oc] = dma(x1d[:, :].rearrange("(a p) m -> a p m", a=NCH, p=128)[oc], xs,
                            waits=[("V", v_res[oc])])

        ln2_gate_t = t_pp[NCH - 1]
        ln2_gate_v = v_res[NCH - 1]
        v_ap2 = None

        def ln_phase2():
            d_x = [0] * NCH
            v_st = [0] * NCH
            for c in range(NCH):
                w = [("D", d_x1w[c])]
                if c >= 2:
                    w.append(("V", v_st[c - 2]))
                d_x[c] = dma(FA[:, G0 + 2048 * (c % 2):G0 + 2048 * (c % 2) + 2048],
                             x1d[:, :].rearrange("(a p) m -> a p m", a=NCH, p=128)[c], waits=w)
                for g in range(4):
                    xsrc = FA[:, G0 + 2048 * (c % 2) + 512 * g:G0 + 2048 * (c % 2) + 512 * (g + 1)]
                    exb = BF[:, SP_EX:SP_EX + 512]
                    exq = BF[:, SP_EX + 512:SP_EX + 1024]
                    wv_ = [("D", d_x[c]), ("T", ln2_gate_t)]
                    if c > 0 or g > 0:
                        wv_.append(("T", cnt["T"]))
                    emit("V", lambda e, o=exb, i=xsrc: e.tensor_copy(o, i), wv_)
                    v_c = emit("V", lambda e, o=exq, i=exb: e.tensor_mul(o, i, i), inc=True)
                    emit("T", lambda e, o=PS[0:1, 512 * g:512 * g + 512], l=ONES_BF, r=exb,
                         s=(c == 0), p=(c == NCH - 1):
                         e.matmul(o, l, r, start=s, stop=p), [("V", v_c), ("V", ln2_gate_v)])
                    emit("T", lambda e, o=PS[0:1, 2048 + 512 * g:2048 + 512 * g + 512],
                         l=ONES_BF, r=exq, s=(c == 0), p=(c == NCH - 1):
                         e.matmul(o, l, r, start=s, stop=p), [("V", v_c)], inc=True)
                v_st[c] = cnt["V"]
            t_stats = cnt["T"]
            slotA = FA[0:1, 0:2048]
            slotB = FA[0:1, 2048:4096]
            emit("V", lambda e: e.tensor_scalar(slotA, PS[0:1, 0:2048], 1.0 / C, None, ALU.mult),
                 [("T", t_stats)])
            emit("V", lambda e: e.tensor_mul(slotB, slotA, slotA))
            v_var = emit("V", lambda e: e.scalar_tensor_tensor(
                slotB, PS[0:1, 2048:4096], 1.0 / C, slotB, ALU.mult, ALU.subtract),
                [("T", t_stats)], inc=True)
            emit("V", lambda e: e.tensor_scalar(slotB, slotB, EPS_LN, None, ALU.add))
            s_sd = emit("S", lambda e: e.activation(slotB, slotB, AF.Sqrt),
                        [("V", v_var)], inc=True)
            emit("V", lambda e: e.reciprocal(slotB, slotB), [("S", s_sd)])
            v_negm = emit("V", lambda e: e.tensor_scalar(
                slotA, slotA, -1.0, None, ALU.mult), inc=True)
            t_bc = 0
            for g in range(4):
                emit("T", lambda e, o=PS[:, 512 * g:512 * g + 512],
                     r=FA[0:1, 2048 + 512 * g:2048 + 512 * (g + 1)]:
                     e.matmul(o, ONER128, r, start=True, stop=True), [("V", v_negm)])
                t_bc = emit("T", lambda e, o=PS[:, 2048 + 512 * g:2048 + 512 * g + 512],
                            r=FA[0:1, 512 * g:512 * (g + 1)]:
                            e.matmul(o, ONER128, r, start=True, stop=True),
                            [("V", v_negm)], inc=(g == 3))
            v_ap = [0] * NCH
            for c in range(NCH):
                w = [("T", t_bc)]
                if c >= 2:
                    w.append(("V", v_ap[c - 2]))
                d_a = dma(FA[:, G0 + 2048 * (c % 2):G0 + 2048 * (c % 2) + 2048],
                          x1d[:, :].rearrange("(a p) m -> a p m", a=NCH, p=128)[c], waits=w)
                xs = FA[:, G0 + 2048 * (c % 2):G0 + 2048 * (c % 2) + 2048]
                emit("V", lambda e, x=xs: e.tensor_add(x, x, PS[:, 2048:4096]),
                     [("D", d_a), ("T", t_bc)])
                v_ap[c] = emit("V", lambda e, x=xs, o=BF[:, XH + 2048 * c:XH + 2048 * (c + 1)]:
                               e.tensor_mul(o, x, PS[:, 0:2048]), inc=True)
            return v_ap

        v_ap2 = ln_phase2()

        d_w2rdy = [0] * 32
        d_w1rdy = [0] * 32

        def mk_rdy2(sl):
            def f(d):
                for b_ in range(8):
                    d_w2rdy[8 * sl + b_] = d
            return f

        def mk_rdy1(fc):
            def f(d):
                d_w1rdy[fc] = d
            return f

        first_w2 = [("V", gstate["v_gred"])]
        for sl in range(4):
            for dt in range(8):
                base = w2base(8 * sl)
                oap = BF[:, base:base + 8192].rearrange(
                    "p (b r) -> p b r", b=8, r=1024)[:, :, 128 * dt:128 * dt + 128]
                ex = [("T", t_pv_last)]
                if sl == 2:
                    ex.append(("T", t_pp[NCH - 1]))
                prep.push(w2[128 * dt:128 * (dt + 1), 1024 * sl:1024 * (sl + 1)],
                          "quant", SG2, oap, extra_tr_waits=ex,
                          extra_ld_waits=first_w2,
                          on_ready=mk_rdy2(sl) if dt == 7 else None)
                first_w2 = []
            for fc in range(8 * sl, 8 * sl + 8):
                oap = BF[:, QQ:QQ + 32768].rearrange(
                    "p (b r) -> p b r", b=8, r=4096)[:, :, 128 * fc:128 * fc + 128]
                prep.push(w1[128 * fc:128 * (fc + 1), :],
                          "quant", SG1, oap, extra_tr_waits=[("T", t_pv_last)],
                          on_ready=mk_rdy1(fc))
        prep.flush()

        t_mm1 = {}
        s_relu = {}
        t_mm2 = {}
        v_out = [0] * 8
        d_out = [0] * 8
        pending_mm2 = [None]

        def flush_mm2():
            if pending_mm2[0] is not None:
                pending_mm2[0]()
                pending_mm2[0] = None

        for tp in range(8):
            w = [("T", t_selbc[NCH - 1])]
            if tp >= 2:
                w.append(("V", v_out[tp - 2]))
            if tp < 2:
                w.append(("V", v_ap2[NCH - 1]))
                w.append(("D", d_x1w[NCH - 1]))
            d_x1p = dma(FA[:, G0 + 2048 * (tp % 2):G0 + 2048 * (tp % 2) + 2048].rearrange(
                "p (a m) -> p a m", a=8, m=256),
                x1d[:, :].rearrange("(a p) m -> p a m", a=NCH, p=128)[:, :, 256 * tp:256 * (tp + 1)],
                waits=w)
            for fc in range(32):
                hcol = 512 * (fc % 2)
                wmm1 = [("D", d_w1rdy[fc]), ("V", v_ap2[NCH - 1])]
                prev = (tp, fc - 2) if fc >= 2 else ((tp - 1, 30 + fc) if tp >= 1 else None)
                if prev in s_relu:
                    wmm1.append(("S", s_relu[prev]))
                for cc in range(NCH):
                    t_mm1[(tp, fc)] = emit("T", lambda e, o=PS[:, hcol:hcol + 256],
                                           l=BF[:, QQ + 4096 * cc + 128 * fc:QQ + 4096 * cc + 128 * fc + 128],
                                           r=BF[:, XH + 2048 * cc + 256 * tp:XH + 2048 * cc + 256 * (tp + 1)],
                                           s=(cc == 0), p=(cc == NCH - 1): e.matmul(o, l, r, start=s, stop=p),
                                           wmm1, inc=(cc == NCH - 1))
                    wmm1 = []
                s_relu[(tp, fc)] = emit("S", lambda e, o=BF[:, SP_H + 256 * (fc % 2):SP_H + 256 * (fc % 2) + 256],
                                        i=PS[:, hcol:hcol + 256], b=FA[:, B1o + fc:B1o + fc + 1]:
                                        e.activation(o, i, AF.Relu, bias=b),
                                        [("T", t_mm1[(tp, fc)])], inc=True)
                def make_mm2(tp_=tp, fc_=fc):
                    def do():
                        wmm2 = [("S", s_relu[(tp_, fc_)]), ("D", d_w2rdy[fc_])]
                        if fc_ == 0 and tp_ >= 1:
                            wmm2.append(("V", v_out[tp_ - 1]))
                        for dc in range(NCH):
                            st_ = (fc_ == 0 and dc % 2 == 0)
                            t_mm2[(tp_, fc_)] = emit(
                                "T", lambda e, o=PS[:, 2048 + 256 * dc:2048 + 256 * (dc + 1)],
                                l=BF[:, w2base(fc_) + 128 * dc:w2base(fc_) + 128 * dc + 128],
                                r=BF[:, SP_H + 256 * (fc_ % 2):SP_H + 256 * (fc_ % 2) + 256],
                                s=st_, p=(fc_ == 31 and dc % 2 == 1):
                                e.matmul(o, l, r, start=s, stop=p),
                                wmm2, inc=(dc == NCH - 1))
                            wmm2[:] = []
                    return do

                flush_mm2()
                pending_mm2[0] = make_mm2()
            flush_mm2()
            for dc in range(NCH):
                xs = FA[:, G0 + 2048 * (tp % 2) + 256 * dc:G0 + 2048 * (tp % 2) + 256 * (dc + 1)]
                v_out[tp] = emit("V", lambda e, x=xs, i=PS[:, 2048 + 256 * dc:2048 + 256 * (dc + 1)],
                                 b=FA[:, B2o + dc:B2o + dc + 1]:
                                 e.scalar_tensor_tensor(x, i, b, x, ALU.add, ALU.add),
                                 [("T", t_mm2[(tp, 31)]), ("D", d_x1p)], inc=(dc == NCH - 1))
            d_out[tp] = dma(out_fm[:, :].rearrange("(a p) m -> p a m", a=NCH, p=128)[
                :, :, 256 * tp:256 * (tp + 1)],
                FA[:, G0 + 2048 * (tp % 2):G0 + 2048 * (tp % 2) + 2048].rearrange(
                    "p (a m) -> p a m", a=8, m=256),
                waits=[("V", v_out[tp])])

        finish()
    return nc


def _perm():
    perm = np.zeros(C, dtype=np.int64)
    for g in range(C):
        chk, p = g // 128, g % 128
        h = 2 * chk + p // 64
        lp = p % 64
        quad = lp // 32
        r = lp % 32
        if r < 16:
            i_idx, par = r + 16 * quad, 0
        else:
            i_idx, par = (r - 16) + 16 * quad, 1
        perm[g] = 64 * h + 2 * i_idx + par
    return perm


def _rope_tables():
    i = np.arange(32, dtype=np.float32)
    theta = (np.float32(1.0) / (np.float32(10000.0) ** (np.float32(2.0) * i / np.float32(64)))).astype(np.float32)
    t = np.arange(T, dtype=np.float32)[:, None]
    ang = (t * theta[None, :]).astype(np.float32)
    cos = np.cos(ang.astype(np.float64)).astype(np.float32)
    sin = np.sin(ang.astype(np.float64)).astype(np.float32)
    C1 = np.zeros((128, T), np.float32)
    C2 = np.zeros((128, T), np.float32)
    for p in range(128):
        lp = p % 64
        quad = lp // 32
        r = lp % 32
        if r < 16:
            i_idx, is_even = r + 16 * quad, True
        else:
            i_idx, is_even = (r - 16) + 16 * quad, False
        C1[p] = cos[:, i_idx]
        C2[p] = (-sin[:, i_idx]) if is_even else sin[:, i_idx]
    return C1, C2


def _consts_pack(b1, b2, bq_p, bk_p, bp):
    A = np.zeros((128, CONST_COLS), np.float32)
    for c2 in range(NCH):
        for j in range(128):
            h = 2 * c2 + (1 if j >= 64 else 0)
            A[h, 128 * c2 + j] = 1.0
    for p in range(128):
        A[p, 1024:1024 + p] = NEG
    A[:, 1152:1184] = b1.reshape(32, 128).T
    A[:, 1184:1192] = b2.reshape(8, 128).T
    A[:, 1192:1200] = bq_p.reshape(8, 128).T
    A[:, 1200:1208] = bk_p.reshape(8, 128).T
    A[:, 1208:1216] = bp.reshape(8, 128).T
    return A


_NC_CACHE = {}


def kernel(L0_attn_x, L0_attn_wq, L0_attn_bq, L0_attn_wk, L0_attn_bk,
           L0_attn_wv, L0_attn_bv, L0_attn_wp, L0_attn_bp,
           L0_ffn_W1, L0_ffn_b1, L0_ffn_W2, L0_ffn_b2):
    x = np.asarray(L0_attn_x, dtype=np.float32)
    perm = _perm()
    wq_p = np.ascontiguousarray(np.asarray(L0_attn_wq, np.float32)[perm])
    wk_p = np.ascontiguousarray(np.asarray(L0_attn_wk, np.float32)[perm])
    bq_p = np.ascontiguousarray(np.asarray(L0_attn_bq, np.float32)[perm])
    bk_p = np.ascontiguousarray(np.asarray(L0_attn_bk, np.float32)[perm])
    wv = np.ascontiguousarray(np.asarray(L0_attn_wv, np.float32))
    wp = np.ascontiguousarray(np.asarray(L0_attn_wp, np.float32))
    w1 = np.ascontiguousarray(np.asarray(L0_ffn_W1, np.float32))
    w2 = np.ascontiguousarray(np.asarray(L0_ffn_W2, np.float32))
    C1, C2 = _rope_tables()
    consts = _consts_pack(np.asarray(L0_ffn_b1, np.float32), np.asarray(L0_ffn_b2, np.float32),
                          bq_p, bk_p, np.asarray(L0_attn_bp, np.float32))
    bvr = np.ascontiguousarray(np.asarray(L0_attn_bv, np.float32).reshape(1, C))
    cbf_host = np.zeros((128, 256), dtype=ml_dtypes.bfloat16)
    cbf_host[:, 0:128] = np.eye(128, dtype=np.float32).astype(ml_dtypes.bfloat16)
    mtri = np.zeros((128, 128), np.float32)
    for p in range(128):
        mtri[p, :p] = NEG
    cbf_host[:, 128:256] = mtri.astype(ml_dtypes.bfloat16)

    if "nc" not in _NC_CACHE:
        _NC_CACHE["nc"] = build_nc()
    nc = _NC_CACHE["nc"]

    in_maps = []
    for b in range(8):
        in_maps.append({
            "xfm": np.ascontiguousarray(x[b].T),
            "wq": wq_p, "wk": wk_p, "wv": wv, "wp": wp,
            "w1": w1, "w2": w2,
            "cr1": C1, "cr2": C2, "consts": consts, "bvr": bvr, "cbf": cbf_host,
        })
    res = run_bass_kernel_spmd(nc, in_maps, core_ids=list(range(8)))
    out = np.stack([np.asarray(r["out_fm"]).T for r in res.results], axis=0)
    return out.astype(np.float32)


# version 13
# speedup vs baseline: 1.7517x; 1.0426x over previous
import os
import numpy as np
import ml_dtypes
from contextlib import ExitStack

import concourse.bass as bass
import concourse.mybir as mybir
from concourse.bass_utils import run_bass_kernel_spmd

F32 = mybir.dt.float32
BF16 = mybir.dt.bfloat16
AF = mybir.ActivationFunctionType
ALU = mybir.AluOpType
AX = mybir.AxisListType

T = 2048
C = 1024
H = 16
DH = 64
FF = 4096
NCH = 8
NTB = 16
NEG = -1.0e10
EPS_LN = 1e-5
EPS_Q = 1e-5

XH = 0
QQ = 16384
KK = 32768
VV = 49152
WT = 65792
SC = 73984
WB = 78080
TPc = 80128
SP = 82176
SP_H = SP
SP_PREP = SP + 512
SP_ONE = SP + 1536
SP_EX = SP + 1552
SP_I = SP + 2576
SP_T2 = SP + 2832
NBF = SP + 2832 + 1024

def w2base(fc):
    if fc < 16:
        return VV + 1024 * fc
    if fc < 24:
        return WT + 1024 * (fc - 16)
    if fc < 28:
        return SC + 1024 * (fc - 24)
    return WB + 1024 * (fc - 28)

G0 = 0
WS = 4096
SELo = 6144
MTRIo = 7168
B1o = 7296
B2o = 7328
BQo = 7336
BKo = 7344
BPo = 7352
GACCo = 7360
ONECOLo = 7424
GT = 7425
BVo = 7456
ONERo = 8480
NFA = 8992

CONST_COLS = 1232


_SCHED = None


def _semname(sm):
    return getattr(sm, "name", None) if sm is not None else None


def build_nc():
    nc = bass.Bass("TRN2", target_bir_lowering=False, detect_race_conditions=False)
    xfm = nc.dram_tensor("xfm", [C, T], F32, kind="ExternalInput").ap()
    wq = nc.dram_tensor("wq", [C, C], F32, kind="ExternalInput").ap()
    wk = nc.dram_tensor("wk", [C, C], F32, kind="ExternalInput").ap()
    wv = nc.dram_tensor("wv", [C, C], F32, kind="ExternalInput").ap()
    wp = nc.dram_tensor("wp", [C, C], F32, kind="ExternalInput").ap()
    w1 = nc.dram_tensor("w1", [FF, C], F32, kind="ExternalInput").ap()
    w2 = nc.dram_tensor("w2", [C, FF], F32, kind="ExternalInput").ap()
    cr1 = nc.dram_tensor("cr1", [128, T], F32, kind="ExternalInput").ap()
    cr2 = nc.dram_tensor("cr2", [128, T], F32, kind="ExternalInput").ap()
    consts = nc.dram_tensor("consts", [128, CONST_COLS], F32, kind="ExternalInput").ap()
    bvr = nc.dram_tensor("bvr", [1, C], F32, kind="ExternalInput").ap()
    cbf = nc.dram_tensor("cbf", [128, 256], BF16, kind="ExternalInput").ap()
    out_fm = nc.dram_tensor("out_fm", [C, T], F32, kind="ExternalOutput").ap()
    x1d = nc.dram_tensor("x1d", [C, T], F32, kind="Internal").ap()
    sums_d = nc.dram_tensor("sums_d", [H, T], F32, kind="Internal").ap()

    with ExitStack() as st:
        BFt = st.enter_context(nc.sbuf_tensor("BF", [128, NBF], BF16))
        FAt = st.enter_context(nc.sbuf_tensor("FA", [128, NFA], F32))
        PSt = st.enter_context(nc.psum_tensor("PS", [128, 4096], F32))
        sD = st.enter_context(nc.semaphore("sD"))
        sT = st.enter_context(nc.semaphore("sT"))
        sV = st.enter_context(nc.semaphore("sV"))
        sS = st.enter_context(nc.semaphore("sS"))
        sG = st.enter_context(nc.semaphore("sG"))
        sA = st.enter_context(nc.semaphore("sA"))
        block = st.enter_context(nc.Block())

        BF = BFt.ap()
        FA = FAt.ap()
        PS = PSt.ap()

        ops = {"sync": [], "tensor": [], "vector": [], "scalar": [], "gpsimd": []}
        cnt = {"D": 0, "T": 0, "V": 0, "S": 0, "G": 0, "A": 0}
        sem = {"D": sD, "T": sT, "V": sV, "S": sS, "G": sG, "A": sA}
        seen = {e: {} for e in ops}
        ENG_OF = {"D": "sync", "T": "tensor", "V": "vector", "S": "scalar",
                  "G": "gpsimd", "A": "scalar"}

        def emit(ch, fn, waits=(), inc=False):
            eng = ENG_OF[ch]
            w2 = []
            for (sname, v) in waits:
                if v <= 0:
                    continue
                if sname == ch and ch not in ("D", "A"):
                    continue

                if v > seen[eng].get(sname, 0):
                    seen[eng][sname] = v
                    w2.append((sname, v))
            amt = 0
            if inc:
                amt = 16 if ch in ("D", "A") else 1
                cnt[ch] += amt
            ops[eng].append((fn, w2, sem[ch] if inc else None, amt))
            return cnt[ch]

        last_tr = [0]
        TABW = [0]

        def dma(out_ap, in_ap, waits=(), transpose=False, ch="D"):
            w = list(waits)
            if transpose:
                w.append(("D", cnt["D"]))
                w.append(("A", cnt["A"]))
            elif last_tr[0] > 0:
                w.append(("D", last_tr[0]))
            d = emit(
                ch,
                lambda e, o=out_ap, i=in_ap, tr=transpose: e.dma_start(out=o, in_=i, transpose=tr),
                w, inc=True)
            if transpose:
                last_tr[0] = d
            return d

        def bf3(off, span, b, r):
            return BF[:, off:off + span].rearrange("p (b r) -> p b r", b=b, r=r)

        PREP_SLOT = [SP_PREP, SP_EX]

        class Prep:
            def __init__(self):
                self.qi = 0
                self.v_q = {}
                self.d_tr = {}
                self.onready = {}
                self.pending = []

            def _emit_load(self, src_ap, extra_ld_waits):
                qi = self.qi + len(self.pending)
                par = qi % 2
                w = list(extra_ld_waits)
                if qi >= 2:
                    w.append(("V", self.v_q[qi - 2]))
                d_ld = dma(FA[:, WS + 1024 * par:WS + 1024 * par + 1024], src_ap, waits=w)
                return d_ld, par

            def push(self, src_ap, kind, sg, oap, extra_tr_waits=(), extra_ld_waits=(),
                     on_ready=None):
                uidx = self.qi + len(self.pending)
                if on_ready is not None:
                    self.onready[uidx] = on_ready
                d_ld, par = self._emit_load(src_ap, extra_ld_waits)
                self.pending.append((d_ld, par, kind, sg, oap, list(extra_tr_waits)))
                if len(self.pending) > 1:
                    self._pop()

            def _pop(self):
                d_ld, par, kind, sg, oap, extra_tr = self.pending.pop(0)
                qi = self.qi
                src = FA[:, WS + 1024 * par:WS + 1024 * par + 1024]
                t1 = BF[:, PREP_SLOT[par]:PREP_SLOT[par] + 1024]
                t2 = BF[:, SP_T2:SP_T2 + 1024]
                wq_ = [("D", d_ld)]
                if qi >= 2:
                    wq_.append(("D", self.d_tr[qi - 2]))
                if kind == "cast":
                    self.v_q[qi] = emit("V", lambda e, o=t1, i=src: e.tensor_copy(o, i),
                                        wq_, inc=True)
                else:
                    emit("V", lambda e, o=t1, i=src, g=sg:
                         e.tensor_scalar(o, i, g, 0.5, ALU.mult, ALU.is_ge), wq_)
                    emit("V", lambda e, o=t2, i=src, g=sg:
                         e.tensor_scalar(o, i, g, -0.5, ALU.mult, ALU.is_le))
                    self.v_q[qi] = emit("V", lambda e, o=t1, a=t1, b=t2:
                                        e.tensor_sub(o, a, b), inc=True)
                d_tr = dma(oap, t1, waits=[("V", self.v_q[qi]), ("S", TABW[0])] + extra_tr,
                           transpose=True)
                self.d_tr[qi] = d_tr
                if qi in self.onready:
                    self.onready.pop(qi)(d_tr)
                self.qi += 1
                return d_tr

            def flush(self):
                d = 0
                while self.pending:
                    d = self._pop()
                return d

        prep = Prep()

        KPH = int(os.environ.get("KPH", "9"))

        def finish():
            final_d = cnt["D"]
            ops["sync"].append((lambda e: None, [("D", final_d)], None, 0))
            import kernel as _km
            _km._SCHED = {"ops": {e: [(w, _semname(sm), amt)
                                      for (fn, w, sm, amt) in ops[e]] for e in ops}}

            def emit_block(eng_name, e):
                for (fn, waits, s_, amt) in ops[eng_name]:
                    for (sname, v) in waits:
                        e.wait_ge(sem[sname], v)
                    inst = fn(e)
                    if s_ is not None:
                        inst.then_inc(s_, amt)

            @block.sync
            def _(e):
                emit_block("sync", e)

            @block.tensor
            def _(e):
                emit_block("tensor", e)

            @block.vector
            def _(e):
                emit_block("vector", e)

            @block.scalar
            def _(e):
                emit_block("scalar", e)

            @block.gpsimd
            def _(e):
                emit_block("gpsimd", e)

        d_const = dma(FA[:, SELo:SELo + CONST_COLS], consts)
        d_bv = dma(FA[0:1, BVo:BVo + C], bvr)
        d_cbf = dma(BF[:, SP_I:SP_I + 256], cbf)
        v_ones = emit("V", lambda e: e.memset(FA[0:1, ONERo:ONERo + 512], 1.0), inc=True)
        v_ones = emit("V", lambda e: e.memset(FA[:, ONECOLo:ONECOLo + 1], 1.0), inc=True)
        v_ones = emit("V", lambda e: e.memset(BF[:, SP_ONE:SP_ONE + 16], 1.0), inc=True)
        vone_ap = BF[:, VV:VV + 16640].rearrange("p (t h d) -> p t h d", t=16, h=16, d=65)[:, :, :, 64:65]
        v_ones = emit("V", lambda e: e.memset(vone_ap, 1.0), inc=True)

        s_tab = emit("S", lambda e: e.activation(FA[0:1, GT + 6:GT + 7],
                                                 FA[0:1, ONECOLo:ONECOLo + 1], AF.Ln),
                     [("V", v_ones)], inc=True)
        TABW[0] = s_tab

        ONES_BF = BF[:, SP_ONE:SP_ONE + 1]
        ONER128 = FA[0:1, ONERo:ONERo + 128]

        def ln_phase(src_dram, first, hook=None):
            t_before = cnt["T"]
            d_x = [0] * NCH
            v_st = [0] * NCH
            for c in range(NCH):
                w = []
                if c >= 2:
                    w.append(("V", v_st[c - 2]))
                d_x[c] = dma(FA[:, G0 + 2048 * (c % 2):G0 + 2048 * (c % 2) + 2048],
                             src_dram[:, :].rearrange("(a p) m -> a p m", a=NCH, p=128)[c],
                             waits=w)
                for g in range(4):
                    xsrc = FA[:, G0 + 2048 * (c % 2) + 512 * g:G0 + 2048 * (c % 2) + 512 * (g + 1)]
                    exb = BF[:, SP_EX:SP_EX + 512]
                    exq = BF[:, SP_EX + 512:SP_EX + 1024]
                    wv_ = [("D", d_x[c])]
                    if c > 0 or g > 0:
                        wv_.append(("T", cnt["T"]))
                    emit("V", lambda e, o=exb, i=xsrc: e.tensor_copy(o, i), wv_)
                    v_c = emit("V", lambda e, o=exq, i=exb: e.tensor_mul(o, i, i), inc=True)
                    emit("T", lambda e, o=PS[0:1, 512 * g:512 * g + 512], l=ONES_BF, r=exb,
                         s=(c == 0), p=(c == NCH - 1):
                         e.matmul(o, l, r, start=s, stop=p), [("V", v_c)])
                    t_c = emit("T", lambda e, o=PS[0:1, 2048 + 512 * g:2048 + 512 * g + 512],
                               l=ONES_BF, r=exq, s=(c == 0), p=(c == NCH - 1):
                               e.matmul(o, l, r, start=s, stop=p), [("V", v_c)], inc=True)
                v_st[c] = cnt["V"]
            t_stats = cnt["T"]
            slotA = FA[0:1, 0:2048]
            slotB = FA[0:1, 2048:4096]
            emit("V", lambda e: e.tensor_scalar(slotA, PS[0:1, 0:2048], 1.0 / C, None, ALU.mult),
                 [("T", t_stats)])
            emit("V", lambda e: e.tensor_mul(slotB, slotA, slotA))
            v_var = emit("V", lambda e: e.scalar_tensor_tensor(
                slotB, PS[0:1, 2048:4096], 1.0 / C, slotB, ALU.mult, ALU.subtract),
                [("T", t_stats)], inc=True)
            emit("V", lambda e: e.tensor_scalar(slotB, slotB, EPS_LN, None, ALU.add))
            emit("S", lambda e: e.activation(slotB, slotB, AF.Ln), [("V", v_var)])
            s_sd = emit("S", lambda e: e.activation(slotB, slotB, AF.Exp, scale=-0.5),
                        inc=True)
            v_negm = emit("V", lambda e: e.tensor_scalar(
                slotA, slotA, -1.0, None, ALU.mult), inc=True)
            t_bc = 0
            for g in range(4):
                emit("T", lambda e, o=PS[:, 512 * g:512 * g + 512],
                     r=FA[0:1, 2048 + 512 * g:2048 + 512 * (g + 1)]:
                     e.matmul(o, ONER128, r, start=True, stop=True),
                     [("V", v_negm), ("S", s_sd)])
                t_bc = emit("T", lambda e, o=PS[:, 2048 + 512 * g:2048 + 512 * g + 512],
                            r=FA[0:1, 512 * g:512 * (g + 1)]:
                            e.matmul(o, ONER128, r, start=True, stop=True),
                            [("V", v_negm)], inc=(g == 3))
            v_ap = [0] * NCH
            for c in range(NCH):
                if hook is not None:
                    hook(c)
                w = [("T", t_bc)]
                if c >= 2:
                    w.append(("V", v_ap[c - 2]))
                d_a = dma(FA[:, G0 + 2048 * (c % 2):G0 + 2048 * (c % 2) + 2048],
                          src_dram[:, :].rearrange("(a p) m -> a p m", a=NCH, p=128)[c],
                          waits=w)
                xs = FA[:, G0 + 2048 * (c % 2):G0 + 2048 * (c % 2) + 2048]
                emit("V", lambda e, x=xs: e.tensor_add(x, x, PS[:, 2048:4096]),
                     [("D", d_a), ("T", t_bc)])
                v_ap[c] = emit("V", lambda e, x=xs, o=BF[:, XH + 2048 * c:XH + 2048 * (c + 1)]:
                               e.tensor_mul(o, x, PS[:, 0:2048]), inc=True)
            return v_ap

        def push_w_unit(wdram, dst, ot, extra_tr=()):
            oap = BF[:, dst:dst + 8192].rearrange(
                "p (b r) -> p b r", b=8, r=1024)[:, :, 128 * ot:128 * ot + 128]
            prep.push(wdram[128 * ot:128 * (ot + 1), :], "cast", None, oap,
                      extra_tr_waits=extra_tr)

        v_ap1 = ln_phase(xfm, True, hook=lambda c: push_w_unit(wk, WT, c))
        if KPH < 2:
            finish()
            return nc

        d_c1 = dma(FA[:, 0:2048], cr1, waits=[("V", v_ap1[6])])
        d_c2 = dma(FA[:, 2048:4096], cr2, waits=[("V", v_ap1[7])])

        def prep_weight(wdram, wait_wt_free_t, dst=WT):
            extra_tr = [("T", wait_wt_free_t)] if wait_wt_free_t else []
            for ot in range(8):
                oap = BF[:, dst:dst + 8192].rearrange(
                    "p (b r) -> p b r", b=8, r=1024)[:, :, 128 * ot:128 * ot + 128]
                prep.push(wdram[128 * ot:128 * (ot + 1), :],
                          "cast", None, oap, extra_tr_waits=extra_tr)
            return prep.flush()

        def fm_proj(dst_off, bias_off, d_wt, extra_first_waits, wbase=WT, hook=None):
            v_rot = [0] * NCH
            s_qb = [0] * NCH
            t_oc = [0] * NCH
            for oc in range(NCH):
                if hook is not None:
                    hook(oc)
                pcol = (oc % 2) * 2048
                w = [("D", d_wt)]
                if oc == 0:
                    w += extra_first_waits
                if oc >= 2:
                    w.append(("S", s_qb[oc - 2]))
                for cc in range(NCH):
                    for g in range(4):
                        last = (cc == NCH - 1 and g == 3)
                        t_oc[oc] = emit("T", lambda e,
                                        o=PS[:, pcol + 512 * g:pcol + 512 * (g + 1)],
                                        l=BF[:, wbase + 1024 * cc + 128 * oc:wbase + 1024 * cc + 128 * oc + 128],
                                        r=BF[:, XH + 2048 * cc + 512 * g:XH + 2048 * cc + 512 * (g + 1)],
                                        s=(cc == 0), p=(cc == NCH - 1):
                                        e.matmul(o, l, r, start=s, stop=p),
                                        w, inc=last)
                    w = []
                qb = BF[:, SC:SC + 2048]
                qs = BF[:, SC + 2048:SC + 4096]
                wS = [("T", t_oc[oc])]
                if oc >= 1:
                    wS.append(("V", v_rot[oc - 1]))
                s_qb[oc] = emit("S", lambda e, o=qb, i=PS[:, pcol:pcol + 2048],
                                b=FA[:, bias_off + oc:bias_off + oc + 1]:
                                e.activation(o, i, AF.Identity, bias=b), wS, inc=True)
                emit("V", lambda e, o=qs, i=qb:
                     e.stream_shuffle(o, i, mask=[(j + 16) % 32 for j in range(32)]),
                     [("S", s_qb[oc])])
                emit("V", lambda e, o=qb, i=qb: e.tensor_mul(o, i, FA[:, 0:2048]),
                     [("D", d_c1)])
                emit("V", lambda e, o=qs, i=qs: e.tensor_mul(o, i, FA[:, 2048:4096]),
                     [("D", d_c2)])
                v_rot[oc] = emit("V", lambda e, o=BF[:, dst_off + 2048 * oc:dst_off + 2048 * (oc + 1)],
                                 a=qb, b=qs: e.tensor_add(o, a, b), inc=True)
            return t_oc[NCH - 1], v_rot, s_qb

        d_wkT = prep.flush()
        t_kproj, v_rotk, s_qbk = fm_proj(
            KK, BKo, d_wkT, [("V", v_ap1[7])],
            hook=lambda oc: push_w_unit(wv, QQ, oc))
        d_wvT = prep.flush()
        t_vp = [0] * NTB
        v_vev = [0] * NTB
        for tb in range(NTB):
            if tb % 2 == 0:
                push_w_unit(wq, WT, tb // 2, extra_tr=[("T", t_kproj)])
            pcol = 2048 + 1024 * (tb % 2)
            w = [("D", d_wvT)]
            if tb < 2:
                w.append(("S", s_qbk[NCH - 1]))
            if tb >= 2:
                w.append(("V", v_vev[tb - 2]))
            for cc in range(NCH):
                for g in range(2):
                    emit("T", lambda e, o=PS[:, pcol + 512 * g:pcol + 512 * (g + 1)],
                         l=BF[:, XH + 2048 * cc + 128 * tb:XH + 2048 * cc + 128 * tb + 128],
                         r=BF[:, QQ + 1024 * cc + 512 * g:QQ + 1024 * cc + 512 * (g + 1)],
                         s=(cc == 0): e.matmul(o, l, r, start=s, stop=False), w)
                w = []
            for g in range(2):
                t_vp[tb] = emit("T", lambda e, o=PS[:, pcol + 512 * g:pcol + 512 * (g + 1)],
                                l=ONER128, r=FA[0:1, BVo + 512 * g:BVo + 512 * (g + 1)]:
                                e.matmul(o, l, r, start=False, stop=True),
                                [("D", d_bv)], inc=(g == 1))
            oap = BF[:, VV + 1040 * tb:VV + 1040 * tb + 1040].rearrange(
                "p (h d) -> p h d", h=16, d=65)[:, :, 0:64]
            iap = PS[:, pcol:pcol + 1024].rearrange("p (h d) -> p h d", h=16, d=64)
            v_vev[tb] = emit("V", lambda e, o=oap, i=iap: e.tensor_copy(o, i),
                             [("T", t_vp[tb])], inc=True)
        d_wqT = prep.flush()
        t_qproj, v_rotq, s_qbq = fm_proj(
            QQ, BQo, d_wqT, [("V", v_vev[NTB - 1]), ("S", s_qbk[NCH - 1])])

        if KPH < 3:
            finish()
            return nc
        d_wpT = prep_weight(wp, t_qproj)

        _vafter_preps = cnt["V"]
        gstate = {"gi": 0, "v_gprev": [_vafter_preps, _vafter_preps], "v_gred": 0}

        def gamma_unit():
            gi = gstate["gi"]
            if gi >= 64:
                return
            w = [("V", gstate["v_gprev"][gi % 2])]
            if gi < 64 - 32:
                iap = w1[128 * gi:128 * (gi + 1), :]
            else:
                tt = gi - 32
                iap = w2[:, :].rearrange("(a p) (b m) -> a b p m", p=128, m=1024)[tt // 4, tt % 4]
            d_g = dma(FA[:, WS + 1024 * (gi % 2):WS + 1024 * (gi % 2) + 1024], iap, waits=w)
            v = emit("V", lambda e, o=FA[:, GACCo + gi:GACCo + gi + 1],
                     i=FA[:, WS + 1024 * (gi % 2):WS + 1024 * (gi % 2) + 1024]:
                     e.tensor_reduce(o, i, AX.X, ALU.add, apply_absolute_value=True),
                     [("D", d_g)], inc=True)
            gstate["v_gprev"][gi % 2] = v
            gstate["v_gred"] = v
            gstate["gi"] = gi + 1

        pc = 0
        s_exp = {}
        t_pv_last = 0
        v_yev = [0] * H
        v_sum = [0] * H
        d_sumst = [0] * H
        pending_pv = [None]

        def flush_pv():
            nonlocal_t = pending_pv[0]
            if nonlocal_t is not None:
                nonlocal_t()
                pending_pv[0] = None

        for h in range(H):
            ch = h // 2
            pb = 64 * (h % 2)
            for _ in range(4):
                gamma_unit()
            for i in range(NTB):
                q0 = 128 * i
                Nq = T - q0
                pieces = [(q0, min(1024, Nq))]
                if Nq > 1024:
                    pieces.append((q0 + 1024, Nq - 1024))
                for pi, (qoff, plen) in enumerate(pieces):
                    par = pc % 2
                    stc = 1024 * par
                    wT_ = []
                    if pc >= 2:
                        wT_.append(("S", s_exp[pc - 2]))
                    if h == 0 and i == 0 and pi == 0:
                        wT_.append(("V", v_rotq[NCH - 1]))
                        wT_.append(("V", v_rotk[NCH - 1]))
                    t_kq = 0
                    ng = (plen + 511) // 512
                    for g in range(ng):
                        gw = min(512, plen - 512 * g)
                        diag = (pi == 0 and g == 0)
                        t_kq = emit("T", lambda e, o=PS[:, stc + 512 * g:stc + 512 * g + gw],
                                    l=BF[pb:pb + 64, KK + 2048 * ch + q0:KK + 2048 * ch + q0 + 128],
                                    r=BF[pb:pb + 64, QQ + 2048 * ch + qoff + 512 * g:
                                         QQ + 2048 * ch + qoff + 512 * g + gw],
                                    p=(not diag):
                                    e.matmul(o, l, r, start=True, stop=p),
                                    wT_, inc=(g == ng - 1 and not diag))
                        wT_ = []
                        if diag:
                            t_kq = emit("T", lambda e, o=PS[:, stc:stc + 128],
                                        l=BF[:, SP_I:SP_I + 128],
                                        r=BF[:, SP_I + 128:SP_I + 256]:
                                        e.matmul(o, l, r, start=False, stop=True),
                                        [("D", d_cbf)], inc=(g == ng - 1))
                    s_exp[pc] = emit("S", lambda e, o=BF[:, TPc + 1024 * par:TPc + 1024 * par + plen],
                                     i=PS[:, stc:stc + plen]:
                                     e.activation(o, i, AF.Exp, scale=0.125),
                                     [("T", t_kq)], inc=True)

                    def make_pv(pc_=pc, par_=par, qoff_=qoff, plen_=plen, i_=i, h_=h, pi_=pi):
                        def do():
                            nonlocal t_pv_last
                            wPV = [("S", s_exp[pc_])]
                            if i_ == 0 and pi_ == 0 and h_ > 0:
                                wPV.append(("V", v_sum[h_ - 1]))
                            off = 0
                            while off < plen_:
                                gw = min(512 - ((qoff_ + off) % 512), plen_ - off)
                                if pi_ == 0 and off == 0:
                                    gw = min(gw, 128)
                                stp = (pi_ == 0 and off == 0 and (i_ % 4) == 3)
                                st_ = (i_ == 0 and ((qoff_ + off) % 512) == 0)
                                t_pv_last = emit("T", lambda e,
                                                 o=PS[0:65, 2048 + qoff_ + off:2048 + qoff_ + off + gw],
                                                 l=BF[:, VV + 1040 * i_ + 65 * h_:VV + 1040 * i_ + 65 * h_ + 65],
                                                 r=BF[:, TPc + 1024 * par_ + off:TPc + 1024 * par_ + off + gw],
                                                 s=st_, p=stp:
                                                 e.matmul(o, l, r, start=s, stop=p),
                                                 wPV, inc=(off + gw >= plen_))
                                wPV = []
                                off += gw
                        return do

                    flush_pv()
                    pending_pv[0] = make_pv()
                    pc += 1
            flush_pv()
            wY = [("T", t_pv_last)]
            if h == 0:
                wY.append(("T", t_qproj))
            v_yev[h] = emit("V", lambda e, o=BF[pb:pb + 64, XH + 2048 * ch:XH + 2048 * (ch + 1)],
                            i=PS[0:64, 2048:4096]: e.tensor_copy(o, i), wY, inc=True)
            wSm = []
            if h >= 1:
                wSm.append(("D", d_sumst[h - 1]))
            v_sum[h] = emit("V", lambda e, o=FA[0:1, 0:2048], i=PS[64:65, 2048:4096]:
                            e.tensor_copy(o, i), wSm, inc=True)
            d_sumst[h] = dma(sums_d[h:h + 1, :], FA[0:1, 0:2048], waits=[("V", v_sum[h])])

        d_sumld = dma(FA[0:16, 0:2048], sums_d, waits=[("D", d_sumst[H - 1])])
        v_inv = emit("V", lambda e: e.reciprocal(FA[0:16, 0:2048], FA[0:16, 0:2048]),
                     [("D", d_sumld)], inc=True)
        t_selbc = [0] * NCH
        v_ynorm = [0] * NCH
        for c2 in range(NCH):
            pcol = (c2 % 2) * 2048
            w = [("V", v_inv), ("D", d_const)]
            if c2 >= 2:
                w.append(("V", v_ynorm[c2 - 2]))
            for g in range(4):
                t_selbc[c2] = emit("T", lambda e, o=PS[:, pcol + 512 * g:pcol + 512 * (g + 1)],
                                   l=FA[0:16, SELo + 128 * c2:SELo + 128 * (c2 + 1)],
                                   r=FA[0:16, 512 * g:512 * (g + 1)]:
                                   e.matmul(o, l, r, start=True, stop=True), w, inc=(g == 3))
                w = []
            v_ynorm[c2] = emit("V", lambda e, o=BF[:, XH + 2048 * c2:XH + 2048 * (c2 + 1)],
                               i=PS[:, pcol:pcol + 2048]:
                               e.tensor_mul(o, o, i),
                               [("T", t_selbc[c2]), ("V", v_yev[2 * c2 + 1])], inc=True)

        if KPH < 4:
            finish()
            return nc
        while gstate["gi"] < 64:
            gamma_unit()
        emit("V", lambda e: e.tensor_reduce(FA[:, GT:GT + 1], FA[:, GACCo:GACCo + 32], AX.X, ALU.add),
             [("V", gstate["v_gred"])])
        v_g2 = emit("V", lambda e: e.tensor_reduce(FA[:, GT + 1:GT + 2], FA[:, GACCo + 32:GACCo + 64],
                                                   AX.X, ALU.add), inc=True)
        emit("T", lambda e: e.matmul(PS[0:1, 0:1], FA[:, ONECOLo:ONECOLo + 1], FA[:, GT:GT + 1],
                                     start=True, stop=True),
             [("V", v_g2), ("V", v_ynorm[NCH - 1])])
        t_gm = emit("T", lambda e: e.matmul(PS[0:1, 1:2], FA[:, ONECOLo:ONECOLo + 1],
                                            FA[:, GT + 1:GT + 2], start=True, stop=True),
                    [("V", v_g2)], inc=True)
        emit("V", lambda e: e.tensor_scalar(FA[0:1, GT + 4:GT + 5], PS[0:1, 0:1],
                                            1.0 / (FF * C), EPS_Q, ALU.mult, ALU.add),
             [("T", t_gm)])
        emit("V", lambda e: e.tensor_scalar(FA[0:1, GT + 5:GT + 6], PS[0:1, 1:2],
                                            1.0 / (FF * C), EPS_Q, ALU.mult, ALU.add))
        emit("V", lambda e: e.reciprocal(FA[0:1, GT + 4:GT + 5], FA[0:1, GT + 4:GT + 5]))
        v_gi = emit("V", lambda e: e.reciprocal(FA[0:1, GT + 5:GT + 6], FA[0:1, GT + 5:GT + 6]),
                    inc=True)
        emit("T", lambda e: e.matmul(PS[:, 4:5], ONER128, FA[0:1, GT + 4:GT + 5],
                                     start=True, stop=True), [("V", v_gi)])
        t_gb = emit("T", lambda e: e.matmul(PS[:, 5:6], ONER128, FA[0:1, GT + 5:GT + 6],
                                            start=True, stop=True), inc=True)
        emit("V", lambda e: e.tensor_copy(FA[:, GT + 2:GT + 3], PS[:, 4:5]), [("T", t_gb)])
        v_sg = emit("V", lambda e: e.tensor_copy(FA[:, GT + 3:GT + 4], PS[:, 5:6]), inc=True)
        SG1 = FA[:, GT + 2:GT + 3]
        SG2 = FA[:, GT + 3:GT + 4]

        t_pp = [0] * NCH
        v_res = [0] * NCH
        d_x1w = [0] * NCH
        for oc in range(NCH):
            pcol = (oc % 2) * 2048
            w = [("D", d_wpT), ("V", v_sg), ("V", v_ynorm[NCH - 1])]
            if oc >= 2:
                w.append(("V", v_res[oc - 2]))
            for cc in range(NCH):
                for g in range(4):
                    t_pp[oc] = emit("T", lambda e, o=PS[:, pcol + 512 * g:pcol + 512 * (g + 1)],
                                    l=BF[:, WT + 1024 * cc + 128 * oc:WT + 1024 * cc + 128 * oc + 128],
                                    r=BF[:, XH + 2048 * cc + 512 * g:XH + 2048 * cc + 512 * (g + 1)],
                                    s=(cc == 0), p=(cc == NCH - 1): e.matmul(o, l, r, start=s, stop=p),
                                    w, inc=(cc == NCH - 1 and g == 3))
                    w = []
            wD = [("T", t_selbc[NCH - 1]), ("V", v_ynorm[NCH - 1])]
            if oc >= 2:
                wD.append(("V", v_res[oc - 2]))
            d_xr = dma(FA[:, G0 + 2048 * (oc % 2):G0 + 2048 * (oc % 2) + 2048],
                       xfm[:, :].rearrange("(a p) m -> a p m", a=NCH, p=128)[oc], waits=wD)
            xs = FA[:, G0 + 2048 * (oc % 2):G0 + 2048 * (oc % 2) + 2048]
            v_res[oc] = emit("V", lambda e, x=xs, i=PS[:, pcol:pcol + 2048],
                             b=FA[:, BPo + oc:BPo + oc + 1]:
                             e.scalar_tensor_tensor(x, i, b, x, ALU.add, ALU.add),
                             [("T", t_pp[oc]), ("D", d_xr)], inc=True)
            d_x1w[oc] = dma(x1d[:, :].rearrange("(a p) m -> a p m", a=NCH, p=128)[oc], xs,
                            waits=[("V", v_res[oc])])

        ln2_gate_t = t_pp[NCH - 1]
        ln2_gate_v = v_res[NCH - 1]
        v_ap2 = None

        def ln_phase2():
            d_x = [0] * NCH
            v_st = [0] * NCH
            for c in range(NCH):
                w = [("D", d_x1w[c])]
                if c >= 2:
                    w.append(("V", v_st[c - 2]))
                d_x[c] = dma(FA[:, G0 + 2048 * (c % 2):G0 + 2048 * (c % 2) + 2048],
                             x1d[:, :].rearrange("(a p) m -> a p m", a=NCH, p=128)[c], waits=w)
                for g in range(4):
                    xsrc = FA[:, G0 + 2048 * (c % 2) + 512 * g:G0 + 2048 * (c % 2) + 512 * (g + 1)]
                    exb = BF[:, SP_EX:SP_EX + 512]
                    exq = BF[:, SP_EX + 512:SP_EX + 1024]
                    wv_ = [("D", d_x[c]), ("T", ln2_gate_t)]
                    if c > 0 or g > 0:
                        wv_.append(("T", cnt["T"]))
                    emit("V", lambda e, o=exb, i=xsrc: e.tensor_copy(o, i), wv_)
                    v_c = emit("V", lambda e, o=exq, i=exb: e.tensor_mul(o, i, i), inc=True)
                    emit("T", lambda e, o=PS[0:1, 512 * g:512 * g + 512], l=ONES_BF, r=exb,
                         s=(c == 0), p=(c == NCH - 1):
                         e.matmul(o, l, r, start=s, stop=p), [("V", v_c), ("V", ln2_gate_v)])
                    emit("T", lambda e, o=PS[0:1, 2048 + 512 * g:2048 + 512 * g + 512],
                         l=ONES_BF, r=exq, s=(c == 0), p=(c == NCH - 1):
                         e.matmul(o, l, r, start=s, stop=p), [("V", v_c)], inc=True)
                v_st[c] = cnt["V"]
            t_stats = cnt["T"]
            slotA = FA[0:1, 0:2048]
            slotB = FA[0:1, 2048:4096]
            emit("V", lambda e: e.tensor_scalar(slotA, PS[0:1, 0:2048], 1.0 / C, None, ALU.mult),
                 [("T", t_stats)])
            emit("V", lambda e: e.tensor_mul(slotB, slotA, slotA))
            v_var = emit("V", lambda e: e.scalar_tensor_tensor(
                slotB, PS[0:1, 2048:4096], 1.0 / C, slotB, ALU.mult, ALU.subtract),
                [("T", t_stats)], inc=True)
            emit("V", lambda e: e.tensor_scalar(slotB, slotB, EPS_LN, None, ALU.add))
            emit("S", lambda e: e.activation(slotB, slotB, AF.Ln), [("V", v_var)])
            s_sd = emit("S", lambda e: e.activation(slotB, slotB, AF.Exp, scale=-0.5),
                        inc=True)
            v_negm = emit("V", lambda e: e.tensor_scalar(
                slotA, slotA, -1.0, None, ALU.mult), inc=True)
            t_bc = 0
            for g in range(4):
                emit("T", lambda e, o=PS[:, 512 * g:512 * g + 512],
                     r=FA[0:1, 2048 + 512 * g:2048 + 512 * (g + 1)]:
                     e.matmul(o, ONER128, r, start=True, stop=True),
                     [("V", v_negm), ("S", s_sd)])
                t_bc = emit("T", lambda e, o=PS[:, 2048 + 512 * g:2048 + 512 * g + 512],
                            r=FA[0:1, 512 * g:512 * (g + 1)]:
                            e.matmul(o, ONER128, r, start=True, stop=True),
                            [("V", v_negm)], inc=(g == 3))
            v_ap = [0] * NCH
            for c in range(NCH):
                w = [("T", t_bc)]
                if c >= 2:
                    w.append(("V", v_ap[c - 2]))
                d_a = dma(FA[:, G0 + 2048 * (c % 2):G0 + 2048 * (c % 2) + 2048],
                          x1d[:, :].rearrange("(a p) m -> a p m", a=NCH, p=128)[c], waits=w)
                xs = FA[:, G0 + 2048 * (c % 2):G0 + 2048 * (c % 2) + 2048]
                emit("V", lambda e, x=xs: e.tensor_add(x, x, PS[:, 2048:4096]),
                     [("D", d_a), ("T", t_bc)])
                v_ap[c] = emit("V", lambda e, x=xs, o=BF[:, XH + 2048 * c:XH + 2048 * (c + 1)]:
                               e.tensor_mul(o, x, PS[:, 0:2048]), inc=True)
            return v_ap

        v_ap2 = ln_phase2()

        d_w2rdy = [0] * 32
        d_w1rdy = [0] * 32

        def mk_rdy2(sl):
            def f(d):
                for b_ in range(8):
                    d_w2rdy[8 * sl + b_] = d
            return f

        def mk_rdy1(fc):
            def f(d):
                d_w1rdy[fc] = d
            return f

        first_w2 = [("V", gstate["v_gred"])]
        for sl in range(4):
            for dt in range(8):
                base = w2base(8 * sl)
                oap = BF[:, base:base + 8192].rearrange(
                    "p (b r) -> p b r", b=8, r=1024)[:, :, 128 * dt:128 * dt + 128]
                ex = [("T", t_pv_last)]
                if sl == 2:
                    ex.append(("T", t_pp[NCH - 1]))
                prep.push(w2[128 * dt:128 * (dt + 1), 1024 * sl:1024 * (sl + 1)],
                          "quant", SG2, oap, extra_tr_waits=ex,
                          extra_ld_waits=first_w2,
                          on_ready=mk_rdy2(sl) if dt == 7 else None)
                first_w2 = []
            for fc in range(8 * sl, 8 * sl + 8):
                oap = BF[:, QQ:QQ + 32768].rearrange(
                    "p (b r) -> p b r", b=8, r=4096)[:, :, 128 * fc:128 * fc + 128]
                prep.push(w1[128 * fc:128 * (fc + 1), :],
                          "quant", SG1, oap, extra_tr_waits=[("T", t_pv_last)],
                          on_ready=mk_rdy1(fc))
        prep.flush()

        t_mm1 = {}
        s_relu = {}
        t_mm2 = {}
        v_out = [0] * 8
        d_out = [0] * 8
        pending_mm2 = [None]

        def flush_mm2():
            if pending_mm2[0] is not None:
                pending_mm2[0]()
                pending_mm2[0] = None

        for tp in range(8):
            w = [("T", t_selbc[NCH - 1])]
            if tp >= 2:
                w.append(("V", v_out[tp - 2]))
            if tp < 2:
                w.append(("V", v_ap2[NCH - 1]))
                w.append(("D", d_x1w[NCH - 1]))
            d_x1p = dma(FA[:, G0 + 2048 * (tp % 2):G0 + 2048 * (tp % 2) + 2048].rearrange(
                "p (a m) -> p a m", a=8, m=256),
                x1d[:, :].rearrange("(a p) m -> p a m", a=NCH, p=128)[:, :, 256 * tp:256 * (tp + 1)],
                waits=w)
            for fc in range(32):
                hcol = 512 * (fc % 2)
                wmm1 = [("D", d_w1rdy[fc]), ("V", v_ap2[NCH - 1])]
                prev = (tp, fc - 2) if fc >= 2 else ((tp - 1, 30 + fc) if tp >= 1 else None)
                if prev in s_relu:
                    wmm1.append(("S", s_relu[prev]))
                for cc in range(NCH):
                    t_mm1[(tp, fc)] = emit("T", lambda e, o=PS[:, hcol:hcol + 256],
                                           l=BF[:, QQ + 4096 * cc + 128 * fc:QQ + 4096 * cc + 128 * fc + 128],
                                           r=BF[:, XH + 2048 * cc + 256 * tp:XH + 2048 * cc + 256 * (tp + 1)],
                                           s=(cc == 0), p=(cc == NCH - 1): e.matmul(o, l, r, start=s, stop=p),
                                           wmm1, inc=(cc == NCH - 1))
                    wmm1 = []
                s_relu[(tp, fc)] = emit("S", lambda e, o=BF[:, SP_H + 256 * (fc % 2):SP_H + 256 * (fc % 2) + 256],
                                        i=PS[:, hcol:hcol + 256], b=FA[:, B1o + fc:B1o + fc + 1]:
                                        e.activation(o, i, AF.Relu, bias=b),
                                        [("T", t_mm1[(tp, fc)])], inc=True)
                def make_mm2(tp_=tp, fc_=fc):
                    def do():
                        wmm2 = [("S", s_relu[(tp_, fc_)]), ("D", d_w2rdy[fc_])]
                        if fc_ == 0 and tp_ >= 1:
                            wmm2.append(("V", v_out[tp_ - 1]))
                        for dc in range(NCH):
                            st_ = (fc_ == 0 and dc % 2 == 0)
                            t_mm2[(tp_, fc_)] = emit(
                                "T", lambda e, o=PS[:, 2048 + 256 * dc:2048 + 256 * (dc + 1)],
                                l=BF[:, w2base(fc_) + 128 * dc:w2base(fc_) + 128 * dc + 128],
                                r=BF[:, SP_H + 256 * (fc_ % 2):SP_H + 256 * (fc_ % 2) + 256],
                                s=st_, p=(fc_ == 31 and dc % 2 == 1):
                                e.matmul(o, l, r, start=s, stop=p),
                                wmm2, inc=(dc == NCH - 1))
                            wmm2[:] = []
                    return do

                flush_mm2()
                pending_mm2[0] = make_mm2()
            flush_mm2()
            for dc in range(NCH):
                xs = FA[:, G0 + 2048 * (tp % 2) + 256 * dc:G0 + 2048 * (tp % 2) + 256 * (dc + 1)]
                v_out[tp] = emit("V", lambda e, x=xs, i=PS[:, 2048 + 256 * dc:2048 + 256 * (dc + 1)],
                                 b=FA[:, B2o + dc:B2o + dc + 1]:
                                 e.scalar_tensor_tensor(x, i, b, x, ALU.add, ALU.add),
                                 [("T", t_mm2[(tp, 31)]), ("D", d_x1p)], inc=(dc == NCH - 1))
            d_out[tp] = dma(out_fm[:, :].rearrange("(a p) m -> p a m", a=NCH, p=128)[
                :, :, 256 * tp:256 * (tp + 1)],
                FA[:, G0 + 2048 * (tp % 2):G0 + 2048 * (tp % 2) + 2048].rearrange(
                    "p (a m) -> p a m", a=8, m=256),
                waits=[("V", v_out[tp])])

        finish()
    return nc


def _perm():
    perm = np.zeros(C, dtype=np.int64)
    for g in range(C):
        chk, p = g // 128, g % 128
        h = 2 * chk + p // 64
        lp = p % 64
        quad = lp // 32
        r = lp % 32
        if r < 16:
            i_idx, par = r + 16 * quad, 0
        else:
            i_idx, par = (r - 16) + 16 * quad, 1
        perm[g] = 64 * h + 2 * i_idx + par
    return perm


def _rope_tables():
    i = np.arange(32, dtype=np.float32)
    theta = (np.float32(1.0) / (np.float32(10000.0) ** (np.float32(2.0) * i / np.float32(64)))).astype(np.float32)
    t = np.arange(T, dtype=np.float32)[:, None]
    ang = (t * theta[None, :]).astype(np.float32)
    cos = np.cos(ang.astype(np.float64)).astype(np.float32)
    sin = np.sin(ang.astype(np.float64)).astype(np.float32)
    C1 = np.zeros((128, T), np.float32)
    C2 = np.zeros((128, T), np.float32)
    for p in range(128):
        lp = p % 64
        quad = lp // 32
        r = lp % 32
        if r < 16:
            i_idx, is_even = r + 16 * quad, True
        else:
            i_idx, is_even = (r - 16) + 16 * quad, False
        C1[p] = cos[:, i_idx]
        C2[p] = (-sin[:, i_idx]) if is_even else sin[:, i_idx]
    return C1, C2


def _consts_pack(b1, b2, bq_p, bk_p, bp):
    A = np.zeros((128, CONST_COLS), np.float32)
    for c2 in range(NCH):
        for j in range(128):
            h = 2 * c2 + (1 if j >= 64 else 0)
            A[h, 128 * c2 + j] = 1.0
    for p in range(128):
        A[p, 1024:1024 + p] = NEG
    A[:, 1152:1184] = b1.reshape(32, 128).T
    A[:, 1184:1192] = b2.reshape(8, 128).T
    A[:, 1192:1200] = bq_p.reshape(8, 128).T
    A[:, 1200:1208] = bk_p.reshape(8, 128).T
    A[:, 1208:1216] = bp.reshape(8, 128).T
    return A


_NC_CACHE = {}


def kernel(L0_attn_x, L0_attn_wq, L0_attn_bq, L0_attn_wk, L0_attn_bk,
           L0_attn_wv, L0_attn_bv, L0_attn_wp, L0_attn_bp,
           L0_ffn_W1, L0_ffn_b1, L0_ffn_W2, L0_ffn_b2):
    x = np.asarray(L0_attn_x, dtype=np.float32)
    perm = _perm()
    wq_p = np.ascontiguousarray(np.asarray(L0_attn_wq, np.float32)[perm])
    wk_p = np.ascontiguousarray(np.asarray(L0_attn_wk, np.float32)[perm])
    bq_p = np.ascontiguousarray(np.asarray(L0_attn_bq, np.float32)[perm])
    bk_p = np.ascontiguousarray(np.asarray(L0_attn_bk, np.float32)[perm])
    wv = np.ascontiguousarray(np.asarray(L0_attn_wv, np.float32))
    wp = np.ascontiguousarray(np.asarray(L0_attn_wp, np.float32))
    w1 = np.ascontiguousarray(np.asarray(L0_ffn_W1, np.float32))
    w2 = np.ascontiguousarray(np.asarray(L0_ffn_W2, np.float32))
    C1, C2 = _rope_tables()
    consts = _consts_pack(np.asarray(L0_ffn_b1, np.float32), np.asarray(L0_ffn_b2, np.float32),
                          bq_p, bk_p, np.asarray(L0_attn_bp, np.float32))
    bvr = np.ascontiguousarray(np.asarray(L0_attn_bv, np.float32).reshape(1, C))
    cbf_host = np.zeros((128, 256), dtype=ml_dtypes.bfloat16)
    cbf_host[:, 0:128] = np.eye(128, dtype=np.float32).astype(ml_dtypes.bfloat16)
    mtri = np.zeros((128, 128), np.float32)
    for p in range(128):
        mtri[p, :p] = NEG
    cbf_host[:, 128:256] = mtri.astype(ml_dtypes.bfloat16)

    if "nc" not in _NC_CACHE:
        _NC_CACHE["nc"] = build_nc()
    nc = _NC_CACHE["nc"]

    in_maps = []
    for b in range(8):
        in_maps.append({
            "xfm": np.ascontiguousarray(x[b].T),
            "wq": wq_p, "wk": wk_p, "wv": wv, "wp": wp,
            "w1": w1, "w2": w2,
            "cr1": C1, "cr2": C2, "consts": consts, "bvr": bvr, "cbf": cbf_host,
        })
    res = run_bass_kernel_spmd(nc, in_maps, core_ids=list(range(8)))
    out = np.stack([np.asarray(r["out_fm"]).T for r in res.results], axis=0)
    return out.astype(np.float32)


# version 14 (final)
# speedup vs baseline: 1.9230x; 1.0978x over previous
import os
import numpy as np
import ml_dtypes
from contextlib import ExitStack

import concourse.bass as bass
import concourse.mybir as mybir
from concourse.bass_utils import run_bass_kernel_spmd

F32 = mybir.dt.float32
BF16 = mybir.dt.bfloat16
AF = mybir.ActivationFunctionType
ALU = mybir.AluOpType
AX = mybir.AxisListType

T = 2048
C = 1024
H = 16
DH = 64
FF = 4096
NCH = 8
NTB = 16
NEG = -1.0e10
EPS_LN = 1e-5
EPS_Q = 1e-5

XH = 0
QQ = 16384
KK = 32768
VV = 49152
WT = 65792
SC = 73984
WB = 78080
TPc = 80128
SP = 82176
SP_H = SP
SP_PREP = SP + 512
SP_ONE = SP + 1536
SP_EX = SP + 1552
SP_I = SP + 2576
SP_T2 = SP + 2832
NBF = SP + 2832 + 1024

def w2base(fc):
    if fc < 16:
        return VV + 1024 * fc
    if fc < 24:
        return WT + 1024 * (fc - 16)
    if fc < 28:
        return SC + 1024 * (fc - 24)
    return WB + 1024 * (fc - 28)

G0 = 0
WS = 4096
SELo = 6144
MTRIo = 7168
B1o = 7296
B2o = 7328
BQo = 7336
BKo = 7344
BPo = 7352
GACCo = 7360
ONECOLo = 7424
GT = 7425
BVo = 7456
ONERo = 8480
NFA = 8992

CONST_COLS = 1232


_SCHED = None


def _semname(sm):
    return getattr(sm, "name", None) if sm is not None else None


def build_nc():
    nc = bass.Bass("TRN2", target_bir_lowering=False, detect_race_conditions=False)
    xfm = nc.dram_tensor("xfm", [C, T], F32, kind="ExternalInput").ap()
    wq = nc.dram_tensor("wq", [C, C], F32, kind="ExternalInput").ap()
    wk = nc.dram_tensor("wk", [C, C], F32, kind="ExternalInput").ap()
    wv = nc.dram_tensor("wv", [C, C], F32, kind="ExternalInput").ap()
    wp = nc.dram_tensor("wp", [C, C], F32, kind="ExternalInput").ap()
    w1 = nc.dram_tensor("w1", [FF, C], F32, kind="ExternalInput").ap()
    w2 = nc.dram_tensor("w2", [C, FF], F32, kind="ExternalInput").ap()
    cr1 = nc.dram_tensor("cr1", [128, T], F32, kind="ExternalInput").ap()
    cr2 = nc.dram_tensor("cr2", [128, T], F32, kind="ExternalInput").ap()
    consts = nc.dram_tensor("consts", [128, CONST_COLS], F32, kind="ExternalInput").ap()
    bvr = nc.dram_tensor("bvr", [1, C], F32, kind="ExternalInput").ap()
    cbf = nc.dram_tensor("cbf", [128, 256], BF16, kind="ExternalInput").ap()
    out_fm = nc.dram_tensor("out_fm", [C, T], F32, kind="ExternalOutput").ap()
    x1d = nc.dram_tensor("x1d", [C, T], F32, kind="Internal").ap()
    sums_d = nc.dram_tensor("sums_d", [H, T], F32, kind="Internal").ap()

    with ExitStack() as st:
        BFt = st.enter_context(nc.sbuf_tensor("BF", [128, NBF], BF16))
        FAt = st.enter_context(nc.sbuf_tensor("FA", [128, NFA], F32))
        PSt = st.enter_context(nc.psum_tensor("PS", [128, 4096], F32))
        sD = st.enter_context(nc.semaphore("sD"))
        sT = st.enter_context(nc.semaphore("sT"))
        sV = st.enter_context(nc.semaphore("sV"))
        sS = st.enter_context(nc.semaphore("sS"))
        sG = st.enter_context(nc.semaphore("sG"))
        sA = st.enter_context(nc.semaphore("sA"))
        block = st.enter_context(nc.Block())

        BF = BFt.ap()
        FA = FAt.ap()
        PS = PSt.ap()

        ops = {"sync": [], "tensor": [], "vector": [], "scalar": [], "gpsimd": []}
        cnt = {"D": 0, "T": 0, "V": 0, "S": 0, "G": 0, "A": 0}
        sem = {"D": sD, "T": sT, "V": sV, "S": sS, "G": sG, "A": sA}
        seen = {e: {} for e in ops}
        ENG_OF = {"D": "sync", "T": "tensor", "V": "vector", "S": "scalar",
                  "G": "gpsimd", "A": "scalar"}

        def emit(ch, fn, waits=(), inc=False):
            eng = ENG_OF[ch]
            w2 = []
            for (sname, v) in waits:
                if v <= 0:
                    continue
                if sname == ch and ch not in ("D", "A"):
                    continue

                if v > seen[eng].get(sname, 0):
                    seen[eng][sname] = v
                    w2.append((sname, v))
            amt = 0
            if inc:
                amt = 16 if ch in ("D", "A") else 1
                cnt[ch] += amt
            ops[eng].append((fn, w2, sem[ch] if inc else None, amt))
            return cnt[ch]

        last_tr = [0]
        TABW = [0]

        def dma(out_ap, in_ap, waits=(), transpose=False, ch="D"):
            w = list(waits)
            if transpose:
                w.append(("D", cnt["D"]))
                w.append(("A", cnt["A"]))
            elif last_tr[0] > 0:
                w.append(("D", last_tr[0]))
            d = emit(
                ch,
                lambda e, o=out_ap, i=in_ap, tr=transpose: e.dma_start(out=o, in_=i, transpose=tr),
                w, inc=True)
            if transpose:
                last_tr[0] = d
            return d

        def bf3(off, span, b, r):
            return BF[:, off:off + span].rearrange("p (b r) -> p b r", b=b, r=r)

        PREP_SLOT = [SP_PREP, SP_EX]

        class Prep:
            def __init__(self):
                self.qi = 0
                self.v_q = {}
                self.d_tr = {}
                self.onready = {}
                self.pending = []

            def _emit_load(self, src_ap, extra_ld_waits):
                qi = self.qi + len(self.pending)
                par = qi % 2
                w = list(extra_ld_waits)
                if qi >= 2:
                    w.append(("V", self.v_q[qi - 2]))
                d_ld = dma(FA[:, WS + 1024 * par:WS + 1024 * par + 1024], src_ap, waits=w)
                return d_ld, par

            def push(self, src_ap, kind, sg, oap, extra_tr_waits=(), extra_ld_waits=(),
                     on_ready=None):
                uidx = self.qi + len(self.pending)
                if on_ready is not None:
                    self.onready[uidx] = on_ready
                d_ld, par = self._emit_load(src_ap, extra_ld_waits)
                self.pending.append((d_ld, par, kind, sg, oap, list(extra_tr_waits)))
                if len(self.pending) > (0 if "p" in KSER else 1):
                    self._pop()

            def _pop(self):
                d_ld, par, kind, sg, oap, extra_tr = self.pending.pop(0)
                qi = self.qi
                src = FA[:, WS + 1024 * par:WS + 1024 * par + 1024]
                t1 = BF[:, PREP_SLOT[par]:PREP_SLOT[par] + 1024]
                t2 = BF[:, SP_T2:SP_T2 + 1024]
                wq_ = [("D", d_ld)]
                if qi >= 2:
                    wq_.append(("D", self.d_tr[qi - 2]))
                if kind == "cast":
                    self.v_q[qi] = emit("V", lambda e, o=t1, i=src: e.tensor_copy(o, i),
                                        wq_, inc=True)
                else:
                    emit("V", lambda e, o=t1, i=src, g=sg:
                         e.tensor_scalar(o, i, g, 0.5, ALU.mult, ALU.is_ge), wq_)
                    emit("V", lambda e, o=t2, i=src, g=sg:
                         e.tensor_scalar(o, i, g, -0.5, ALU.mult, ALU.is_le))
                    self.v_q[qi] = emit("V", lambda e, o=t1, a=t1, b=t2:
                                        e.tensor_sub(o, a, b), inc=True)
                d_tr = dma(oap, t1, waits=[("V", self.v_q[qi]), ("S", TABW[0])] + extra_tr,
                           transpose=True)
                self.d_tr[qi] = d_tr
                if qi in self.onready:
                    self.onready.pop(qi)(d_tr)
                self.qi += 1
                return d_tr

            def flush(self):
                d = 0
                while self.pending:
                    d = self._pop()
                return d

        prep = Prep()

        KPH = int(os.environ.get("KPH", "9"))
        KSER = os.environ.get("KSER", "")

        def finish():
            final_d = cnt["D"]
            ops["sync"].append((lambda e: None, [("D", final_d)], None, 0))
            import kernel as _km
            _km._SCHED = {"ops": {e: [(w, _semname(sm), amt)
                                      for (fn, w, sm, amt) in ops[e]] for e in ops}}

            def emit_block(eng_name, e):
                for (fn, waits, s_, amt) in ops[eng_name]:
                    for (sname, v) in waits:
                        e.wait_ge(sem[sname], v)
                    inst = fn(e)
                    if s_ is not None:
                        inst.then_inc(s_, amt)

            @block.sync
            def _(e):
                emit_block("sync", e)

            @block.tensor
            def _(e):
                emit_block("tensor", e)

            @block.vector
            def _(e):
                emit_block("vector", e)

            @block.scalar
            def _(e):
                emit_block("scalar", e)

            @block.gpsimd
            def _(e):
                emit_block("gpsimd", e)

        d_const = dma(FA[:, SELo:SELo + CONST_COLS], consts)
        d_bv = dma(FA[0:1, BVo:BVo + C], bvr)
        d_cbf = dma(BF[:, SP_I:SP_I + 256], cbf)
        v_ones = emit("V", lambda e: e.memset(FA[0:1, ONERo:ONERo + 512], 1.0), inc=True)
        v_ones = emit("V", lambda e: e.memset(FA[:, ONECOLo:ONECOLo + 1], 1.0), inc=True)
        v_ones = emit("V", lambda e: e.memset(BF[:, SP_ONE:SP_ONE + 16], 1.0), inc=True)
        vone_ap = BF[:, VV:VV + 16640].rearrange("p (t h d) -> p t h d", t=16, h=16, d=65)[:, :, :, 64:65]
        v_ones = emit("V", lambda e: e.memset(vone_ap, 1.0), inc=True)

        s_tab = emit("S", lambda e: e.activation(FA[0:1, GT + 6:GT + 7],
                                                 FA[0:1, ONECOLo:ONECOLo + 1], AF.Ln),
                     [("V", v_ones)], inc=True)
        TABW[0] = s_tab

        ONES_BF = BF[:, SP_ONE:SP_ONE + 1]
        ONER128 = FA[0:1, ONERo:ONERo + 128]

        def ln_phase(src_dram, first, hook=None):
            t_before = cnt["T"]
            d_x = [0] * NCH
            v_st = [0] * NCH
            for c in range(NCH):
                w = []
                if c >= 2:
                    w.append(("V", v_st[c - 2]))
                d_x[c] = dma(FA[:, G0 + 2048 * (c % 2):G0 + 2048 * (c % 2) + 2048],
                             src_dram[:, :].rearrange("(a p) m -> a p m", a=NCH, p=128)[c],
                             waits=w)
                for g in range(4):
                    xsrc = FA[:, G0 + 2048 * (c % 2) + 512 * g:G0 + 2048 * (c % 2) + 512 * (g + 1)]
                    exb = BF[:, SP_EX:SP_EX + 512]
                    exq = BF[:, SP_EX + 512:SP_EX + 1024]
                    wv_ = [("D", d_x[c])]
                    if c > 0 or g > 0:
                        wv_.append(("T", cnt["T"]))
                    emit("V", lambda e, o=exb, i=xsrc: e.tensor_copy(o, i), wv_)
                    v_c = emit("V", lambda e, o=exq, i=exb: e.tensor_mul(o, i, i), inc=True)
                    emit("T", lambda e, o=PS[0:1, 512 * g:512 * g + 512], l=ONES_BF, r=exb,
                         s=(c == 0), p=(c == NCH - 1):
                         e.matmul(o, l, r, start=s, stop=p), [("V", v_c)])
                    t_c = emit("T", lambda e, o=PS[0:1, 2048 + 512 * g:2048 + 512 * g + 512],
                               l=ONES_BF, r=exq, s=(c == 0), p=(c == NCH - 1):
                               e.matmul(o, l, r, start=s, stop=p), [("V", v_c)], inc=True)
                v_st[c] = cnt["V"]
            t_stats = cnt["T"]
            slotA = FA[0:1, 0:2048]
            slotB = FA[0:1, 2048:4096]
            emit("V", lambda e: e.tensor_scalar(slotA, PS[0:1, 0:2048], 1.0 / C, None, ALU.mult),
                 [("T", t_stats)])
            emit("V", lambda e: e.tensor_mul(slotB, slotA, slotA))
            v_var = emit("V", lambda e: e.scalar_tensor_tensor(
                slotB, PS[0:1, 2048:4096], 1.0 / C, slotB, ALU.mult, ALU.subtract),
                [("T", t_stats)], inc=True)
            emit("V", lambda e: e.tensor_scalar(slotB, slotB, EPS_LN, None, ALU.add))
            emit("S", lambda e: e.activation(slotB, slotB, AF.Ln), [("V", v_var)])
            s_sd = emit("S", lambda e: e.activation(slotB, slotB, AF.Exp, scale=-0.5),
                        inc=True)
            if os.environ.get("KD2") == "s" and KPH == 1:
                vds = emit("V", lambda e: e.tensor_copy(FA[:, WS:WS + 2048][0:1, :], slotB),
                           [("S", s_sd)], inc=True)
                dma(out_fm[0:1, :], FA[0:1, WS:WS + 2048], waits=[("V", vds)])
                vds2 = emit("V", lambda e: e.tensor_copy(FA[:, WS:WS + 2048][1:2, :] if False else FA[0:1, WS:WS + 2048], slotA),
                            [("V", vds), ("D", cnt["D"])], inc=True)
                dma(out_fm[1:2, :], FA[0:1, WS:WS + 2048], waits=[("V", vds2)])
            v_negm = emit("V", lambda e: e.tensor_scalar(
                slotA, slotA, -1.0, None, ALU.mult), inc=True)
            t_bc = 0
            for g in range(4):
                emit("T", lambda e, o=PS[:, 512 * g:512 * g + 512],
                     r=FA[0:1, 2048 + 512 * g:2048 + 512 * (g + 1)]:
                     e.matmul(o, ONER128, r, start=True, stop=True),
                     [("V", v_negm), ("S", s_sd)])
                t_bc = emit("T", lambda e, o=PS[:, 2048 + 512 * g:2048 + 512 * g + 512],
                            r=FA[0:1, 512 * g:512 * (g + 1)]:
                            e.matmul(o, ONER128, r, start=True, stop=True),
                            [("V", v_negm)], inc=(g == 3))
            v_ap = [0] * NCH
            for c in range(NCH):
                if hook is not None:
                    hook(c)
                w = [("T", t_bc)]
                if c >= 2:
                    w.append(("V", v_ap[c - 2]))
                d_a = dma(FA[:, G0 + 2048 * (c % 2):G0 + 2048 * (c % 2) + 2048],
                          src_dram[:, :].rearrange("(a p) m -> a p m", a=NCH, p=128)[c],
                          waits=w)
                xs = FA[:, G0 + 2048 * (c % 2):G0 + 2048 * (c % 2) + 2048]
                if os.environ.get("KD2") == "q":
                    ts_ = FA[:, WS:WS + 2048]
                    emit("V", lambda e, x=xs, t_=ts_: e.tensor_add(t_, x, PS[:, 2048:4096]),
                         [("D", d_a), ("T", t_bc)])
                    v_ap[c] = emit("V", lambda e, t_=ts_, o=BF[:, XH + 2048 * c:XH + 2048 * (c + 1)]:
                                   e.tensor_mul(o, t_, PS[:, 0:2048]), inc=True)
                else:
                    emit("V", lambda e, x=xs: e.tensor_add(x, x, PS[:, 2048:4096]),
                         [("D", d_a), ("T", t_bc)])
                    v_ap[c] = emit("V", lambda e, x=xs, o=BF[:, XH + 2048 * c:XH + 2048 * (c + 1)]:
                                   e.tensor_mul(o, x, PS[:, 0:2048]), inc=True)
            return v_ap

        def push_w_unit(wdram, dst, ot, extra_tr=()):
            oap = BF[:, dst:dst + 8192].rearrange(
                "p (b r) -> p b r", b=8, r=1024)[:, :, 128 * ot:128 * ot + 128]
            prep.push(wdram[128 * ot:128 * (ot + 1), :], "cast", None, oap,
                      extra_tr_waits=extra_tr)

        v_ap1 = ln_phase(xfm, True,
                         hook=(lambda c: push_w_unit(wk, WT, c)) if "k" not in KSER else None)

        if KPH == 1:
            if os.environ.get("KD2") in ("r", "n"):
                _pcol = 0 if os.environ.get("KD2") == "r" else 2048
                vd = emit("V", lambda e, pc_=_pcol: e.tensor_copy(FA[:, G0:G0 + 2048],
                                                                  PS[:, pc_:pc_ + 2048]),
                          [("V", v_ap1[NCH - 1])], inc=True)
                dma(out_fm[:, :].rearrange("(a p) m -> a p m", a=NCH, p=128)[0],
                    FA[:, G0:G0 + 2048], waits=[("V", vd)])
            elif os.environ.get("KD2") == "t":
                orear = out_fm[:, :].rearrange("(a p) m -> a p m", a=NCH, p=128)
                for (dst, mode) in ((0, "xh"), (6, "xh"), (1, "x")):
                    if mode == "xh":
                        vd = emit("V", lambda e, o=FA[:, WS:WS + 2048],
                                  i=BF[:, XH + 2048 * 6:XH + 2048 * 7]:
                                  e.tensor_copy(o, i),
                                  [("V", v_ap1[NCH - 1]), ("D", cnt["D"])], inc=True)
                        dma(orear[dst], FA[:, WS:WS + 2048], waits=[("V", vd)])
                    else:
                        dref = dma(FA[:, WS:WS + 2048],
                                   xfm[:, :].rearrange("(a p) m -> a p m", a=NCH, p=128)[6],
                                   waits=[("D", cnt["D"]), ("V", cnt["V"])])
                        vd = emit("V", lambda e: e.tensor_copy(FA[:, G0:G0 + 2048],
                                                               FA[:, WS:WS + 2048]),
                                  [("D", dref)], inc=True)
                        dma(orear[dst], FA[:, G0:G0 + 2048], waits=[("V", vd)])
            elif os.environ.get("KD2") == "u":
                orear = out_fm[:, :].rearrange("(a p) m -> a p m", a=NCH, p=128)
                slotA_ = FA[0:1, 0:2048]
                slotB_ = FA[0:1, 2048:4096]
                items = [
                    (0, lambda e: e.tensor_copy(FA[:, WS:WS + 2048], BF[:, XH:XH + 2048])),
                    (1, lambda e: e.tensor_copy(FA[:, WS:WS + 2048], BF[:, XH + 2048 * 3:XH + 2048 * 4])),
                    (2, lambda e: e.tensor_copy(FA[:, WS:WS + 2048], BF[:, XH + 2048 * 6:XH + 2048 * 7])),
                    (3, lambda e: e.tensor_copy(FA[:, WS:WS + 2048], PS[:, 0:2048])),
                    (4, lambda e: e.tensor_copy(FA[:, WS:WS + 2048], PS[:, 2048:4096])),
                    (5, lambda e: e.tensor_copy(FA[:, WS:WS + 2048][0:1, :], slotB_)),
                    (6, lambda e: e.tensor_copy(FA[:, WS:WS + 2048][0:1, :], slotA_)),
                ]
                for dst, fn in items:
                    vd = emit("V", fn, [("V", v_ap1[NCH - 1]), ("D", cnt["D"])], inc=True)
                    dma(orear[dst], FA[:, WS:WS + 2048], waits=[("V", vd)])
            else:
                _soff = WS if os.environ.get("KD2") == "w" else G0
                for c2 in range(NCH):
                    vd = emit("V", lambda e, o=FA[:, _soff:_soff + 2048],
                              i=BF[:, XH + 2048 * c2:XH + 2048 * (c2 + 1)]:
                              e.tensor_copy(o, i),
                              [("V", v_ap1[NCH - 1]), ("D", cnt["D"])], inc=True)
                    dma(out_fm[:, :].rearrange("(a p) m -> a p m", a=NCH, p=128)[c2],
                        FA[:, _soff:_soff + 2048], waits=[("V", vd)])
        if KPH < 2:
            finish()
            return nc

        d_c1 = dma(FA[:, 0:2048], cr1, waits=[("V", v_ap1[6])])
        d_c2 = dma(FA[:, 2048:4096], cr2, waits=[("V", v_ap1[7])])

        def prep_weight(wdram, wait_wt_free_t, dst=WT):
            extra_tr = [("T", wait_wt_free_t)] if wait_wt_free_t else []
            for ot in range(8):
                oap = BF[:, dst:dst + 8192].rearrange(
                    "p (b r) -> p b r", b=8, r=1024)[:, :, 128 * ot:128 * ot + 128]
                prep.push(wdram[128 * ot:128 * (ot + 1), :],
                          "cast", None, oap, extra_tr_waits=extra_tr)
            return prep.flush()

        def fm_proj(dst_off, bias_off, d_wt, extra_first_waits, wbase=WT, hook=None):
            v_rot = [0] * NCH
            s_qb = [0] * NCH
            t_oc = [0] * NCH
            for oc in range(NCH):
                if hook is not None:
                    hook(oc)
                pcol = (oc % 2) * 2048
                w = [("D", d_wt)]
                if oc == 0:
                    w += extra_first_waits
                if oc >= 2:
                    w.append(("S", s_qb[oc - 2]))
                if "b" in KSER and oc >= 1:
                    w.append(("S", s_qb[oc - 1]))
                    w.append(("V", v_rot[oc - 1]))
                for cc in range(NCH):
                    for g in range(4):
                        last = (cc == NCH - 1 and g == 3)
                        t_oc[oc] = emit("T", lambda e,
                                        o=PS[:, pcol + 512 * g:pcol + 512 * (g + 1)],
                                        l=BF[:, wbase + 1024 * cc + 128 * oc:wbase + 1024 * cc + 128 * oc + 128],
                                        r=BF[:, XH + 2048 * cc + 512 * g:XH + 2048 * cc + 512 * (g + 1)],
                                        s=(cc == 0), p=(cc == NCH - 1):
                                        e.matmul(o, l, r, start=s, stop=p),
                                        w, inc=last)
                    w = []
                qb = BF[:, SC:SC + 2048]
                qs = BF[:, SC + 2048:SC + 4096]
                wS = [("T", t_oc[oc])]
                if oc >= 1:
                    wS.append(("V", v_rot[oc - 1]))
                s_qb[oc] = emit("S", lambda e, o=qb, i=PS[:, pcol:pcol + 2048],
                                b=FA[:, bias_off + oc:bias_off + oc + 1]:
                                e.activation(o, i, AF.Identity, bias=b), wS, inc=True)
                emit("V", lambda e, o=qs, i=qb:
                     e.stream_shuffle(o, i, mask=[(j + 16) % 32 for j in range(32)]),
                     [("S", s_qb[oc])])
                emit("V", lambda e, o=qb, i=qb: e.tensor_mul(o, i, FA[:, 0:2048]),
                     [("D", d_c1)])
                emit("V", lambda e, o=qs, i=qs: e.tensor_mul(o, i, FA[:, 2048:4096]),
                     [("D", d_c2)])
                v_rot[oc] = emit("V", lambda e, o=BF[:, dst_off + 2048 * oc:dst_off + 2048 * (oc + 1)],
                                 a=qb, b=qs: e.tensor_add(o, a, b), inc=True)
            return t_oc[NCH - 1], v_rot, s_qb

        d_wkT = prep.flush()
        t_kproj, v_rotk, s_qbk = fm_proj(
            KK, BKo, d_wkT, [("V", v_ap1[7])],
            hook=lambda oc: push_w_unit(wv, QQ, oc))
        d_wvT = prep.flush()
        t_vp = [0] * NTB
        v_vev = [0] * NTB
        for tb in range(NTB):
            if tb % 2 == 0:
                push_w_unit(wq, WT, tb // 2, extra_tr=[("T", t_kproj)])
            pcol = 2048 + 1024 * (tb % 2)
            w = [("D", d_wvT)]
            if tb < 2:
                w.append(("S", s_qbk[NCH - 1]))
            if tb >= 2:
                w.append(("V", v_vev[tb - 2]))
            if "b" in KSER and tb >= 1:
                w.append(("V", v_vev[tb - 1]))
            for cc in range(NCH):
                for g in range(2):
                    emit("T", lambda e, o=PS[:, pcol + 512 * g:pcol + 512 * (g + 1)],
                         l=BF[:, XH + 2048 * cc + 128 * tb:XH + 2048 * cc + 128 * tb + 128],
                         r=BF[:, QQ + 1024 * cc + 512 * g:QQ + 1024 * cc + 512 * (g + 1)],
                         s=(cc == 0): e.matmul(o, l, r, start=s, stop=False), w)
                w = []
            for g in range(2):
                t_vp[tb] = emit("T", lambda e, o=PS[:, pcol + 512 * g:pcol + 512 * (g + 1)],
                                l=ONER128, r=FA[0:1, BVo + 512 * g:BVo + 512 * (g + 1)]:
                                e.matmul(o, l, r, start=False, stop=True),
                                [("D", d_bv)], inc=(g == 1))
            oap = BF[:, VV + 1040 * tb:VV + 1040 * tb + 1040].rearrange(
                "p (h d) -> p h d", h=16, d=65)[:, :, 0:64]
            iap = PS[:, pcol:pcol + 1024].rearrange("p (h d) -> p h d", h=16, d=64)
            v_vev[tb] = emit("V", lambda e, o=oap, i=iap: e.tensor_copy(o, i),
                             [("T", t_vp[tb])], inc=True)
        d_wqT = prep.flush()
        t_qproj, v_rotq, s_qbq = fm_proj(
            QQ, BQo, d_wqT, [("V", v_vev[NTB - 1]), ("S", s_qbk[NCH - 1])])

        if KPH == 2:
            for c2 in range(NCH):
                vd = emit("V", lambda e, o=FA[:, G0:G0 + 2048],
                          i=BF[:, (KK if os.environ.get("KD2") is None else (XH if os.environ.get("KD2") == "x" else WT)) + 2048 * c2:(KK if os.environ.get("KD2") is None else (XH if os.environ.get("KD2") == "x" else WT)) + 2048 * (c2 + 1)]:
                          e.tensor_copy(o, i),
                          [("V", v_rotq[NCH - 1]), ("D", cnt["D"])], inc=True)
                dma(out_fm[:, :].rearrange("(a p) m -> a p m", a=NCH, p=128)[c2],
                    FA[:, G0:G0 + 2048], waits=[("V", vd)])

        if KPH < 3:
            finish()
            return nc
        d_wpT = prep_weight(wp, t_qproj)

        _vafter_preps = cnt["V"]
        gstate = {"gi": 0, "v_gprev": [_vafter_preps, _vafter_preps], "v_gred": 0}

        def gamma_unit():
            gi = gstate["gi"]
            if gi >= 64:
                return
            w = [("V", gstate["v_gprev"][gi % 2])]
            if gi < 64 - 32:
                iap = w1[128 * gi:128 * (gi + 1), :]
            else:
                tt = gi - 32
                iap = w2[:, :].rearrange("(a p) (b m) -> a b p m", p=128, m=1024)[tt // 4, tt % 4]
            d_g = dma(FA[:, WS + 1024 * (gi % 2):WS + 1024 * (gi % 2) + 1024], iap, waits=w)
            v = emit("V", lambda e, o=FA[:, GACCo + gi:GACCo + gi + 1],
                     i=FA[:, WS + 1024 * (gi % 2):WS + 1024 * (gi % 2) + 1024]:
                     e.tensor_reduce(o, i, AX.X, ALU.add, apply_absolute_value=True),
                     [("D", d_g)], inc=True)
            gstate["v_gprev"][gi % 2] = v
            gstate["v_gred"] = v
            gstate["gi"] = gi + 1

        emit("G", lambda e: e.memset(BF[64:128, SC:SC + 256], 0.0),
             [("V", v_rotq[NCH - 1]), ("V", v_rotk[NCH - 1])])
        emit("G", lambda e: e.memset(BF[0:64, SC + 256:SC + 512], 0.0), inc=True)
        kq_slot_t = {}
        g_kp = {}

        def kpad_push(h2, i2):
            pb2 = 64 * (h2 % 2)
            kb = SC + 256 * (h2 % 2) + 128 * (i2 % 2)
            w = [("T", kq_slot_t.get((h2 % 2, i2 % 2), 0))]
            g_kp[(h2, i2)] = emit(
                "G", lambda e, o=BF[pb2:pb2 + 64, kb:kb + 128],
                i=BF[pb2:pb2 + 64, KK + 2048 * (h2 // 2) + 128 * i2:
                     KK + 2048 * (h2 // 2) + 128 * i2 + 128]:
                e.tensor_copy(o, i), w, inc=True)

        pc = 0
        s_exp = {}
        t_pv_last = 0
        v_yev = [0] * H
        v_sum = [0] * H
        d_sumst = [0] * H
        pending_pv = [None]

        def flush_pv():
            nonlocal_t = pending_pv[0]
            if nonlocal_t is not None:
                nonlocal_t()
                pending_pv[0] = None

        for h in range(H):
            ch = h // 2
            pb = 64 * (h % 2)
            kpad_push(h, 0)
            kpad_push(h, 1)
            for _ in range(4):
                gamma_unit()
            for i in range(NTB):
                q0 = 128 * i
                kbase = SC + 256 * (h % 2) + 128 * (i % 2)
                Nq = T - q0
                pieces = [(q0, min(1024, Nq))]
                if Nq > 1024:
                    pieces.append((q0 + 1024, Nq - 1024))
                for pi, (qoff, plen) in enumerate(pieces):
                    par = pc % 2
                    stc = 1024 * par
                    wT_ = []
                    if pc >= 2:
                        wT_.append(("S", s_exp[pc - 2]))
                    if "c" in KSER and pc >= 1:
                        wT_.append(("S", s_exp[pc - 1]))
                        wT_.append(("T", t_pv_last))
                    if pi == 0:
                        wT_.append(("G", g_kp[(h, i)]))
                    if h == 0 and i == 0 and pi == 0:
                        wT_.append(("V", v_rotq[NCH - 1]))
                        wT_.append(("V", v_rotk[NCH - 1]))
                    t_kq = 0
                    ng = (plen + 511) // 512
                    for g in range(ng):
                        gw = min(512, plen - 512 * g)
                        diag = (pi == 0 and g == 0)
                        t_kq = emit("T", lambda e, o=PS[:, stc + 512 * g:stc + 512 * g + gw],
                                    l=BF[:, kbase:kbase + 128],
                                    r=BF[:, QQ + 2048 * ch + qoff + 512 * g:
                                         QQ + 2048 * ch + qoff + 512 * g + gw],
                                    p=(not diag):
                                    e.matmul(o, l, r, start=True, stop=p),
                                    wT_, inc=(g == ng - 1 and not diag))
                        wT_ = []
                        if diag:
                            t_kq = emit("T", lambda e, o=PS[:, stc:stc + 128],
                                        l=BF[:, SP_I:SP_I + 128],
                                        r=BF[:, SP_I + 128:SP_I + 256]:
                                        e.matmul(o, l, r, start=False, stop=True),
                                        [("D", d_cbf)], inc=(g == ng - 1))
                    s_exp[pc] = emit("S", lambda e, o=BF[:, TPc + 1024 * par:TPc + 1024 * par + plen],
                                     i=PS[:, stc:stc + plen]:
                                     e.activation(o, i, AF.Exp, scale=0.125),
                                     [("T", t_kq)], inc=True)

                    def make_pv(pc_=pc, par_=par, qoff_=qoff, plen_=plen, i_=i, h_=h, pi_=pi):
                        def do():
                            nonlocal t_pv_last
                            wPV = [("S", s_exp[pc_])]
                            if i_ == 0 and pi_ == 0 and h_ > 0:
                                wPV.append(("V", v_sum[h_ - 1]))
                            off = 0
                            while off < plen_:
                                gw = min(512 - ((qoff_ + off) % 512), plen_ - off)
                                if pi_ == 0 and off == 0:
                                    gw = min(gw, 128)
                                stp = (pi_ == 0 and off == 0 and (i_ % 4) == 3)
                                st_ = (i_ == 0 and ((qoff_ + off) % 512) == 0)
                                t_pv_last = emit("T", lambda e,
                                                 o=PS[0:65, 2048 + qoff_ + off:2048 + qoff_ + off + gw],
                                                 l=BF[:, VV + 1040 * i_ + 65 * h_:VV + 1040 * i_ + 65 * h_ + 65],
                                                 r=BF[:, TPc + 1024 * par_ + off:TPc + 1024 * par_ + off + gw],
                                                 s=st_, p=stp:
                                                 e.matmul(o, l, r, start=s, stop=p),
                                                 wPV, inc=(off + gw >= plen_))
                                wPV = []
                                off += gw
                        return do

                    flush_pv()
                    pending_pv[0] = make_pv()
                    pc += 1
                kq_slot_t[(h % 2, i % 2)] = cnt["T"]
                if i + 2 < NTB:
                    kpad_push(h, i + 2)
            flush_pv()
            wY = [("T", t_pv_last)]
            if h == 0:
                wY.append(("T", t_qproj))
            v_yev[h] = emit("V", lambda e, o=BF[pb:pb + 64, XH + 2048 * ch:XH + 2048 * (ch + 1)],
                            i=PS[0:64, 2048:4096]: e.tensor_copy(o, i), wY, inc=True)
            wSm = []
            if h >= 1:
                wSm.append(("D", d_sumst[h - 1]))
            v_sum[h] = emit("V", lambda e, o=FA[0:1, 0:2048], i=PS[64:65, 2048:4096]:
                            e.tensor_copy(o, i), wSm, inc=True)
            d_sumst[h] = dma(sums_d[h:h + 1, :], FA[0:1, 0:2048], waits=[("V", v_sum[h])])

        if KPH == 3:
            for c2 in range(NCH):
                vd = emit("V", lambda e, o=FA[:, G0 + 2048:G0 + 4096],
                          i=BF[:, XH + 2048 * c2:XH + 2048 * (c2 + 1)]:
                          e.tensor_copy(o, i),
                          [("V", v_sum[H - 1]), ("D", cnt["D"])], inc=True)
                dma(out_fm[:, :].rearrange("(a p) m -> a p m", a=NCH, p=128)[c2],
                    FA[:, G0 + 2048:G0 + 4096], waits=[("V", vd)])

        d_sumld = dma(FA[0:16, 0:2048], sums_d, waits=[("D", d_sumst[H - 1])])
        v_inv = emit("V", lambda e: e.reciprocal(FA[0:16, 0:2048], FA[0:16, 0:2048]),
                     [("D", d_sumld)], inc=True)
        t_selbc = [0] * NCH
        v_ynorm = [0] * NCH
        for c2 in range(NCH):
            pcol = (c2 % 2) * 2048
            w = [("V", v_inv), ("D", d_const)]
            if c2 >= 2:
                w.append(("V", v_ynorm[c2 - 2]))
            for g in range(4):
                t_selbc[c2] = emit("T", lambda e, o=PS[:, pcol + 512 * g:pcol + 512 * (g + 1)],
                                   l=FA[0:16, SELo + 128 * c2:SELo + 128 * (c2 + 1)],
                                   r=FA[0:16, 512 * g:512 * (g + 1)]:
                                   e.matmul(o, l, r, start=True, stop=True), w, inc=(g == 3))
                w = []
            v_ynorm[c2] = emit("V", lambda e, o=BF[:, XH + 2048 * c2:XH + 2048 * (c2 + 1)],
                               i=PS[:, pcol:pcol + 2048]:
                               e.tensor_mul(o, o, i),
                               [("T", t_selbc[c2]), ("V", v_yev[2 * c2 + 1])], inc=True)

        if KPH < 4:
            finish()
            return nc
        while gstate["gi"] < 64:
            gamma_unit()
        emit("V", lambda e: e.tensor_reduce(FA[:, GT:GT + 1], FA[:, GACCo:GACCo + 32], AX.X, ALU.add),
             [("V", gstate["v_gred"])])
        v_g2 = emit("V", lambda e: e.tensor_reduce(FA[:, GT + 1:GT + 2], FA[:, GACCo + 32:GACCo + 64],
                                                   AX.X, ALU.add), inc=True)
        emit("T", lambda e: e.matmul(PS[0:1, 0:1], FA[:, ONECOLo:ONECOLo + 1], FA[:, GT:GT + 1],
                                     start=True, stop=True),
             [("V", v_g2), ("V", v_ynorm[NCH - 1])])
        t_gm = emit("T", lambda e: e.matmul(PS[0:1, 1:2], FA[:, ONECOLo:ONECOLo + 1],
                                            FA[:, GT + 1:GT + 2], start=True, stop=True),
                    [("V", v_g2)], inc=True)
        emit("V", lambda e: e.tensor_scalar(FA[0:1, GT + 4:GT + 5], PS[0:1, 0:1],
                                            1.0 / (FF * C), EPS_Q, ALU.mult, ALU.add),
             [("T", t_gm)])
        emit("V", lambda e: e.tensor_scalar(FA[0:1, GT + 5:GT + 6], PS[0:1, 1:2],
                                            1.0 / (FF * C), EPS_Q, ALU.mult, ALU.add))
        emit("V", lambda e: e.reciprocal(FA[0:1, GT + 4:GT + 5], FA[0:1, GT + 4:GT + 5]))
        v_gi = emit("V", lambda e: e.reciprocal(FA[0:1, GT + 5:GT + 6], FA[0:1, GT + 5:GT + 6]),
                    inc=True)
        emit("T", lambda e: e.matmul(PS[:, 4:5], ONER128, FA[0:1, GT + 4:GT + 5],
                                     start=True, stop=True), [("V", v_gi)])
        t_gb = emit("T", lambda e: e.matmul(PS[:, 5:6], ONER128, FA[0:1, GT + 5:GT + 6],
                                            start=True, stop=True), inc=True)
        emit("V", lambda e: e.tensor_copy(FA[:, GT + 2:GT + 3], PS[:, 4:5]), [("T", t_gb)])
        v_sg = emit("V", lambda e: e.tensor_copy(FA[:, GT + 3:GT + 4], PS[:, 5:6]), inc=True)
        SG1 = FA[:, GT + 2:GT + 3]
        SG2 = FA[:, GT + 3:GT + 4]

        t_pp = [0] * NCH
        v_res = [0] * NCH
        d_x1w = [0] * NCH
        for oc in range(NCH):
            pcol = (oc % 2) * 2048
            w = [("D", d_wpT), ("V", v_sg), ("V", v_ynorm[NCH - 1])]
            if oc >= 2:
                w.append(("V", v_res[oc - 2]))
            if "d" in KSER and oc >= 1:
                w.append(("V", v_res[oc - 1]))
            for cc in range(NCH):
                for g in range(4):
                    t_pp[oc] = emit("T", lambda e, o=PS[:, pcol + 512 * g:pcol + 512 * (g + 1)],
                                    l=BF[:, WT + 1024 * cc + 128 * oc:WT + 1024 * cc + 128 * oc + 128],
                                    r=BF[:, XH + 2048 * cc + 512 * g:XH + 2048 * cc + 512 * (g + 1)],
                                    s=(cc == 0), p=(cc == NCH - 1): e.matmul(o, l, r, start=s, stop=p),
                                    w, inc=(cc == NCH - 1 and g == 3))
                    w = []
            wD = [("T", t_selbc[NCH - 1]), ("V", v_ynorm[NCH - 1])]
            if oc >= 2:
                wD.append(("V", v_res[oc - 2]))
            d_xr = dma(FA[:, G0 + 2048 * (oc % 2):G0 + 2048 * (oc % 2) + 2048],
                       xfm[:, :].rearrange("(a p) m -> a p m", a=NCH, p=128)[oc], waits=wD)
            xs = FA[:, G0 + 2048 * (oc % 2):G0 + 2048 * (oc % 2) + 2048]
            v_res[oc] = emit("V", lambda e, x=xs, i=PS[:, pcol:pcol + 2048],
                             b=FA[:, BPo + oc:BPo + oc + 1]:
                             e.scalar_tensor_tensor(x, i, b, x, ALU.add, ALU.add),
                             [("T", t_pp[oc]), ("D", d_xr)], inc=True)
            d_x1w[oc] = dma(x1d[:, :].rearrange("(a p) m -> a p m", a=NCH, p=128)[oc], xs,
                            waits=[("V", v_res[oc])])

        ln2_gate_t = t_pp[NCH - 1]
        ln2_gate_v = v_res[NCH - 1]
        v_ap2 = None

        def ln_phase2():
            d_x = [0] * NCH
            v_st = [0] * NCH
            for c in range(NCH):
                w = [("D", d_x1w[c])]
                if c >= 2:
                    w.append(("V", v_st[c - 2]))
                d_x[c] = dma(FA[:, G0 + 2048 * (c % 2):G0 + 2048 * (c % 2) + 2048],
                             x1d[:, :].rearrange("(a p) m -> a p m", a=NCH, p=128)[c], waits=w)
                for g in range(4):
                    xsrc = FA[:, G0 + 2048 * (c % 2) + 512 * g:G0 + 2048 * (c % 2) + 512 * (g + 1)]
                    exb = BF[:, SP_EX:SP_EX + 512]
                    exq = BF[:, SP_EX + 512:SP_EX + 1024]
                    wv_ = [("D", d_x[c]), ("T", ln2_gate_t)]
                    if c > 0 or g > 0:
                        wv_.append(("T", cnt["T"]))
                    emit("V", lambda e, o=exb, i=xsrc: e.tensor_copy(o, i), wv_)
                    v_c = emit("V", lambda e, o=exq, i=exb: e.tensor_mul(o, i, i), inc=True)
                    emit("T", lambda e, o=PS[0:1, 512 * g:512 * g + 512], l=ONES_BF, r=exb,
                         s=(c == 0), p=(c == NCH - 1):
                         e.matmul(o, l, r, start=s, stop=p), [("V", v_c), ("V", ln2_gate_v)])
                    emit("T", lambda e, o=PS[0:1, 2048 + 512 * g:2048 + 512 * g + 512],
                         l=ONES_BF, r=exq, s=(c == 0), p=(c == NCH - 1):
                         e.matmul(o, l, r, start=s, stop=p), [("V", v_c)], inc=True)
                v_st[c] = cnt["V"]
            t_stats = cnt["T"]
            slotA = FA[0:1, 0:2048]
            slotB = FA[0:1, 2048:4096]
            emit("V", lambda e: e.tensor_scalar(slotA, PS[0:1, 0:2048], 1.0 / C, None, ALU.mult),
                 [("T", t_stats)])
            emit("V", lambda e: e.tensor_mul(slotB, slotA, slotA))
            v_var = emit("V", lambda e: e.scalar_tensor_tensor(
                slotB, PS[0:1, 2048:4096], 1.0 / C, slotB, ALU.mult, ALU.subtract),
                [("T", t_stats)], inc=True)
            emit("V", lambda e: e.tensor_scalar(slotB, slotB, EPS_LN, None, ALU.add))
            emit("S", lambda e: e.activation(slotB, slotB, AF.Ln), [("V", v_var)])
            s_sd = emit("S", lambda e: e.activation(slotB, slotB, AF.Exp, scale=-0.5),
                        inc=True)
            v_negm = emit("V", lambda e: e.tensor_scalar(
                slotA, slotA, -1.0, None, ALU.mult), inc=True)
            t_bc = 0
            for g in range(4):
                emit("T", lambda e, o=PS[:, 512 * g:512 * g + 512],
                     r=FA[0:1, 2048 + 512 * g:2048 + 512 * (g + 1)]:
                     e.matmul(o, ONER128, r, start=True, stop=True),
                     [("V", v_negm), ("S", s_sd)])
                t_bc = emit("T", lambda e, o=PS[:, 2048 + 512 * g:2048 + 512 * g + 512],
                            r=FA[0:1, 512 * g:512 * (g + 1)]:
                            e.matmul(o, ONER128, r, start=True, stop=True),
                            [("V", v_negm)], inc=(g == 3))
            v_ap = [0] * NCH
            for c in range(NCH):
                w = [("T", t_bc)]
                if c >= 2:
                    w.append(("V", v_ap[c - 2]))
                d_a = dma(FA[:, G0 + 2048 * (c % 2):G0 + 2048 * (c % 2) + 2048],
                          x1d[:, :].rearrange("(a p) m -> a p m", a=NCH, p=128)[c], waits=w)
                xs = FA[:, G0 + 2048 * (c % 2):G0 + 2048 * (c % 2) + 2048]
                emit("V", lambda e, x=xs: e.tensor_add(x, x, PS[:, 2048:4096]),
                     [("D", d_a), ("T", t_bc)])
                v_ap[c] = emit("V", lambda e, x=xs, o=BF[:, XH + 2048 * c:XH + 2048 * (c + 1)]:
                               e.tensor_mul(o, x, PS[:, 0:2048]), inc=True)
            return v_ap

        v_ap2 = ln_phase2()

        d_w2rdy = [0] * 32
        d_w1rdy = [0] * 32

        def mk_rdy2(sl):
            def f(d):
                for b_ in range(8):
                    d_w2rdy[8 * sl + b_] = d
            return f

        def mk_rdy1(fc):
            def f(d):
                d_w1rdy[fc] = d
            return f

        first_w2 = [("V", gstate["v_gred"])]
        for sl in range(4):
            for dt in range(8):
                base = w2base(8 * sl)
                oap = BF[:, base:base + 8192].rearrange(
                    "p (b r) -> p b r", b=8, r=1024)[:, :, 128 * dt:128 * dt + 128]
                ex = [("T", t_pv_last)]
                if sl == 2:
                    ex.append(("T", t_pp[NCH - 1]))
                prep.push(w2[128 * dt:128 * (dt + 1), 1024 * sl:1024 * (sl + 1)],
                          "quant", SG2, oap, extra_tr_waits=ex,
                          extra_ld_waits=first_w2,
                          on_ready=mk_rdy2(sl) if dt == 7 else None)
                first_w2 = []
            for fc in range(8 * sl, 8 * sl + 8):
                oap = BF[:, QQ:QQ + 32768].rearrange(
                    "p (b r) -> p b r", b=8, r=4096)[:, :, 128 * fc:128 * fc + 128]
                prep.push(w1[128 * fc:128 * (fc + 1), :],
                          "quant", SG1, oap, extra_tr_waits=[("T", t_pv_last)],
                          on_ready=mk_rdy1(fc))
        prep.flush()

        t_mm1 = {}
        s_relu = {}
        t_mm2 = {}
        v_out = [0] * 8
        d_out = [0] * 8
        pending_mm2 = [None]

        def flush_mm2():
            if pending_mm2[0] is not None:
                pending_mm2[0]()
                pending_mm2[0] = None

        for tp in range(8):
            w = [("T", t_selbc[NCH - 1])]
            if tp >= 2:
                w.append(("V", v_out[tp - 2]))
            if tp < 2:
                w.append(("V", v_ap2[NCH - 1]))
                w.append(("D", d_x1w[NCH - 1]))
            d_x1p = dma(FA[:, G0 + 2048 * (tp % 2):G0 + 2048 * (tp % 2) + 2048].rearrange(
                "p (a m) -> p a m", a=8, m=256),
                x1d[:, :].rearrange("(a p) m -> p a m", a=NCH, p=128)[:, :, 256 * tp:256 * (tp + 1)],
                waits=w)
            for fc in range(32):
                hcol = 512 * (fc % 2)
                wmm1 = [("D", d_w1rdy[fc]), ("V", v_ap2[NCH - 1])]
                prev = (tp, fc - 2) if fc >= 2 else ((tp - 1, 30 + fc) if tp >= 1 else None)
                if prev in s_relu:
                    wmm1.append(("S", s_relu[prev]))
                for cc in range(NCH):
                    t_mm1[(tp, fc)] = emit("T", lambda e, o=PS[:, hcol:hcol + 256],
                                           l=BF[:, QQ + 4096 * cc + 128 * fc:QQ + 4096 * cc + 128 * fc + 128],
                                           r=BF[:, XH + 2048 * cc + 256 * tp:XH + 2048 * cc + 256 * (tp + 1)],
                                           s=(cc == 0), p=(cc == NCH - 1): e.matmul(o, l, r, start=s, stop=p),
                                           wmm1, inc=(cc == NCH - 1))
                    wmm1 = []
                s_relu[(tp, fc)] = emit("S", lambda e, o=BF[:, SP_H + 256 * (fc % 2):SP_H + 256 * (fc % 2) + 256],
                                        i=PS[:, hcol:hcol + 256], b=FA[:, B1o + fc:B1o + fc + 1]:
                                        e.activation(o, i, AF.Relu, bias=b),
                                        [("T", t_mm1[(tp, fc)])], inc=True)
                def make_mm2(tp_=tp, fc_=fc):
                    def do():
                        wmm2 = [("S", s_relu[(tp_, fc_)]), ("D", d_w2rdy[fc_])]
                        if fc_ == 0 and tp_ >= 1:
                            wmm2.append(("V", v_out[tp_ - 1]))
                        for dc in range(NCH):
                            st_ = (fc_ == 0 and dc % 2 == 0)
                            t_mm2[(tp_, fc_)] = emit(
                                "T", lambda e, o=PS[:, 2048 + 256 * dc:2048 + 256 * (dc + 1)],
                                l=BF[:, w2base(fc_) + 128 * dc:w2base(fc_) + 128 * dc + 128],
                                r=BF[:, SP_H + 256 * (fc_ % 2):SP_H + 256 * (fc_ % 2) + 256],
                                s=st_, p=(fc_ == 31 and dc % 2 == 1):
                                e.matmul(o, l, r, start=s, stop=p),
                                wmm2, inc=(dc == NCH - 1))
                            wmm2[:] = []
                    return do

                if "f" in KSER:
                    pending_mm2[0] = make_mm2()
                    flush_mm2()
                else:
                    flush_mm2()
                    pending_mm2[0] = make_mm2()
            flush_mm2()
            for dc in range(NCH):
                xs = FA[:, G0 + 2048 * (tp % 2) + 256 * dc:G0 + 2048 * (tp % 2) + 256 * (dc + 1)]
                v_out[tp] = emit("V", lambda e, x=xs, i=PS[:, 2048 + 256 * dc:2048 + 256 * (dc + 1)],
                                 b=FA[:, B2o + dc:B2o + dc + 1]:
                                 e.scalar_tensor_tensor(x, i, b, x, ALU.add, ALU.add),
                                 [("T", t_mm2[(tp, 31)]), ("D", d_x1p)], inc=(dc == NCH - 1))
            d_out[tp] = dma(out_fm[:, :].rearrange("(a p) m -> p a m", a=NCH, p=128)[
                :, :, 256 * tp:256 * (tp + 1)],
                FA[:, G0 + 2048 * (tp % 2):G0 + 2048 * (tp % 2) + 2048].rearrange(
                    "p (a m) -> p a m", a=8, m=256),
                waits=[("V", v_out[tp])])

        finish()
    return nc


def _perm():
    perm = np.zeros(C, dtype=np.int64)
    for g in range(C):
        chk, p = g // 128, g % 128
        h = 2 * chk + p // 64
        lp = p % 64
        quad = lp // 32
        r = lp % 32
        if r < 16:
            i_idx, par = r + 16 * quad, 0
        else:
            i_idx, par = (r - 16) + 16 * quad, 1
        perm[g] = 64 * h + 2 * i_idx + par
    return perm


def _rope_tables():
    i = np.arange(32, dtype=np.float32)
    theta = (np.float32(1.0) / (np.float32(10000.0) ** (np.float32(2.0) * i / np.float32(64)))).astype(np.float32)
    t = np.arange(T, dtype=np.float32)[:, None]
    ang = (t * theta[None, :]).astype(np.float32)
    cos = np.cos(ang.astype(np.float64)).astype(np.float32)
    sin = np.sin(ang.astype(np.float64)).astype(np.float32)
    C1 = np.zeros((128, T), np.float32)
    C2 = np.zeros((128, T), np.float32)
    for p in range(128):
        lp = p % 64
        quad = lp // 32
        r = lp % 32
        if r < 16:
            i_idx, is_even = r + 16 * quad, True
        else:
            i_idx, is_even = (r - 16) + 16 * quad, False
        C1[p] = cos[:, i_idx]
        C2[p] = (-sin[:, i_idx]) if is_even else sin[:, i_idx]
    return C1, C2


def _consts_pack(b1, b2, bq_p, bk_p, bp):
    A = np.zeros((128, CONST_COLS), np.float32)
    for c2 in range(NCH):
        for j in range(128):
            h = 2 * c2 + (1 if j >= 64 else 0)
            A[h, 128 * c2 + j] = 1.0
    for p in range(128):
        A[p, 1024:1024 + p] = NEG
    A[:, 1152:1184] = b1.reshape(32, 128).T
    A[:, 1184:1192] = b2.reshape(8, 128).T
    A[:, 1192:1200] = bq_p.reshape(8, 128).T
    A[:, 1200:1208] = bk_p.reshape(8, 128).T
    A[:, 1208:1216] = bp.reshape(8, 128).T
    return A


_NC_CACHE = {}


def kernel(L0_attn_x, L0_attn_wq, L0_attn_bq, L0_attn_wk, L0_attn_bk,
           L0_attn_wv, L0_attn_bv, L0_attn_wp, L0_attn_bp,
           L0_ffn_W1, L0_ffn_b1, L0_ffn_W2, L0_ffn_b2):
    x = np.asarray(L0_attn_x, dtype=np.float32)
    perm = _perm()
    wq_p = np.ascontiguousarray(np.asarray(L0_attn_wq, np.float32)[perm])
    wk_p = np.ascontiguousarray(np.asarray(L0_attn_wk, np.float32)[perm])
    bq_p = np.ascontiguousarray(np.asarray(L0_attn_bq, np.float32)[perm])
    bk_p = np.ascontiguousarray(np.asarray(L0_attn_bk, np.float32)[perm])
    wv = np.ascontiguousarray(np.asarray(L0_attn_wv, np.float32))
    wp = np.ascontiguousarray(np.asarray(L0_attn_wp, np.float32))
    w1 = np.ascontiguousarray(np.asarray(L0_ffn_W1, np.float32))
    w2 = np.ascontiguousarray(np.asarray(L0_ffn_W2, np.float32))
    C1, C2 = _rope_tables()
    consts = _consts_pack(np.asarray(L0_ffn_b1, np.float32), np.asarray(L0_ffn_b2, np.float32),
                          bq_p, bk_p, np.asarray(L0_attn_bp, np.float32))
    bvr = np.ascontiguousarray(np.asarray(L0_attn_bv, np.float32).reshape(1, C))
    cbf_host = np.zeros((128, 256), dtype=ml_dtypes.bfloat16)
    cbf_host[:, 0:128] = np.eye(128, dtype=np.float32).astype(ml_dtypes.bfloat16)
    mtri = np.zeros((128, 128), np.float32)
    for p in range(128):
        mtri[p, :p] = NEG
    cbf_host[:, 128:256] = mtri.astype(ml_dtypes.bfloat16)

    if "nc" not in _NC_CACHE:
        _NC_CACHE["nc"] = build_nc()
    nc = _NC_CACHE["nc"]

    in_maps = []
    for b in range(8):
        in_maps.append({
            "xfm": np.ascontiguousarray(x[b].T),
            "wq": wq_p, "wk": wk_p, "wv": wv, "wp": wp,
            "w1": w1, "w2": w2,
            "cr1": C1, "cr2": C2, "consts": consts, "bvr": bvr, "cbf": cbf_host,
        })
    res = run_bass_kernel_spmd(nc, in_maps, core_ids=list(range(8)))
    out = np.stack([np.asarray(r["out_fm"]).T for r in res.results], axis=0)
    return out.astype(np.float32)
